# version 10
# baseline (speedup 1.0000x reference)
import numpy as np
import ml_dtypes
import concourse.bass as bass
import concourse.mybir as mybir
from concourse.bass_utils import run_bass_kernel_spmd

F32 = mybir.dt.float32
BF16 = mybir.dt.bfloat16
AF = mybir.ActivationFunctionType
ALU = mybir.AluOpType
AXX = mybir.AxisListType.X

NEG = -1e30
EPS = 1e-5

S, E, H, HID, EXP = 4096, 1024, 16, 4096, 8
D = E // H
OWN = S // 2
QS = S // 8
EC = E // 128
HC = HID // 128
TAIL = S // 4
NSLOT = 4
NQT = QS // 128
SC = S // 128
ISQ = 1.0 / float(D) ** 0.5


def ceil_div(a, b):
    return (a + b - 1) // b


class Sem:
    __slots__ = ("h", "n")

    def __init__(self, nc, name):
        self.h = nc.alloc_semaphore(name=name)
        self.n = 0

    def inc(self, instr, k=1):
        instr.then_inc(self.h, k)
        self.n += k
        return (self, self.n)

    def tok(self):
        return (self, self.n)


def W(e, tk):
    if tk is None:
        return
    if isinstance(tk, (list, tuple)) and tk and isinstance(tk[0], (list, tuple)):
        for t in tk:
            W(e, t)
        return
    if tk[1] > 0:
        e.wait_ge(tk[0].h, tk[1])


class DmaPool:
    __slots__ = ("chans", "i")

    def __init__(self, nc, name, n=8):
        self.chans = [Sem(nc, "%s%d" % (name, i)) for i in range(n)]
        self.i = 0


class Bank:
    __slots__ = ("cm", "ps", "ready", "free")

    def __init__(self, nc, i):
        self.cm = nc.psum_tensor("pbank%d" % i, [128, 512], F32)
        self.ps = self.cm.__enter__()
        self.ready = None
        self.free = None


def build_program():
    nc = bass.Bass("TRN2", target_bir_lowering=False)

    x_full = nc.dram_tensor("x_full", [S, E], F32, kind="ExternalInput").ap()
    x_own = nc.dram_tensor("x_own", [OWN, E], F32, kind="ExternalInput").ap()
    mask_in = nc.dram_tensor("mask_in", [QS, TAIL], BF16, kind="ExternalInput").ap()
    maskT_in = nc.dram_tensor("maskT_in", [TAIL, QS], BF16, kind="ExternalInput").ap()
    ipw = nc.dram_tensor("ipw", [3 * E, E], F32, kind="ExternalInput").ap()
    ipb = nc.dram_tensor("ipb", [3 * E], F32, kind="ExternalInput").ap()
    opw = nc.dram_tensor("opw", [E, E], F32, kind="ExternalInput").ap()
    opb = nc.dram_tensor("opb", [E], F32, kind="ExternalInput").ap()
    wgw = nc.dram_tensor("wgw", [E, EXP], F32, kind="ExternalInput").ap()
    bgv = nc.dram_tensor("bgv", [EXP], F32, kind="ExternalInput").ap()
    w1 = nc.dram_tensor("w1", [EXP, E, HID], F32, kind="ExternalInput").ap()
    b1 = nc.dram_tensor("b1", [EXP, HID], F32, kind="ExternalInput").ap()
    w2 = nc.dram_tensor("w2", [EXP, HID, E], F32, kind="ExternalInput").ap()
    b2 = nc.dram_tensor("b2", [EXP, E], F32, kind="ExternalInput").ap()
    ident = nc.dram_tensor("ident", [128, 128], F32, kind="ExternalInput").ap()
    identb = nc.dram_tensor("identb", [128, 128], BF16, kind="ExternalInput").ap()
    onescol = nc.dram_tensor("onescol", [128, 1], F32, kind="ExternalInput").ap()
    onesrow = nc.dram_tensor("onesrow", [1, 128], F32, kind="ExternalInput").ap()
    cvec = nc.dram_tensor("cvec", [128, 4], F32, kind="ExternalInput").ap()
    selm = nc.dram_tensor("selm", [EXP, EXP * 128], F32, kind="ExternalInput").ap()
    out = nc.dram_tensor("out", [OWN, E], F32, kind="ExternalOutput").ap()

    Ks = nc.dram_tensor("Ks", [E, S], F32, kind="Internal").ap()
    Vs = nc.dram_tensor("Vs", [H, S, D], F32, kind="Internal").ap()
    Qs = nc.dram_tensor("Qs", [E, OWN], F32, kind="Internal").ap()
    X1s = nc.dram_tensor("X1s", [E, OWN], F32, kind="Internal").ap()

    live = []

    def SBuf(scope, name, shape, dt=F32):
        cm = nc.sbuf_tensor(name, shape, dt)
        t = cm.__enter__()
        scope.append(cm)
        return t

    def close_scope(scope):
        for cm in reversed(scope):
            cm.__exit__(None, None, None)
        scope.clear()

    s_id = SBuf(live, "s_id", [128, 128])
    s_idb = SBuf(live, "s_idb", [128, 128], BF16)
    s_1c = SBuf(live, "s_1c", [128, 1])
    s_1r = SBuf(live, "s_1r", [1, 128])
    s_cv = SBuf(live, "s_cv", [128, 4])
    s_bqkv = SBuf(live, "s_bqkv", [128, 24])
    s_bo = SBuf(live, "s_bo", [128, EC])
    s_bg = SBuf(live, "s_bg", [EXP, 1])
    s_b1 = SBuf(live, "s_b1", [128, EXP * HC])
    s_b2 = SBuf(live, "s_b2", [EXP, E])
    s_wg = SBuf(live, "s_wg", [128, EC * EXP])
    s_sel = SBuf(live, "s_sel", [EXP, EXP * 128])
    nc.const_aps.register(F32, EPS, s_cv[:, 0:1])
    nc.const_aps.register(F32, 0.0, s_cv[:, 1:2])

    B = [Bank(nc, i) for i in range(8)]
    pe = Sem(nc, "pe")
    ev = Sem(nc, "ev")
    ex = Sem(nc, "ex")
    gp = Sem(nc, "gp")
    ld = DmaPool(nc, "ld", 8)
    sw = DmaPool(nc, "sw", 8)

    def dma(e, pool, dst, src, slow=False):
        ch = pool.chans[pool.i % len(pool.chans)]
        pool.i += 1
        if ch.n:
            e.wait_ge(ch.h, ch.n)
        if slow:
            ins = e.dma_start(dst, src, allow_slow_non_contiguous=True)
        else:
            ins = e.dma_start(dst, src)
        ins.then_inc(ch.h, 16)
        ch.n += 16
        return (ch, ch.n)

    def dma_drain(e, pool):
        for ch in pool.chans:
            if ch.n:
                e.wait_ge(ch.h, ch.n)

    def emit_transposes(block, jobs, banks, evac="vector", pre_toks=(),
                        evac_pre_toks=()):
        nb = len(jobs)
        nbatch = ceil_div(nb, 4)
        sem = ev if evac == "vector" else ex

        for bi in range(nbatch):
            bank = banks[bi % 2]

            def _(e, bi=bi, bank=bank):
                if bi == 0:
                    for tk in pre_toks:
                        W(e, tk)
                W(e, bank.free)
                n4 = min(4, nb - bi * 4)
                for s in range(n4):
                    src, _d, idap = jobs[bi * 4 + s]
                    op_, ow = src.shape[1], src.shape[0]
                    ins = e.transpose(bank.ps[0:op_, s * 128:s * 128 + ow], src, idap)
                bank.ready = pe.inc(ins)
            block.tensor(_)

            def _(e, bi=bi, bank=bank):
                if bi == 0:
                    for tk in evac_pre_toks:
                        W(e, tk)
                W(e, bank.ready)
                n4 = min(4, nb - bi * 4)
                for s in range(n4):
                    src, dst, _i = jobs[bi * 4 + s]
                    op_, ow = src.shape[1], src.shape[0]
                    if evac == "vector":
                        ins = e.tensor_copy(dst, bank.ps[0:op_, s * 128:s * 128 + ow])
                    else:
                        ins = e.copy(dst, bank.ps[0:op_, s * 128:s * 128 + ow])
                bank.free = sem.inc(ins)
            if evac == "vector":
                block.vector(_)
            else:
                block.scalar(_)
        return sem.tok()

    phA = []
    s_wkvT = SBuf(phA, "s_wkvT", [128, EC * 2 * E])
    s_wqT = SBuf(phA, "s_wqT", [128, EC * E])
    s_wtmp = SBuf(phA, "s_wtmp", [128, 2 * E])
    s_xtm = SBuf(phA, "s_xtm", [128, 2 * 4 * E])
    s_xfm = SBuf(phA, "s_xfm", [128, 2 * EC * 512])
    s_kvq = SBuf(phA, "s_kvq", [128, 4 * 512])
    s_vout = SBuf(phA, "s_vout", [128, 2 * 512])

    with nc.named_scope("phaseA"), nc.Block() as block:
        def _(e):
            dma(e, ld, s_id[:], ident)
            dma(e, ld, s_idb[:], identb)
            dma(e, ld, s_1c[:], onescol)
            dma(e, ld, s_1r[:], onesrow)
            dma(e, ld, s_cv[:], cvec)
            dma(e, ld, s_bqkv[:], ipb.rearrange("(c p) -> p c", p=128), slow=True)
            dma(e, ld, s_bo[:], opb.rearrange("(c p) -> p c", p=128), slow=True)
            dma(e, ld, s_bg[:], bgv.rearrange("(e o) -> e o", o=1), slow=True)
            dma(e, ld, s_b1[:].rearrange("p (ee c) -> p ee c", c=HC),
                b1.rearrange("ee (c p) -> p ee c", p=128), slow=True)
            dma(e, ld, s_b2[:], b2)
            dma(e, ld, s_sel[:], selm)
            dma(e, ld, s_wg[:].rearrange("p (c e) -> p c e", e=EXP),
                wgw.rearrange("(c p) e -> p c e", p=128), slow=True)
        block.sync(_)

        strip_tok = [None] * 24
        strip_free = [None, None]

        def _(e):
            for r in range(2):
                strip_tok[r] = dma(e, ld, s_wtmp[:, r * E:(r + 1) * E],
                                   ipw[r * 128:(r + 1) * 128, :])
        block.sync(_)

        for r in range(24):
            jobs = []
            for ifc in range(EC):
                src = s_wtmp[:, (r % 2) * E + ifc * 128:(r % 2) * E + ifc * 128 + 128]
                if r < 8:
                    dst = s_wqT[:, ifc * E + r * 128: ifc * E + r * 128 + 128]
                else:
                    dst = s_wkvT[:, ifc * 2 * E + (r - 8) * 128:
                                 ifc * 2 * E + (r - 8) * 128 + 128]
                jobs.append((src, dst, s_id[:]))
            emit_transposes(block, jobs, [B[0], B[1]], pre_toks=(strip_tok[r],))
            strip_free[r % 2] = pe.tok()

            if r + 2 < 24:
                def _(e, r=r):
                    W(e, strip_free[r % 2])
                    strip_tok[r + 2] = dma(
                        e, ld, s_wtmp[:, (r % 2) * E:(r % 2) * E + E],
                        ipw[(r + 2) * 128:(r + 3) * 128, :])
                block.sync(_)
        wT_tok = ev.tok()

        kvq_free = [None] * 4
        kvq_i = [0]
        vout_free = [None, None]
        vb_i = [0]
        xtm_free = [None, None]
        xfm_free = [None, None]

        def emit_x_block(src_rows, parity):
            tok_holder = []

            def _(e, parity=parity):
                W(e, xtm_free[parity])
                tok_holder.append(dma(
                    e, ld,
                    s_xtm[:, parity * 4 * E: parity * 4 * E + 4 * E].rearrange(
                        "p (t d) -> p t d", d=E),
                    src_rows.rearrange("(t p) d -> p t d", p=128)))
            block.sync(_)

            jobs = []
            for j in range(4 * EC):
                t, ifc = divmod(j, EC)
                src = s_xtm[:, parity * 4 * E + t * E + ifc * 128:
                            parity * 4 * E + t * E + ifc * 128 + 128]
                dst = s_xfm[:, parity * EC * 512 + ifc * 512 + t * 128:
                            parity * EC * 512 + ifc * 512 + t * 128 + 128]
                jobs.append((src, dst, s_id[:]))
            emit_transposes(block, jobs, [B[2], B[3]],
                            pre_toks=(tok_holder[0], xfm_free[parity]))
            xtm_free[parity] = pe.tok()

        def emit_proj(parity, ofm_list, blk):
            for ofm in ofm_list:
                gb = B[4 + (ofm % 2)]
                is_q = ofm < 8

                def _(e, ofm=ofm, gb=gb, is_q=is_q, parity=parity):
                    W(e, gb.free)
                    W(e, wT_tok)
                    W(e, ev.tok())
                    for ifc in range(EC):
                        if is_q:
                            lhs = s_wqT[:, ifc * E + ofm * 128: ifc * E + ofm * 128 + 128]
                        else:
                            lhs = s_wkvT[:, ifc * 2 * E + (ofm - 8) * 128:
                                         ifc * 2 * E + (ofm - 8) * 128 + 128]
                        rhs = s_xfm[:, parity * EC * 512 + ifc * 512:
                                    parity * EC * 512 + ifc * 512 + 512]
                        ins = e.matmul(gb.ps[:], lhs, rhs,
                                       start=(ifc == 0), stop=(ifc == EC - 1))
                    gb.ready = pe.inc(ins)
                block.tensor(_)

                slot = kvq_i[0] % 4
                kvq_i[0] += 1

                def _(e, ofm=ofm, gb=gb, slot=slot):
                    W(e, gb.ready)
                    W(e, kvq_free[slot])
                    ins = e.activation(s_kvq[:, slot * 512:slot * 512 + 512],
                                       gb.ps[:], AF.Identity,
                                       bias=s_bqkv[:, ofm:ofm + 1], scale=1.0)
                    gb.free = ex.inc(ins)
                block.scalar(_)
                evac_tok = ex.tok()

                if is_q or ofm < 16:
                    def _(e, ofm=ofm, slot=slot, evac_tok=evac_tok, blk=blk,
                          is_q=is_q):
                        W(e, evac_tok)
                        if is_q:
                            dst = Qs[ofm * 128:(ofm + 1) * 128,
                                     blk * 512:(blk + 1) * 512]
                        else:
                            dst = Ks[(ofm - 8) * 128:(ofm - 7) * 128,
                                     blk * 512:(blk + 1) * 512]
                        kvq_free[slot] = dma(e, sw, dst,
                                             s_kvq[:, slot * 512:slot * 512 + 512])
                    block.gpsimd(_)
                else:
                    vb = vb_i[0] % 2
                    vb_i[0] += 1
                    vjobs = []
                    for t in range(4):
                        src = s_kvq[:, slot * 512 + t * 128: slot * 512 + t * 128 + 128]
                        dst = s_vout[:, vb * 512 + t * 128: vb * 512 + t * 128 + 128]
                        vjobs.append((src, dst, s_id[:]))
                    vtok = emit_transposes(block, vjobs, [B[6], B[7]],
                                           pre_toks=(evac_tok,),
                                           evac_pre_toks=vout_free[vb] or ())
                    kvq_free[slot] = pe.tok()

                    def _(e, ofm=ofm, blk=blk, vb=vb, vtok=vtok):
                        W(e, vtok)
                        tks = []
                        for hh in range(2):
                            hidx = (ofm - 16) * 2 + hh
                            src = s_vout[:, vb * 512:(vb + 1) * 512].rearrange(
                                "p (t dd) -> p t dd", dd=128)[:, :, hh * 64:(hh + 1) * 64]
                            dst = Vs[hidx, blk * 512:(blk + 1) * 512, :].rearrange(
                                "(t p) d -> p t d", p=128)
                            tks.append(dma(e, sw, dst, src))
                        vout_free[vb] = tuple(tks)
                    block.gpsimd(_)

        for tb in range(8):
            emit_x_block(x_full[tb * 512:(tb + 1) * 512, :], tb % 2)
            emit_proj(tb % 2, list(range(8, 24)), tb)
            xfm_free[tb % 2] = ex.tok()
        for tq in range(4):
            emit_x_block(x_own[tq * 512:(tq + 1) * 512, :], tq % 2)
            emit_proj(tq % 2, list(range(0, 8)), tq)
            xfm_free[tq % 2] = ex.tok()

        def _(e):
            dma_drain(e, sw)
        block.gpsimd(_)

    close_scope(phA)

    phBB2 = []
    s_ctx = SBuf(phBB2, "s_ctx", [128, EC * OWN])

    phB = []
    s_K = SBuf(phB, "s_K", [128, S])
    s_V = SBuf(phB, "s_V", [128, 2 * SC * D])
    s_Q = SBuf(phB, "s_Q", [128, OWN])
    s_mask = SBuf(phB, "s_mask", [128, NQT * TAIL], BF16)
    s_maskT = SBuf(phB, "s_maskT", [128, (TAIL // 128) * QS], BF16)
    s_sc = SBuf(phB, "s_sc", [128, 2 * S])
    s_ptf = SBuf(phB, "s_ptf", [128, 2 * 512])
    s_sm = SBuf(phB, "s_sm", [128, 24])
    s_rr = SBuf(phB, "s_rr", [1, 512])
    s_rb = SBuf(phB, "s_rb", [64, 512])

    with nc.named_scope("phaseB"), nc.Block() as block:
        mask_tok = [None]

        def _(e):
            t1 = dma(e, ld, s_mask[:].rearrange("p (i j) -> p i j", j=TAIL),
                     mask_in.rearrange("(i p) j -> p i j", p=128))
            t2 = dma(e, ld, s_maskT[:].rearrange("p (c q) -> p c q", q=QS),
                     maskT_in.rearrange("(c p) q -> p c q", p=128))
            mask_tok[0] = (t1, t2)
        block.sync(_)

        head_end = [None, None]
        kvq_toks = [None, None]
        sc_free = [None, None]
        ptf_free = [None, None]
        smset_free = [None] * 4
        rb_free = [None]

        iters = [(k, qi) for k in range(NSLOT) for qi in range(NQT)]
        NIT = len(iters)
        ctx_banks = [B[4], B[5]]
        srv_bank = B[6]
        bc_bank = B[7]

        def emit_kvq_load(h):
            hb = h % 2

            def _(e, h=h, hb=hb):
                W(e, head_end[hb])
                t1 = dma(e, ld, s_K[hb * 64:(hb + 1) * 64, :], Ks[h * D:(h + 1) * D, :])
                t2 = dma(e, ld,
                         s_V[:, hb * SC * D:(hb + 1) * SC * D].rearrange(
                             "p (c d) -> p c d", d=D),
                         Vs[h].rearrange("(c p) d -> p c d", p=128))
                t3 = dma(e, ld, s_Q[hb * 64:(hb + 1) * 64, :],
                         Qs[h * D:(h + 1) * D, :])
                kvq_toks[hb] = (t1, t2, t3)
            block.sync(_)

        def emit_qside(h, k, qi, g):
            ek = (k + 1) * TAIL
            nkc = ek // 512
            hb = h % 2
            pbuf = g % 2
            sset = g % 4
            qcol = k * QS + qi * 128
            for kc in range(nkc):
                bank = B[kc % 2]

                def _(e, kc=kc, bank=bank, h=h, hb=hb, qcol=qcol, nkc=nkc,
                      qi=qi, k=k, g=g):
                    W(e, bank.free)
                    if kc == 0 and k == 0 and qi == 0:
                        for tk in kvq_toks[hb]:
                            W(e, tk)
                        if h == 0:
                            W(e, mask_tok[0])
                    lhs = s_Q[hb * 64:(hb + 1) * 64, qcol: qcol + 128]
                    rhs = s_K[hb * 64:(hb + 1) * 64, kc * 512: kc * 512 + 512]
                    tail = kc >= nkc - 2
                    ins = e.matmul(bank.ps[:], lhs, rhs, start=True, stop=not tail)
                    if tail:
                        moff = (kc - (nkc - 2)) * 512
                        ins = e.matmul(bank.ps[:], s_idb[:],
                                       s_mask[:, qi * TAIL + moff: qi * TAIL + moff + 512],
                                       start=False, stop=True)
                    bank.ready = pe.inc(ins)
                block.tensor(_)

                def _(e, kc=kc, bank=bank, pbuf=pbuf):
                    W(e, bank.ready)
                    if kc == 0:
                        W(e, sc_free[pbuf])
                    ins = e.tensor_copy(s_sc[:, pbuf * S + kc * 512: pbuf * S + kc * 512 + 512],
                                        bank.ps[:])
                    bank.free = ev.inc(ins)
                block.vector(_)
            evac_tok = ev.tok()

            c = lambda i: s_sm[:, sset * 6 + i: sset * 6 + i + 1]

            def _(e, ek=ek, pbuf=pbuf, et=evac_tok, c=c):
                W(e, et)
                W(e, smset_free[sset])
                ins = e.reduce_max(c(0), s_sc[:, pbuf * S: pbuf * S + ek], axis=AXX)
                ev.inc(ins)
                e.wait_ge(ev.h, ev.n)
                ins = e.tensor_scalar(c(1), c(0), -ISQ, None, ALU.mult)
                ev.inc(ins)
            block.vector(_)
            nm_tok = ev.tok()

            def _(e, ek=ek, pbuf=pbuf, nm_tok=nm_tok, c=c):
                W(e, nm_tok)
                ins = e.activation(s_sc[:, pbuf * S: pbuf * S + ek],
                                   s_sc[:, pbuf * S: pbuf * S + ek],
                                   AF.Exp, bias=c(1), scale=ISQ,
                                   accum_out=c(2))
                ex.inc(ins)
                ins = e.activation(c(4), c(1), AF.Exp, bias=0.0, scale=1.0)
                ex.inc(ins)
            block.scalar(_)
            exp_tok = ex.tok()
            sc_free[pbuf] = exp_tok

            def _(e, exp_tok=exp_tok, c=c):
                W(e, exp_tok)
                ins = e.reciprocal(c(3), c(2))
                ev.inc(ins)
                e.wait_ge(ev.h, ev.n)
                ins = e.tensor_tensor(c(5), c(3), c(4), ALU.mult)
                ev.inc(ins)
            block.vector(_)
            srv_tok = ev.tok()

            def srv_tr(e, qi=qi, srv_tok=srv_tok, c=c, sset=sset):
                if qi == 0:
                    W(e, srv_bank.free)
                W(e, srv_tok)
                ins = e.transpose(srv_bank.ps[0:1, qi * 128:(qi + 1) * 128],
                                  c(5), s_id[:])
                t = pe.inc(ins)
                if qi == NQT - 1:
                    srv_bank.ready = t
                smset_free[sset] = pe.tok()
            return srv_tr

        def emit_ktside(h, k, srv_jobs):
            ek = (k + 1) * TAIL
            nkc128 = ek // 128
            hb = h % 2
            cbank = ctx_banks[k % 2]
            for kc in range(nkc128):
                bank = B[2 + kc % 2]

                def _(e, kc=kc, bank=bank, hb=hb, k=k, nkc128=nkc128):
                    W(e, bank.free)
                    lhs = s_K[hb * 64:(hb + 1) * 64, kc * 128: kc * 128 + 128]
                    rhs = s_Q[hb * 64:(hb + 1) * 64, k * QS: k * QS + 512]
                    tailc = kc >= nkc128 - 8
                    ins = e.matmul(bank.ps[:], lhs, rhs, start=True, stop=not tailc)
                    if tailc:
                        tc = kc - (nkc128 - 8)
                        ins = e.matmul(bank.ps[:], s_idb[:],
                                       s_maskT[:, tc * QS: tc * QS + 512],
                                       start=False, stop=True)
                    bank.ready = pe.inc(ins)
                block.tensor(_)

                ps_ = kc % 2

                def _(e, kc=kc, bank=bank, ps_=ps_):
                    W(e, bank.ready)
                    W(e, ptf_free[ps_])
                    ins = e.activation(s_ptf[:, ps_ * 512: ps_ * 512 + 512],
                                       bank.ps[:], AF.Exp, bias=0.0, scale=ISQ)
                    bank.free = ex.inc(ins)
                block.scalar(_)
                ptf_tok = ex.tok()

                def _(e, kc=kc, hb=hb, ps_=ps_, ptf_tok=ptf_tok, cbank=cbank,
                      nkc128=nkc128):
                    W(e, ptf_tok)
                    if kc == 0:
                        W(e, cbank.free)
                    lhs = s_V[:, hb * SC * D + kc * D: hb * SC * D + kc * D + D]
                    rhs = s_ptf[:, ps_ * 512: ps_ * 512 + 512]
                    ins = e.matmul(cbank.ps[0:64, 0:512], lhs, rhs,
                                   start=(kc == 0), stop=(kc == nkc128 - 1))
                    t = pe.inc(ins)
                    if kc == nkc128 - 1:
                        cbank.ready = t
                block.tensor(_)
                ptf_free[ps_] = pe.tok()
                if 3 <= kc < 3 + len(srv_jobs) and kc - 3 < NQT:
                    block.tensor(srv_jobs[kc - 3])

            def _(e):
                W(e, srv_bank.ready)
                W(e, rb_free[0])
                ins = e.tensor_copy(s_rr[0:1, :], srv_bank.ps[0:1, 0:512])
                srv_bank.free = ev.inc(ins)
            block.vector(_)
            rr_tok = ev.tok()

            def _(e, rr_tok=rr_tok):
                W(e, rr_tok)
                W(e, bc_bank.free)
                ins = e.matmul(bc_bank.ps[0:64, 0:512], s_1r[0:1, 0:64],
                               s_rr[0:1, :], start=True, stop=True)
                bc_bank.ready = pe.inc(ins)
            block.tensor(_)

            def _(e):
                W(e, bc_bank.ready)
                ins = e.tensor_copy(s_rb[:], bc_bank.ps[0:64, 0:512])
                bc_bank.free = ev.inc(ins)
            block.vector(_)
            rb_tok = ev.tok()

            def _(e, h=h, k=k, cbank=cbank, rb_tok=rb_tok):
                W(e, cbank.ready)
                W(e, rb_tok)
                ins = e.tensor_tensor(
                    s_ctx[(h % 2) * 64:(h % 2) * 64 + 64,
                          (h // 2) * OWN + k * QS:(h // 2) * OWN + k * QS + QS],
                    cbank.ps[0:64, 0:QS], s_rb[:], ALU.mult)
                cbank.free = ev.inc(ins)
            block.vector(_)
            rb_free[0] = ev.tok()

        for h in range(H):
            emit_kvq_load(h)
            g = h * NIT
            for k in range(NSLOT):
                srv_jobs = []
                for qi in range(NQT):
                    srv_jobs.append(emit_qside(h, k, qi, g))
                    g += 1
                emit_ktside(h, k, srv_jobs)
            head_end[h % 2] = pe.tok()

    close_scope(phB)

    def emit_ln(block, src, tmp, dst, s_sq, s_row, pre_tok):
        bsum, bsq, bbc1, bbc2 = B[0], B[1], B[2], B[3]
        sq_toks = []
        sqmm_toks = []

        for ofc in range(EC):
            def _(e, ofc=ofc):
                if ofc == 0:
                    W(e, pre_tok)
                if ofc >= 2:
                    W(e, sqmm_toks[ofc - 2])
                ins = e.activation(s_sq[:, (ofc % 2) * 512:(ofc % 2) * 512 + 512],
                                   src(ofc), AF.Square, bias=0.0, scale=1.0)
                sq_toks.append(ex.inc(ins))
            block.scalar(_)

            def _(e, ofc=ofc):
                if ofc == 0:
                    W(e, pre_tok)
                    W(e, bsum.free)
                    W(e, bsq.free)
                ins = e.matmul(bsum.ps[0:1, 0:512], s_1c[:], src(ofc),
                               start=(ofc == 0), stop=(ofc == EC - 1))
                if ofc == EC - 1:
                    bsum.ready = pe.inc(ins)
                W(e, sq_toks[ofc])
                ins = e.matmul(bsq.ps[0:1, 0:512], s_1c[:],
                               s_sq[:, (ofc % 2) * 512:(ofc % 2) * 512 + 512],
                               start=(ofc == 0), stop=(ofc == EC - 1))
                sqmm_toks.append(pe.inc(ins))
            block.tensor(_)
        bsq.ready = sqmm_toks[-1]

        r = lambda i: s_row[0:1, i * 512:(i + 1) * 512]

        def _(e):
            W(e, bsum.ready)
            ins = e.tensor_copy(r(0), bsum.ps[0:1, 0:512])
            bsum.free = ev.inc(ins)
            W(e, bsq.ready)
            ins = e.tensor_copy(r(1), bsq.ps[0:1, 0:512])
            bsq.free = ev.inc(ins)
            e.wait_ge(ev.h, ev.n)
            ins = e.tensor_scalar(r(2), r(0), 1.0 / E, None, ALU.mult)
            ev.inc(ins)
            ins = e.tensor_scalar(r(3), r(1), 1.0 / E, None, ALU.mult)
            ev.inc(ins)
            e.wait_ge(ev.h, ev.n)
            ins = e.tensor_tensor(r(4), r(2), r(2), ALU.mult)
            ev.inc(ins)
            e.wait_ge(ev.h, ev.n)
            ins = e.tensor_tensor(r(4), r(3), r(4), ALU.subtract)
            ev.inc(ins)
        block.vector(_)
        var_tok = ev.tok()

        def _(e, var_tok=var_tok):
            W(e, var_tok)
            ins = e.activation(r(3), r(4), AF.Sqrt, bias=EPS, scale=1.0)
            ex.inc(ins)
        block.scalar(_)
        sd_tok = ex.tok()

        def _(e, sd_tok=sd_tok):
            W(e, sd_tok)
            ins = e.reciprocal(r(5), r(3))
            ev.inc(ins)
            e.wait_ge(ev.h, ev.n)
            ins = e.tensor_tensor(r(4), r(2), r(5), ALU.mult)
            ev.inc(ins)
        block.vector(_)
        nmr_tok = ev.tok()

        def _(e, nmr_tok=nmr_tok):
            W(e, nmr_tok)
            W(e, bbc1.free)
            ins = e.matmul(bbc1.ps[:, 0:512], s_1r[:], r(5), start=True, stop=True)
            bbc1.ready = pe.inc(ins)
            W(e, bbc2.free)
            ins = e.matmul(bbc2.ps[:, 0:512], s_1r[:], r(4), start=True, stop=True)
            bbc2.ready = pe.inc(ins)
        block.tensor(_)

        def _(e):
            W(e, bbc1.ready)
            W(e, bbc2.ready)
            for ofc in range(EC):
                ins = e.tensor_tensor(tmp(ofc), src(ofc), bbc1.ps[:, 0:512], ALU.mult)
                if ofc == EC - 1:
                    bbc1.free = ev.inc(ins)
            e.wait_ge(ev.h, ev.n)
            for ofc in range(EC):
                ins = e.tensor_tensor(dst(ofc), tmp(ofc), bbc2.ps[:, 0:512], ALU.subtract)
                if ofc == EC - 1:
                    bbc2.free = ev.inc(ins)
        block.vector(_)
        return ev.tok()

    phB2 = []
    s_woT = SBuf(phB2, "s_woT", [128, EC * E])
    s_xo = SBuf(phB2, "s_xo", [128, EC * 512])
    s_xtm2 = SBuf(phB2, "s_xtm2", [128, 4 * E])
    s_t1 = SBuf(phB2, "s_t1", [128, EC * 512])
    s_sqA = SBuf(phB2, "s_sqA", [128, 2 * 512])
    s_rowA = SBuf(phB2, "s_rowA", [1, 6 * 512])

    with nc.named_scope("phaseB2"), nc.Block() as block:
        xtm2_free = [None]
        for rr in range(8):
            tok_holder = []

            def _(e, rr=rr):
                W(e, xtm2_free[0])
                tok_holder.append(dma(e, ld, s_xtm2[:, 0:E],
                                      opw[rr * 128:(rr + 1) * 128, :]))
            block.sync(_)
            jobs = []
            for cfc in range(EC):
                src = s_xtm2[:, cfc * 128:cfc * 128 + 128]
                dst = s_woT[:, cfc * E + rr * 128: cfc * E + rr * 128 + 128]
                jobs.append((src, dst, s_id[:]))
            emit_transposes(block, jobs, [B[0], B[1]], pre_toks=(tok_holder[0],))
            xtm2_free[0] = pe.tok()
        woT_tok = ev.tok()

        for nt in range(4):
            tok_holder = []
            xo_prev = ev.tok()

            def _(e, nt=nt):
                W(e, xtm2_free[0])
                tok_holder.append(dma(
                    e, ld,
                    s_xtm2[:].rearrange("p (t d) -> p t d", d=E),
                    x_own[nt * 512:(nt + 1) * 512, :].rearrange("(t p) d -> p t d", p=128)))
            block.sync(_)
            jobs = []
            for j in range(4 * EC):
                t, ifc = divmod(j, EC)
                srcj = s_xtm2[:, t * E + ifc * 128: t * E + ifc * 128 + 128]
                dstj = s_xo[:, ifc * 512 + t * 128: ifc * 512 + t * 128 + 128]
                jobs.append((srcj, dstj, s_id[:]))
            emit_transposes(block, jobs, [B[2], B[3]],
                            pre_toks=(tok_holder[0],), evac_pre_toks=(xo_prev,))
            xtm2_free[0] = pe.tok()
            xo_tok = ev.tok()

            evac_toks = []
            for ofm in range(EC):
                gb = B[4 + ofm % 2]

                def _(e, ofm=ofm, gb=gb, nt=nt):
                    W(e, gb.free)
                    if nt == 0 and ofm == 0:
                        W(e, woT_tok)
                    for cfc in range(EC):
                        lhs = s_woT[:, cfc * E + ofm * 128: cfc * E + ofm * 128 + 128]
                        rhs = s_ctx[:, cfc * OWN + nt * 512: cfc * OWN + nt * 512 + 512]
                        ins = e.matmul(gb.ps[:], lhs, rhs,
                                       start=(cfc == 0), stop=(cfc == EC - 1))
                    gb.ready = pe.inc(ins)
                block.tensor(_)

                def _(e, ofm=ofm, gb=gb):
                    W(e, gb.ready)
                    ins = e.activation(s_t1[:, ofm * 512: ofm * 512 + 512],
                                       gb.ps[:], AF.Identity,
                                       bias=s_bo[:, ofm:ofm + 1], scale=1.0)
                    gb.free = ex.inc(ins)
                block.scalar(_)
                evac_toks.append(ex.tok())

            def _(e, nt=nt, et=evac_toks[-1]):
                W(e, et)
                W(e, xo_tok)
                for ofc in range(EC):
                    ins = e.tensor_tensor(
                        s_ctx[:, ofc * OWN + nt * 512: ofc * OWN + nt * 512 + 512],
                        s_xo[:, ofc * 512: ofc * 512 + 512],
                        s_t1[:, ofc * 512: ofc * 512 + 512], ALU.add)
                ev.inc(ins)
            block.vector(_)
            add_tok = ev.tok()

            src = lambda ofc, nt=nt: s_ctx[:, ofc * OWN + nt * 512: ofc * OWN + nt * 512 + 512]
            tmp = lambda ofc, nt=nt: s_xo[:, ofc * 512: ofc * 512 + 512]
            ln_tok = emit_ln(block, src, tmp, src, s_sqA, s_rowA, add_tok)

            def _(e, nt=nt, ln_tok=ln_tok):
                W(e, ln_tok)
                sctx = s_ctx[:].rearrange("p (c t) -> p c t", t=OWN)[:, :, nt * 512:(nt + 1) * 512]
                dstx = X1s.rearrange("(c p) t -> p c t", p=128)[:, :, nt * 512:(nt + 1) * 512]
                dma(e, sw, dstx, sctx)
            block.gpsimd(_)

        def _(e):
            dma_drain(e, sw)
        block.gpsimd(_)

    close_scope(phB2)
    close_scope(phBB2)

    phC = []
    s_x1a = SBuf(phC, "s_x1a", [128, EC * 1024], BF16)
    s_x1b = SBuf(phC, "s_x1b", [128, EC * 1024], BF16)
    s_x2 = SBuf(phC, "s_x2", [128, EC * 1024])
    s_hf = SBuf(phC, "s_hf", [128, 4 * 512])
    s_ha = SBuf(phC, "s_ha", [128, 2 * 4 * 512], BF16)
    s_hb = SBuf(phC, "s_hb", [128, 2 * 4 * 512], BF16)
    s_wst = SBuf(phC, "s_wst", [128, 4096])
    s_w1a = SBuf(phC, "s_w1a", [128, 2 * EC * 512], BF16)
    s_w1b = SBuf(phC, "s_w1b", [128, 2 * EC * 512], BF16)
    s_w2a = SBuf(phC, "s_w2a", [128, 2 * 4 * 1024], BF16)
    s_w2b = SBuf(phC, "s_w2b", [128, 2 * 4 * 1024], BF16)
    s_wga = SBuf(phC, "s_wga", [128, EC * EXP], BF16)
    s_wgb = SBuf(phC, "s_wgb", [128, EC * EXP], BF16)

    s_gt = SBuf(phC, "s_gt", [128, 8 * EXP])
    s_ge = SBuf(phC, "s_ge", [128, 8 * EXP])
    s_gst = SBuf(phC, "s_gst", [128, 32])
    s_mf = SBuf(phC, "s_mf", [EXP, 1024])
    s_gl = s_mf
    s_mb = SBuf(phC, "s_mb", [128, 2 * 512])
    s_tf = SBuf(phC, "s_tf", [128, 2 * 512])
    s_sqC = s_hf
    s_rowC = SBuf(phC, "s_rowC", [1, 6 * 512])
    s_ot = SBuf(phC, "s_ot", [128, 1024])

    with nc.named_scope("phaseC"), nc.Block() as block:
        x1h_free = [None]
        x1ch_free = [None]
        x1_tok = [None]
        half_fence = [None]
        tf_free = [None] * 2
        out_free = [None]
        slabbuf_free = [None, None]
        h_free = [None, None]
        hf_free = [None]
        mb_free = [None]

        def _(e):
            ins = e.tensor_copy(s_wga[:], s_wg[:])
            ev.inc(ins)
            e.wait_ge(ev.h, ev.n)
            ins = e.tensor_tensor(s_wgb[:], s_wg[:], s_wga[:], ALU.subtract)
            ev.inc(ins)
        block.vector(_)
        wg_tok = ev.tok()

        for half in range(2):

            x1sp_tok = None
            for chk in range(2):
                def _(e, half=half, chk=chk):
                    if chk == 0 and x1h_free[0] is not None:
                        for tk in x1h_free[0]:
                            W(e, tk)
                    if chk == 1:
                        W(e, x1ch_free[0])
                    src = X1s.rearrange("(c p) t -> p c t", p=128)[
                        :, chk * 4:(chk + 1) * 4, half * 1024:(half + 1) * 1024]
                    dst = s_wst[:].rearrange("p (c t) -> p c t", t=1024)
                    x1_tok[0] = dma(e, ld, dst, src)
                block.sync(_)

                def _(e, half=half, chk=chk):
                    W(e, x1_tok[0])
                    if chk == 0 and half_fence[0] is not None:
                        for tk in half_fence[0]:
                            W(e, tk)
                    ins = e.tensor_copy(s_x1a[:, chk * 4096:(chk + 1) * 4096],
                                        s_wst[:])
                    ev.inc(ins)
                    e.wait_ge(ev.h, ev.n)
                    ins = e.tensor_tensor(s_x1b[:, chk * 4096:(chk + 1) * 4096],
                                          s_wst[:],
                                          s_x1a[:, chk * 4096:(chk + 1) * 4096],
                                          ALU.subtract)
                    ev.inc(ins)
                block.vector(_)
                x1ch_free[0] = ev.tok()
            x1sp_tok = ev.tok()
            stage_free_init = ev.tok()

            gl_toks = []
            for nt2 in range(2):
                gb = B[0]

                def _(e, nt2=nt2, gb=gb):
                    W(e, gb.free)
                    if nt2 == 0:
                        W(e, x1sp_tok)
                        W(e, wg_tok)
                    first = True
                    for dc in range(EC):
                        for (wg_, x_) in ((s_wga, s_x1a), (s_wgb, s_x1a), (s_wga, s_x1b)):
                            ins = e.matmul(gb.ps[0:EXP, 0:512],
                                           wg_[:, dc * EXP:(dc + 1) * EXP],
                                           x_[:, dc * 1024 + nt2 * 512: dc * 1024 + nt2 * 512 + 512],
                                           start=first, stop=(dc == EC - 1 and x_ is s_x1b))
                            first = False
                    gb.ready = pe.inc(ins)
                block.tensor(_)

                def _(e, nt2=nt2, gb=gb):
                    W(e, gb.ready)
                    ins = e.activation(s_gl[:, nt2 * 512: nt2 * 512 + 512],
                                       gb.ps[0:EXP, 0:512], AF.Identity,
                                       bias=s_bg[:], scale=1.0)
                    gb.free = ex.inc(ins)
                block.scalar(_)
                gl_toks.append(ex.tok())

            jobs = []
            for ti in range(8):
                jobs.append((s_gl[0:EXP, ti * 128:(ti + 1) * 128],
                             s_gt[:, ti * EXP:(ti + 1) * EXP], s_id[0:EXP, 0:EXP]))
            gt_tok = emit_transposes(block, jobs, [B[2], B[3]],
                                     pre_toks=(gl_toks[-1],))

            def _(e, gt_tok=gt_tok):
                W(e, gt_tok)
                for ti in range(8):
                    ins = e.reduce_max(s_gst[:, ti * 4: ti * 4 + 1],
                                       s_gt[:, ti * EXP:(ti + 1) * EXP], axis=AXX)
                    ev.inc(ins)
                e.wait_ge(ev.h, ev.n)
                for ti in range(8):
                    ins = e.tensor_scalar(s_gst[:, ti * 4 + 1: ti * 4 + 2],
                                          s_gst[:, ti * 4: ti * 4 + 1],
                                          -1.0, None, ALU.mult)
                    ev.inc(ins)
            block.vector(_)
            gnm_tok = ev.tok()

            def _(e, gnm_tok=gnm_tok):
                W(e, gnm_tok)
                for ti in range(8):
                    ins = e.activation(s_ge[:, ti * EXP: ti * EXP + EXP],
                                       s_gt[:, ti * EXP:(ti + 1) * EXP],
                                       AF.Exp, bias=s_gst[:, ti * 4 + 1: ti * 4 + 2],
                                       scale=1.0,
                                       accum_out=s_gst[:, ti * 4 + 2: ti * 4 + 3])
                    ex.inc(ins)
            block.scalar(_)
            gs_tok = ex.tok()

            def _(e, gs_tok=gs_tok):
                W(e, gs_tok)
                for ti in range(8):
                    ins = e.reciprocal(s_gst[:, ti * 4 + 3: ti * 4 + 4],
                                       s_gst[:, ti * 4 + 2: ti * 4 + 3])
                    ev.inc(ins)
                e.wait_ge(ev.h, ev.n)
                for ti in range(8):
                    ins = e.tensor_scalar(s_gt[:, ti * EXP:(ti + 1) * EXP],
                                          s_gt[:, ti * EXP:(ti + 1) * EXP],
                                          s_gst[:, ti * 4: ti * 4 + 1],
                                          s_gst[:, ti * 4 + 3: ti * 4 + 4],
                                          ALU.is_equal, ALU.mult)
                    ev.inc(ins)
            block.vector(_)
            mt_tok = ev.tok()

            jobs = []
            for ti in range(8):
                jobs.append((s_gt[:, ti * EXP:(ti + 1) * EXP],
                             s_mf[0:EXP, ti * 128:(ti + 1) * 128], s_id[:]))
            mf_tok = emit_transposes(block, jobs, [B[2], B[3]], pre_toks=(mt_tok,))

            gp_prev = gp.tok()
            for nt2 in range(2):
                for om in range(EC):
                    gb = B[4 + om % 2]

                    def _(e, nt2=nt2, om=om, gb=gb, mf_tok=mf_tok):
                        W(e, gb.free)
                        if nt2 == 0 and om == 0:
                            W(e, mf_tok)
                        ins = e.matmul(gb.ps[:], s_b2[0:EXP, om * 128:(om + 1) * 128],
                                       s_mf[0:EXP, nt2 * 512: nt2 * 512 + 512],
                                       start=True, stop=True)
                        gb.ready = pe.inc(ins)
                    block.tensor(_)

                    def _(e, nt2=nt2, om=om, gb=gb, gp_prev=gp_prev):
                        W(e, gb.ready)
                        W(e, gp_prev)
                        ins = e.tensor_copy(
                            s_x2[:, om * 1024 + nt2 * 512: om * 1024 + nt2 * 512 + 512],
                            gb.ps[:])
                        gb.free = ev.inc(ins)
                    block.vector(_)

            slabs = [(eix, hg) for eix in range(EXP) for hg in range(8)]
            stage_tok = [None]
            stage_free = [stage_free_init]
            split_tok = [None, None]
            mb_toks = {}

            def emit_slab_load_split(si):
                buf = si % 2
                eix, hg = slabs[si]
                for part in range(2):
                    def _(e, si=si, part=part, eix=eix, hg=hg):
                        if stage_free[0] is not None:
                            W(e, stage_free[0])
                        if part == 0:
                            sl = w1[eix][:, hg * 512:(hg + 1) * 512].rearrange(
                                "(c p) h -> p c h", p=128)
                            stage_tok[0] = dma(
                                e, ld, s_wst[:].rearrange("p (c h) -> p c h", h=512), sl)
                        else:
                            sl = w2[eix][hg * 512:(hg + 1) * 512, :].rearrange(
                                "(c p) o -> p c o", p=128)
                            stage_tok[0] = dma(
                                e, ld, s_wst[:].rearrange("p (c o) -> p c o", o=E), sl)
                    block.sync(_)

                    def _(e, buf=buf, part=part):
                        W(e, stage_tok[0])
                        if part == 0 and slabbuf_free[buf] is not None:
                            for tk in (slabbuf_free[buf] if isinstance(slabbuf_free[buf], list)
                                       else [slabbuf_free[buf]]):
                                W(e, tk)
                        wa = s_w1a if part == 0 else s_w2a
                        ins = e.tensor_copy(wa[:, buf * 4096:(buf + 1) * 4096], s_wst[:])
                        gp.inc(ins)
                    block.gpsimd(_)
                    hi_tok = gp.tok()

                    def _(e, buf=buf, part=part, hi_tok=hi_tok):
                        W(e, hi_tok)
                        wa = s_w1a if part == 0 else s_w2a
                        wb = s_w1b if part == 0 else s_w2b
                        ins = e.tensor_tensor(wb[:, buf * 4096:(buf + 1) * 4096],
                                              s_wst[:],
                                              wa[:, buf * 4096:(buf + 1) * 4096],
                                              ALU.subtract)
                        t = ev.inc(ins)
                        if part == 1:
                            split_tok[buf] = t
                        stage_free[0] = ev.tok()
                    block.vector(_)

            def emit_mb(eix):
                toks = []
                for nt2 in range(2):
                    gb = B[1]

                    def _(e, eix=eix, nt2=nt2, gb=gb, mf_tok=mf_tok):
                        W(e, gb.free)
                        if eix == 0 and nt2 == 0:
                            W(e, mf_tok)
                        ins = e.matmul(gb.ps[:],
                                       s_sel[0:EXP, eix * 128:(eix + 1) * 128],
                                       s_mf[0:EXP, nt2 * 512: nt2 * 512 + 512],
                                       start=True, stop=True)
                        gb.ready = pe.inc(ins)
                    block.tensor(_)

                    def _(e, nt2=nt2, gb=gb):
                        W(e, gb.ready)
                        W(e, mb_free[0])
                        ins = e.tensor_copy(s_mb[:, nt2 * 512: nt2 * 512 + 512], gb.ps[:])
                        gb.free = ev.inc(ins)
                    block.vector(_)
                    toks.append(ev.tok())
                return toks

            emit_slab_load_split(0)
            for si, (eix, hg) in enumerate(slabs):
                buf = si % 2
                if hg == 0:
                    mb_toks[eix] = emit_mb(eix)
                if si + 1 < len(slabs):
                    emit_slab_load_split(si + 1)
                my_split = split_tok[buf]
                for nt2 in range(2):
                    hbuf = (si * 2 + nt2) % 2
                    h_toks = []
                    for hm in range(4):
                        gb = B[4 + hm % 2]

                        def _(e, nt2=nt2, hm=hm, gb=gb, buf=buf,
                              st_=my_split):
                            W(e, gb.free)
                            if hm == 0 and nt2 == 0:
                                W(e, st_)
                            first = True
                            for dc in range(EC):
                                off = buf * 4096 + dc * 512 + hm * 128
                                for (w_, x_) in ((s_w1a, s_x1a), (s_w1b, s_x1a),
                                                 (s_w1a, s_x1b)):
                                    ins = e.matmul(
                                        gb.ps[:], w_[:, off: off + 128],
                                        x_[:, dc * 1024 + nt2 * 512: dc * 1024 + nt2 * 512 + 512],
                                        start=first,
                                        stop=(dc == EC - 1 and x_ is s_x1b))
                                    first = False
                            gb.ready = pe.inc(ins)
                        block.tensor(_)

                        def _(e, eix=eix, hg=hg, hm=hm, gb=gb, hbuf=hbuf):
                            W(e, gb.ready)
                            if hm == 0:
                                W(e, h_free[hbuf])
                                W(e, hf_free[0])
                            col = eix * HC + hg * 4 + hm
                            ins = e.activation(
                                s_hf[:, hm * 512: hm * 512 + 512], gb.ps[:],
                                AF.Silu, bias=s_b1[:, col:col + 1], scale=1.0)
                            ex.inc(ins)
                            ins = e.activation(
                                s_ha[:, hbuf * 2048 + hm * 512: hbuf * 2048 + hm * 512 + 512],
                                gb.ps[:],
                                AF.Silu, bias=s_b1[:, col:col + 1], scale=1.0)
                            gb.free = ex.inc(ins)
                        block.scalar(_)
                        h_toks.append(ex.tok())

                    def _(e, hbuf=hbuf, ht=h_toks):
                        W(e, ht[-1])
                        for hm in range(4):
                            ins = e.tensor_tensor(
                                s_hb[:, hbuf * 2048 + hm * 512: hbuf * 2048 + hm * 512 + 512],
                                s_hf[:, hm * 512: hm * 512 + 512],
                                s_ha[:, hbuf * 2048 + hm * 512: hbuf * 2048 + hm * 512 + 512],
                                ALU.subtract)
                            ev.inc(ins)
                    block.vector(_)
                    hsp_tok = ev.tok()
                    hf_free[0] = hsp_tok

                    for om in range(EC):
                        gb = B[6 + om % 2]

                        def _(e, om=om, gb=gb, ht=hsp_tok, buf=buf, hbuf=hbuf):
                            W(e, gb.free)
                            if om == 0:
                                W(e, ht)
                            first = True
                            for hc in range(4):
                                woff = buf * 4096 + hc * 1024 + om * 128
                                hoff = hbuf * 2048 + hc * 512
                                for (w_, h_) in ((s_w2a, s_ha), (s_w2b, s_ha),
                                                 (s_w2a, s_hb)):
                                    ins = e.matmul(
                                        gb.ps[:], w_[:, woff: woff + 128],
                                        h_[:, hoff: hoff + 512],
                                        start=first,
                                        stop=(hc == 3 and h_ is s_hb))
                                    first = False
                            gb.ready = pe.inc(ins)
                        block.tensor(_)

                        tslot = om % 2

                        def _(e, om=om, gb=gb, nt2=nt2, tslot=tslot, eix=eix,
                              mbts=mb_toks[eix]):
                            W(e, gb.ready)
                            if om == 0:
                                W(e, mbts[nt2])
                            W(e, tf_free[tslot])
                            ins = e.tensor_tensor(
                                s_tf[:, tslot * 512: tslot * 512 + 512],
                                gb.ps[:], s_mb[:, nt2 * 512: nt2 * 512 + 512],
                                ALU.mult)
                            gb.free = ev.inc(ins)
                        block.vector(_)
                        t_tok = ev.tok()

                        def _(e, om=om, nt2=nt2, tslot=tslot, t_tok=t_tok):
                            W(e, t_tok)
                            ins = e.tensor_add(
                                s_x2[:, om * 1024 + nt2 * 512: om * 1024 + nt2 * 512 + 512],
                                s_x2[:, om * 1024 + nt2 * 512: om * 1024 + nt2 * 512 + 512],
                                s_tf[:, tslot * 512: tslot * 512 + 512])
                            gp.inc(ins)
                        block.gpsimd(_)
                        tf_free[tslot] = gp.tok()
                    h_free[hbuf] = pe.tok()
                slabbuf_free[buf] = pe.tok()
                mb_free[0] = ev.tok()

            folds_tok = gp.tok()

            def _(e, folds_tok=folds_tok):
                W(e, folds_tok)
                for ofc in range(EC):
                    for nt2 in range(2):
                        ins = e.tensor_tensor(
                            s_x2[:, ofc * 1024 + nt2 * 512: ofc * 1024 + nt2 * 512 + 512],
                            s_x2[:, ofc * 1024 + nt2 * 512: ofc * 1024 + nt2 * 512 + 512],
                            s_x1a[:, ofc * 1024 + nt2 * 512: ofc * 1024 + nt2 * 512 + 512],
                            ALU.add)
                        ev.inc(ins)
                e.wait_ge(ev.h, ev.n)
                for ofc in range(EC):
                    for nt2 in range(2):
                        ins = e.tensor_tensor(
                            s_x2[:, ofc * 1024 + nt2 * 512: ofc * 1024 + nt2 * 512 + 512],
                            s_x2[:, ofc * 1024 + nt2 * 512: ofc * 1024 + nt2 * 512 + 512],
                            s_x1b[:, ofc * 1024 + nt2 * 512: ofc * 1024 + nt2 * 512 + 512],
                            ALU.add)
                        ev.inc(ins)
            block.vector(_)
            x2pre_tok = ev.tok()

            for nt2 in range(2):
                src = lambda ofc, nt2=nt2: s_x2[:, ofc * 1024 + nt2 * 512:
                                                ofc * 1024 + nt2 * 512 + 512]
                tmp = lambda ofc, nt2=nt2: s_wst[:, ofc * 512: ofc * 512 + 512]
                ln_tok = emit_ln(block, src, tmp, src, s_sqC, s_rowC, x2pre_tok)

                for tt4 in range(4):
                    jobs = []
                    for ofc in range(EC):
                        srcT = s_x2[:, ofc * 1024 + nt2 * 512 + tt4 * 128:
                                    ofc * 1024 + nt2 * 512 + tt4 * 128 + 128]
                        dstT = s_ot[:, ofc * 128: ofc * 128 + 128]
                        jobs.append((srcT, dstT, s_id[:]))
                    pre = (ln_tok,) if tt4 == 0 else ()
                    ot_tok = emit_transposes(block, jobs, [B[4], B[5]],
                                             pre_toks=pre,
                                             evac_pre_toks=(out_free[0],))

                    def _(e, half=half, nt2=nt2, tt4=tt4, ot_tok=ot_tok):
                        W(e, ot_tok)
                        row = half * 1024 + nt2 * 512 + tt4 * 128
                        out_free[0] = dma(e, sw, out[row:row + 128, :], s_ot[:])
                    block.gpsimd(_)

            x1h_free[0] = (pe.tok(), ev.tok(), ex.tok())
            slabbuf_free[0] = [pe.tok(), ev.tok(), ex.tok()]
            slabbuf_free[1] = [pe.tok(), ev.tok(), ex.tok()]
            half_fence[0] = (pe.tok(), ev.tok(), ex.tok(), gp.tok())

        def _(e):
            dma_drain(e, sw)
        block.gpsimd(_)

    close_scope(phC)
    for b in reversed(B):
        b.cm.__exit__(None, None, None)
    close_scope(live)
    return nc


def _make_consts():
    identf = np.eye(128, dtype=np.float32)
    identbf = np.eye(128, dtype=np.float32).astype(ml_dtypes.bfloat16)
    onescol = np.ones((128, 1), dtype=np.float32)
    onesrow = np.ones((1, 128), dtype=np.float32)
    cvec = np.zeros((128, 4), dtype=np.float32)
    cvec[:, 0] = EPS
    selm = np.zeros((EXP, EXP * 128), dtype=np.float32)
    for e_ in range(EXP):
        selm[e_, e_ * 128:(e_ + 1) * 128] = 1.0
    return identf, identbf, onescol, onesrow, cvec, selm


def _make_mask(par):
    i = np.arange(QS)[:, None]
    j = np.arange(TAIL)[None, :]
    m = np.where(j > par * 512 + i, NEG, 0.0).astype(np.float32)
    return np.ascontiguousarray(m.astype(ml_dtypes.bfloat16))


def _make_maskT(par):
    j = np.arange(TAIL)[:, None]
    i = np.arange(QS)[None, :]
    m = np.where(j > par * 512 + i, NEG, 0.0).astype(np.float32)
    return np.ascontiguousarray(m.astype(ml_dtypes.bfloat16))


def kernel(L0_attn_x, L0_attn_in_proj_w, L0_attn_in_proj_b, L0_attn_out_proj_w,
           L0_attn_out_proj_b, L0_ffn_Wg, L0_ffn_bg, L0_ffn_W1, L0_ffn_b1,
           L0_ffn_W2, L0_ffn_b2):
    x = np.ascontiguousarray(np.asarray(L0_attn_x, dtype=np.float32))
    Bb = x.shape[0]
    identf, identbf, onescol, onesrow, cvec, selm = _make_consts()
    weights = dict(
        ipw=np.ascontiguousarray(np.asarray(L0_attn_in_proj_w, np.float32)),
        ipb=np.ascontiguousarray(np.asarray(L0_attn_in_proj_b, np.float32)),
        opw=np.ascontiguousarray(np.asarray(L0_attn_out_proj_w, np.float32)),
        opb=np.ascontiguousarray(np.asarray(L0_attn_out_proj_b, np.float32)),
        wgw=np.ascontiguousarray(np.asarray(L0_ffn_Wg, np.float32)),
        bgv=np.ascontiguousarray(np.asarray(L0_ffn_bg, np.float32)),
        w1=np.ascontiguousarray(np.asarray(L0_ffn_W1, np.float32)),
        b1=np.ascontiguousarray(np.asarray(L0_ffn_b1, np.float32)),
        w2=np.ascontiguousarray(np.asarray(L0_ffn_W2, np.float32)),
        b2=np.ascontiguousarray(np.asarray(L0_ffn_b2, np.float32)),
        ident=identf, identb=identbf, onescol=onescol, onesrow=onesrow,
        cvec=cvec, selm=selm,
    )
    masks = [_make_mask(0), _make_mask(1)]
    masksT = [_make_maskT(0), _make_maskT(1)]
    in_maps = []
    for c in range(8):
        b, par = c // 2, c % 2
        blocks = [2 * k + par for k in range(NSLOT)]
        xo = np.ascontiguousarray(
            np.concatenate([x[b, a * 512:(a + 1) * 512] for a in blocks], axis=0))
        m = dict(weights)
        m["x_full"] = np.ascontiguousarray(x[b])
        m["x_own"] = xo
        m["mask_in"] = masks[par]
        m["maskT_in"] = masksT[par]
        in_maps.append(m)

    nc = build_program()
    res = run_bass_kernel_spmd(nc, in_maps, core_ids=list(range(8)))

    y = np.empty((Bb, S, E), dtype=np.float32)
    for c in range(8):
        b, par = c // 2, c % 2
        oc = np.asarray(res.results[c]["out"])
        for k in range(NSLOT):
            a = 2 * k + par
            y[b, a * 512:(a + 1) * 512] = oc[k * 512:(k + 1) * 512]
    return y


# version 11
# speedup vs baseline: 1.2980x; 1.2980x over previous
import numpy as np
import ml_dtypes
import concourse.bass as bass
import concourse.mybir as mybir
from concourse.bass_utils import run_bass_kernel_spmd

F32 = mybir.dt.float32
BF16 = mybir.dt.bfloat16
AF = mybir.ActivationFunctionType
ALU = mybir.AluOpType
AXX = mybir.AxisListType.X

NEG = -1e30
EPS = 1e-5

S, E, H, HID, EXP = 4096, 1024, 16, 4096, 8
D = E // H
OWN = S // 2
QS = S // 8
EC = E // 128
HC = HID // 128
TAIL = S // 4
NSLOT = 4
NQT = QS // 128
SC = S // 128
ISQ = 1.0 / float(D) ** 0.5


def ceil_div(a, b):
    return (a + b - 1) // b


class Sem:
    __slots__ = ("h", "n")

    def __init__(self, nc, name):
        self.h = nc.alloc_semaphore(name=name)
        self.n = 0

    def inc(self, instr, k=1):
        instr.then_inc(self.h, k)
        self.n += k
        return (self, self.n)

    def tok(self):
        return (self, self.n)


def W(e, tk):
    if tk is None:
        return
    if isinstance(tk, (list, tuple)) and tk and isinstance(tk[0], (list, tuple)):
        for t in tk:
            W(e, t)
        return
    if tk[1] > 0:
        e.wait_ge(tk[0].h, tk[1])


class DmaPool:
    __slots__ = ("chans", "i")

    def __init__(self, nc, name, n=8):
        self.chans = [Sem(nc, "%s%d" % (name, i)) for i in range(n)]
        self.i = 0


class Bank:
    __slots__ = ("cm", "ps", "ready", "free")

    def __init__(self, nc, i):
        self.cm = nc.psum_tensor("pbank%d" % i, [128, 512], F32)
        self.ps = self.cm.__enter__()
        self.ready = None
        self.free = None


def build_program():
    nc = bass.Bass("TRN2", target_bir_lowering=False)

    x_full = nc.dram_tensor("x_full", [S, E], F32, kind="ExternalInput").ap()
    x_own = nc.dram_tensor("x_own", [OWN, E], F32, kind="ExternalInput").ap()
    mask_in = nc.dram_tensor("mask_in", [QS, TAIL], BF16, kind="ExternalInput").ap()
    maskT_in = nc.dram_tensor("maskT_in", [TAIL, QS], BF16, kind="ExternalInput").ap()
    ipw = nc.dram_tensor("ipw", [3 * E, E], F32, kind="ExternalInput").ap()
    ipb = nc.dram_tensor("ipb", [3 * E], F32, kind="ExternalInput").ap()
    opw = nc.dram_tensor("opw", [E, E], F32, kind="ExternalInput").ap()
    opb = nc.dram_tensor("opb", [E], F32, kind="ExternalInput").ap()
    wgw = nc.dram_tensor("wgw", [E, EXP], F32, kind="ExternalInput").ap()
    bgv = nc.dram_tensor("bgv", [EXP], F32, kind="ExternalInput").ap()
    w1 = nc.dram_tensor("w1", [EXP, E, HID], F32, kind="ExternalInput").ap()
    b1 = nc.dram_tensor("b1", [EXP, HID], F32, kind="ExternalInput").ap()
    w2 = nc.dram_tensor("w2", [EXP, HID, E], F32, kind="ExternalInput").ap()
    b2 = nc.dram_tensor("b2", [EXP, E], F32, kind="ExternalInput").ap()
    ident = nc.dram_tensor("ident", [128, 128], F32, kind="ExternalInput").ap()
    identb = nc.dram_tensor("identb", [128, 128], BF16, kind="ExternalInput").ap()
    onescol = nc.dram_tensor("onescol", [128, 1], F32, kind="ExternalInput").ap()
    onesrow = nc.dram_tensor("onesrow", [1, 128], F32, kind="ExternalInput").ap()
    cvec = nc.dram_tensor("cvec", [128, 4], F32, kind="ExternalInput").ap()
    selm = nc.dram_tensor("selm", [EXP, EXP * 128], F32, kind="ExternalInput").ap()
    out = nc.dram_tensor("out", [OWN, E], F32, kind="ExternalOutput").ap()

    Ks = nc.dram_tensor("Ks", [E, S], F32, kind="Internal").ap()
    Vs = nc.dram_tensor("Vs", [H, S, D], F32, kind="Internal").ap()
    Qs = nc.dram_tensor("Qs", [E, OWN], F32, kind="Internal").ap()
    X1s = nc.dram_tensor("X1s", [E, OWN], F32, kind="Internal").ap()

    live = []

    def SBuf(scope, name, shape, dt=F32):
        cm = nc.sbuf_tensor(name, shape, dt)
        t = cm.__enter__()
        scope.append(cm)
        return t

    def close_scope(scope):
        for cm in reversed(scope):
            cm.__exit__(None, None, None)
        scope.clear()

    s_id = SBuf(live, "s_id", [128, 128])
    s_idb = SBuf(live, "s_idb", [128, 128], BF16)
    s_1c = SBuf(live, "s_1c", [128, 1])
    s_1r = SBuf(live, "s_1r", [1, 128])
    s_cv = SBuf(live, "s_cv", [128, 4])
    s_bqkv = SBuf(live, "s_bqkv", [128, 24])
    s_bo = SBuf(live, "s_bo", [128, EC])
    s_bg = SBuf(live, "s_bg", [EXP, 1])
    s_b1 = SBuf(live, "s_b1", [128, EXP * HC])
    s_b2 = SBuf(live, "s_b2", [EXP, E])
    s_wg = SBuf(live, "s_wg", [128, EC * EXP])
    s_sel = SBuf(live, "s_sel", [EXP, EXP * 128])
    nc.const_aps.register(F32, EPS, s_cv[:, 0:1])
    nc.const_aps.register(F32, 0.0, s_cv[:, 1:2])

    B = [Bank(nc, i) for i in range(8)]
    pe = Sem(nc, "pe")
    ev = Sem(nc, "ev")
    ex = Sem(nc, "ex")
    gp = Sem(nc, "gp")
    ld = DmaPool(nc, "ld", 8)
    sw = DmaPool(nc, "sw", 8)

    def dma(e, pool, dst, src, slow=False):
        ch = pool.chans[pool.i % len(pool.chans)]
        pool.i += 1
        if ch.n:
            e.wait_ge(ch.h, ch.n)
        if slow:
            ins = e.dma_start(dst, src, allow_slow_non_contiguous=True)
        else:
            ins = e.dma_start(dst, src)
        ins.then_inc(ch.h, 16)
        ch.n += 16
        return (ch, ch.n)

    def dma_drain(e, pool):
        for ch in pool.chans:
            if ch.n:
                e.wait_ge(ch.h, ch.n)

    def emit_transposes(block, jobs, banks, evac="vector", pre_toks=(),
                        evac_pre_toks=()):
        nb = len(jobs)
        nbatch = ceil_div(nb, 4)
        sem = ev if evac == "vector" else ex

        for bi in range(nbatch):
            bank = banks[bi % 2]

            def _(e, bi=bi, bank=bank):
                if bi == 0:
                    for tk in pre_toks:
                        W(e, tk)
                W(e, bank.free)
                n4 = min(4, nb - bi * 4)
                for s in range(n4):
                    src, _d, idap = jobs[bi * 4 + s]
                    op_, ow = src.shape[1], src.shape[0]
                    ins = e.transpose(bank.ps[0:op_, s * 128:s * 128 + ow], src, idap)
                bank.ready = pe.inc(ins)
            block.tensor(_)

            def _(e, bi=bi, bank=bank):
                if bi == 0:
                    for tk in evac_pre_toks:
                        W(e, tk)
                W(e, bank.ready)
                n4 = min(4, nb - bi * 4)
                for s in range(n4):
                    src, dst, _i = jobs[bi * 4 + s]
                    op_, ow = src.shape[1], src.shape[0]
                    if evac == "vector":
                        ins = e.tensor_copy(dst, bank.ps[0:op_, s * 128:s * 128 + ow])
                    else:
                        ins = e.copy(dst, bank.ps[0:op_, s * 128:s * 128 + ow])
                bank.free = sem.inc(ins)
            if evac == "vector":
                block.vector(_)
            else:
                block.scalar(_)
        return sem.tok()

    phA = []
    s_wkvT = SBuf(phA, "s_wkvT", [128, EC * 2 * E])
    s_wqT = SBuf(phA, "s_wqT", [128, EC * E])
    s_wtmp = SBuf(phA, "s_wtmp", [128, 2 * E])
    s_xtm = SBuf(phA, "s_xtm", [128, 2 * 4 * E])
    s_xfm = SBuf(phA, "s_xfm", [128, 2 * EC * 512])
    s_kvq = SBuf(phA, "s_kvq", [128, 4 * 512])
    s_vout = SBuf(phA, "s_vout", [128, 2 * 512])

    with nc.named_scope("phaseA"), nc.Block() as block:
        def _(e):
            dma(e, ld, s_id[:], ident)
            dma(e, ld, s_idb[:], identb)
            dma(e, ld, s_1c[:], onescol)
            dma(e, ld, s_1r[:], onesrow)
            dma(e, ld, s_cv[:], cvec)
            dma(e, ld, s_bqkv[:], ipb.rearrange("(c p) -> p c", p=128), slow=True)
            dma(e, ld, s_bo[:], opb.rearrange("(c p) -> p c", p=128), slow=True)
            dma(e, ld, s_bg[:], bgv.rearrange("(e o) -> e o", o=1), slow=True)
            dma(e, ld, s_b1[:].rearrange("p (ee c) -> p ee c", c=HC),
                b1.rearrange("ee (c p) -> p ee c", p=128), slow=True)
            dma(e, ld, s_b2[:], b2)
            dma(e, ld, s_sel[:], selm)
            dma(e, ld, s_wg[:].rearrange("p (c e) -> p c e", e=EXP),
                wgw.rearrange("(c p) e -> p c e", p=128), slow=True)
        block.sync(_)

        strip_tok = [None] * 24
        strip_free = [None, None]

        def _(e):
            for r in range(2):
                strip_tok[r] = dma(e, ld, s_wtmp[:, r * E:(r + 1) * E],
                                   ipw[r * 128:(r + 1) * 128, :])
        block.sync(_)

        for r in range(24):
            jobs = []
            for ifc in range(EC):
                src = s_wtmp[:, (r % 2) * E + ifc * 128:(r % 2) * E + ifc * 128 + 128]
                if r < 8:
                    dst = s_wqT[:, ifc * E + r * 128: ifc * E + r * 128 + 128]
                else:
                    dst = s_wkvT[:, ifc * 2 * E + (r - 8) * 128:
                                 ifc * 2 * E + (r - 8) * 128 + 128]
                jobs.append((src, dst, s_id[:]))
            emit_transposes(block, jobs, [B[0], B[1]], pre_toks=(strip_tok[r],))
            strip_free[r % 2] = pe.tok()

            if r + 2 < 24:
                def _(e, r=r):
                    W(e, strip_free[r % 2])
                    strip_tok[r + 2] = dma(
                        e, ld, s_wtmp[:, (r % 2) * E:(r % 2) * E + E],
                        ipw[(r + 2) * 128:(r + 3) * 128, :])
                block.sync(_)
        wT_tok = ev.tok()

        kvq_free = [None] * 4
        kvq_i = [0]
        vout_free = [None, None]
        vb_i = [0]
        xtm_free = [None, None]
        xfm_free = [None, None]

        def emit_x_block(src_rows, parity):
            tok_holder = []

            def _(e, parity=parity):
                W(e, xtm_free[parity])
                tok_holder.append(dma(
                    e, ld,
                    s_xtm[:, parity * 4 * E: parity * 4 * E + 4 * E].rearrange(
                        "p (t d) -> p t d", d=E),
                    src_rows.rearrange("(t p) d -> p t d", p=128)))
            block.sync(_)

            jobs = []
            for j in range(4 * EC):
                t, ifc = divmod(j, EC)
                src = s_xtm[:, parity * 4 * E + t * E + ifc * 128:
                            parity * 4 * E + t * E + ifc * 128 + 128]
                dst = s_xfm[:, parity * EC * 512 + ifc * 512 + t * 128:
                            parity * EC * 512 + ifc * 512 + t * 128 + 128]
                jobs.append((src, dst, s_id[:]))
            emit_transposes(block, jobs, [B[2], B[3]],
                            pre_toks=(tok_holder[0], xfm_free[parity]))
            xtm_free[parity] = pe.tok()

        def emit_proj(parity, ofm_list, blk):
            for ofm in ofm_list:
                gb = B[4 + (ofm % 2)]
                is_q = ofm < 8

                def _(e, ofm=ofm, gb=gb, is_q=is_q, parity=parity):
                    W(e, gb.free)
                    W(e, wT_tok)
                    W(e, ev.tok())
                    for ifc in range(EC):
                        if is_q:
                            lhs = s_wqT[:, ifc * E + ofm * 128: ifc * E + ofm * 128 + 128]
                        else:
                            lhs = s_wkvT[:, ifc * 2 * E + (ofm - 8) * 128:
                                         ifc * 2 * E + (ofm - 8) * 128 + 128]
                        rhs = s_xfm[:, parity * EC * 512 + ifc * 512:
                                    parity * EC * 512 + ifc * 512 + 512]
                        ins = e.matmul(gb.ps[:], lhs, rhs,
                                       start=(ifc == 0), stop=(ifc == EC - 1))
                    gb.ready = pe.inc(ins)
                block.tensor(_)

                slot = kvq_i[0] % 4
                kvq_i[0] += 1

                def _(e, ofm=ofm, gb=gb, slot=slot):
                    W(e, gb.ready)
                    W(e, kvq_free[slot])
                    ins = e.activation(s_kvq[:, slot * 512:slot * 512 + 512],
                                       gb.ps[:], AF.Identity,
                                       bias=s_bqkv[:, ofm:ofm + 1], scale=1.0)
                    gb.free = ex.inc(ins)
                block.scalar(_)
                evac_tok = ex.tok()

                if is_q or ofm < 16:
                    def _(e, ofm=ofm, slot=slot, evac_tok=evac_tok, blk=blk,
                          is_q=is_q):
                        W(e, evac_tok)
                        if is_q:
                            dst = Qs[ofm * 128:(ofm + 1) * 128,
                                     blk * 512:(blk + 1) * 512]
                        else:
                            dst = Ks[(ofm - 8) * 128:(ofm - 7) * 128,
                                     blk * 512:(blk + 1) * 512]
                        kvq_free[slot] = dma(e, sw, dst,
                                             s_kvq[:, slot * 512:slot * 512 + 512])
                    block.gpsimd(_)
                else:
                    vb = vb_i[0] % 2
                    vb_i[0] += 1
                    vjobs = []
                    for t in range(4):
                        src = s_kvq[:, slot * 512 + t * 128: slot * 512 + t * 128 + 128]
                        dst = s_vout[:, vb * 512 + t * 128: vb * 512 + t * 128 + 128]
                        vjobs.append((src, dst, s_id[:]))
                    vtok = emit_transposes(block, vjobs, [B[6], B[7]],
                                           pre_toks=(evac_tok,),
                                           evac_pre_toks=vout_free[vb] or ())
                    kvq_free[slot] = pe.tok()

                    def _(e, ofm=ofm, blk=blk, vb=vb, vtok=vtok):
                        W(e, vtok)
                        tks = []
                        for hh in range(2):
                            hidx = (ofm - 16) * 2 + hh
                            src = s_vout[:, vb * 512:(vb + 1) * 512].rearrange(
                                "p (t dd) -> p t dd", dd=128)[:, :, hh * 64:(hh + 1) * 64]
                            dst = Vs[hidx, blk * 512:(blk + 1) * 512, :].rearrange(
                                "(t p) d -> p t d", p=128)
                            tks.append(dma(e, sw, dst, src))
                        vout_free[vb] = tuple(tks)
                    block.gpsimd(_)

        for tb in range(8):
            emit_x_block(x_full[tb * 512:(tb + 1) * 512, :], tb % 2)
            emit_proj(tb % 2, list(range(8, 24)), tb)
            xfm_free[tb % 2] = ex.tok()
        for tq in range(4):
            emit_x_block(x_own[tq * 512:(tq + 1) * 512, :], tq % 2)
            emit_proj(tq % 2, list(range(0, 8)), tq)
            xfm_free[tq % 2] = ex.tok()

        def _(e):
            dma_drain(e, sw)
        block.gpsimd(_)

    close_scope(phA)

    phBB2 = []
    s_ctx = SBuf(phBB2, "s_ctx", [128, EC * OWN])

    phB = []
    s_K = SBuf(phB, "s_K", [128, S])
    s_V = SBuf(phB, "s_V", [128, 2 * SC * 65])
    s_Q = SBuf(phB, "s_Q", [128, OWN])
    s_maskT = SBuf(phB, "s_maskT", [128, (TAIL // 128) * QS], BF16)
    s_ptf = SBuf(phB, "s_ptf", [128, 2 * 512])
    s_rt = SBuf(phB, "s_rt", [128, 512])
    s_rb = SBuf(phB, "s_rb", [64, 512])
    s_o2 = SBuf(phB, "s_o2", [65, 64])

    with nc.named_scope("phaseB"), nc.Block() as block:
        fenceA = (pe.tok(), ev.tok(), ex.tok())
        mask_tok = [None]

        def _(e):
            t2 = dma(e, ld, s_maskT[:].rearrange("p (c q) -> p c q", q=QS),
                     maskT_in.rearrange("(c p) q -> p c q", p=128))
            t3 = dma(e, ld, s_o2[64:65, :], onesrow[0:1, 0:64])
            mask_tok[0] = (t2, t3)
        block.sync(_)

        vone_tok = [None, None]
        head_end = [None, None]
        kvq_toks = [None, None]
        ptf_free = [None, None]
        rb_free = [None]
        ctx_banks = [B[4], B[5]]
        bc_bank = B[6]

        def emit_kvq_load(h):
            hb = h % 2

            def _(e, h=h, hb=hb):
                W(e, head_end[hb])
                if h < 2:
                    W(e, fenceA)
                t1 = dma(e, ld, s_K[hb * 64:(hb + 1) * 64, :], Ks[h * D:(h + 1) * D, :])
                vdst = s_V[:].rearrange("p (b c d) -> p b c d", b=2, d=65)[:, hb, :, 0:64]
                t2 = dma(e, ld, vdst, Vs[h].rearrange("(c p) d -> p c d", p=128))
                t3 = dma(e, ld, s_Q[hb * 64:(hb + 1) * 64, :],
                         Qs[h * D:(h + 1) * D, :])
                kvq_toks[hb] = (t1, t2, t3)
            block.sync(_)

            def _(e, hb=hb):
                W(e, kvq_toks[hb][1])
                vv = s_V[:].rearrange("p (b c d) -> p b c d", b=2, d=65)
                ins = e.tensor_scalar(vv[:, hb, :, 64:65], vv[:, hb, :, 63:64],
                                      0.0, 1.0, ALU.mult, ALU.add)
                vone_tok[hb] = ev.inc(ins)
            block.vector(_)

        def emit_slot(h, k):
            ek = (k + 1) * TAIL
            nkc128 = ek // 128
            hb = h % 2
            cbank = ctx_banks[k % 2]
            for kc0 in range(0, nkc128, 2):
                kcs = [kc0, kc0 + 1]
                banks = [B[kc % 2] for kc in kcs]

                def _(e, kcs=kcs, banks=banks, hb=hb, k=k, h=h, nkc128=nkc128):
                    for b_ in banks:
                        W(e, b_.free)
                    if kcs[0] == 0 and k == 0:
                        for tk in kvq_toks[hb]:
                            W(e, tk)
                        if h == 0:
                            W(e, mask_tok[0])
                    for kc, b_ in zip(kcs, banks):
                        lhs = s_K[hb * 64:(hb + 1) * 64, kc * 128: kc * 128 + 128]
                        rhs = s_Q[hb * 64:(hb + 1) * 64, k * QS: k * QS + 512]
                        tailc = kc >= nkc128 - 8
                        ins = e.matmul(b_.ps[:], lhs, rhs, start=True,
                                       stop=not tailc)
                        if tailc:
                            tc = kc - (nkc128 - 8)
                            ins = e.matmul(b_.ps[:], s_idb[:],
                                           s_maskT[:, tc * QS: tc * QS + 512],
                                           start=False, stop=True)
                        b_.ready = pe.inc(ins)
                block.tensor(_)

                for kc, b_ in zip(kcs, banks):
                    ps_ = kc % 2

                    def _(e, kc=kc, b_=b_, ps_=ps_):
                        W(e, b_.ready)
                        W(e, ptf_free[ps_])
                        ins = e.activation(s_ptf[:, ps_ * 512: ps_ * 512 + 512],
                                           b_.ps[:], AF.Exp, bias=0.0, scale=ISQ)
                        b_.free = ex.inc(ins)
                    block.scalar(_)

                ptf_tok = ex.tok()

                def _(e, kcs=kcs, hb=hb, ptf_tok=ptf_tok, cbank=cbank,
                      nkc128=nkc128, h=h, k=k):
                    W(e, ptf_tok)
                    if kcs[0] == 0:
                        W(e, cbank.free)
                        if k == 0:
                            W(e, vone_tok[hb])
                    for kc in kcs:
                        ps_ = kc % 2
                        lhs = s_V[:, hb * SC * 65 + kc * 65: hb * SC * 65 + kc * 65 + 65]
                        rhs = s_ptf[:, ps_ * 512: ps_ * 512 + 512]
                        ins = e.matmul(cbank.ps[0:65, 0:512], lhs, rhs,
                                       start=(kc == 0), stop=(kc == nkc128 - 1))
                        t = pe.inc(ins)
                        ptf_free[ps_] = pe.tok()
                        if kc == nkc128 - 1:
                            cbank.ready = t
                block.tensor(_)

            def _(e, cbank=cbank):
                W(e, cbank.ready)
                ins = e.reciprocal(s_rt[64:65, :], cbank.ps[64:65, 0:512])
                ev.inc(ins)
            block.vector(_)
            rt_tok = ev.tok()

            def _(e, rt_tok=rt_tok):
                W(e, rt_tok)
                W(e, bc_bank.free)
                ins = e.matmul(bc_bank.ps[0:64, 0:512], s_o2[64:65, :],
                               s_rt[64:65, :], start=True, stop=True)
                bc_bank.ready = pe.inc(ins)
            block.tensor(_)

            def _(e):
                W(e, bc_bank.ready)
                W(e, rb_free[0])
                ins = e.tensor_copy(s_rb[:], bc_bank.ps[0:64, 0:512])
                bc_bank.free = ev.inc(ins)
            block.vector(_)
            rb_tok = ev.tok()

            def _(e, h=h, k=k, cbank=cbank, rb_tok=rb_tok):
                W(e, cbank.ready)
                W(e, rb_tok)
                ins = e.tensor_tensor(
                    s_ctx[(h % 2) * 64:(h % 2) * 64 + 64,
                          (h // 2) * OWN + k * QS:(h // 2) * OWN + k * QS + QS],
                    cbank.ps[0:64, 0:QS], s_rb[:], ALU.mult)
                cbank.free = ev.inc(ins)
            block.vector(_)
            rb_free[0] = ev.tok()

        for h in range(H):
            emit_kvq_load(h)
            for k in range(NSLOT):
                emit_slot(h, k)
            head_end[h % 2] = pe.tok()

    close_scope(phB)

    def emit_ln(block, src, tmp, dst, s_sq, s_row, pre_tok):
        bsum, bsq, bbc1, bbc2 = B[0], B[1], B[2], B[3]
        sq_toks = []
        sqmm_toks = []

        for ofc in range(EC):
            def _(e, ofc=ofc):
                if ofc == 0:
                    W(e, pre_tok)
                if ofc >= 2:
                    W(e, sqmm_toks[ofc - 2])
                ins = e.activation(s_sq[:, (ofc % 2) * 512:(ofc % 2) * 512 + 512],
                                   src(ofc), AF.Square, bias=0.0, scale=1.0)
                sq_toks.append(ex.inc(ins))
            block.scalar(_)

            def _(e, ofc=ofc):
                if ofc == 0:
                    W(e, pre_tok)
                    W(e, bsum.free)
                    W(e, bsq.free)
                ins = e.matmul(bsum.ps[0:1, 0:512], s_1c[:], src(ofc),
                               start=(ofc == 0), stop=(ofc == EC - 1))
                if ofc == EC - 1:
                    bsum.ready = pe.inc(ins)
                W(e, sq_toks[ofc])
                ins = e.matmul(bsq.ps[0:1, 0:512], s_1c[:],
                               s_sq[:, (ofc % 2) * 512:(ofc % 2) * 512 + 512],
                               start=(ofc == 0), stop=(ofc == EC - 1))
                sqmm_toks.append(pe.inc(ins))
            block.tensor(_)
        bsq.ready = sqmm_toks[-1]

        r = lambda i: s_row[0:1, i * 512:(i + 1) * 512]

        def _(e):
            W(e, bsum.ready)
            ins = e.tensor_copy(r(0), bsum.ps[0:1, 0:512])
            bsum.free = ev.inc(ins)
            W(e, bsq.ready)
            ins = e.tensor_copy(r(1), bsq.ps[0:1, 0:512])
            bsq.free = ev.inc(ins)
            e.wait_ge(ev.h, ev.n)
            ins = e.tensor_scalar(r(2), r(0), 1.0 / E, None, ALU.mult)
            ev.inc(ins)
            ins = e.tensor_scalar(r(3), r(1), 1.0 / E, None, ALU.mult)
            ev.inc(ins)
            e.wait_ge(ev.h, ev.n)
            ins = e.tensor_tensor(r(4), r(2), r(2), ALU.mult)
            ev.inc(ins)
            e.wait_ge(ev.h, ev.n)
            ins = e.tensor_tensor(r(4), r(3), r(4), ALU.subtract)
            ev.inc(ins)
        block.vector(_)
        var_tok = ev.tok()

        def _(e, var_tok=var_tok):
            W(e, var_tok)
            ins = e.activation(r(3), r(4), AF.Sqrt, bias=EPS, scale=1.0)
            ex.inc(ins)
        block.scalar(_)
        sd_tok = ex.tok()

        def _(e, sd_tok=sd_tok):
            W(e, sd_tok)
            ins = e.reciprocal(r(5), r(3))
            ev.inc(ins)
            e.wait_ge(ev.h, ev.n)
            ins = e.tensor_tensor(r(4), r(2), r(5), ALU.mult)
            ev.inc(ins)
        block.vector(_)
        nmr_tok = ev.tok()

        def _(e, nmr_tok=nmr_tok):
            W(e, nmr_tok)
            W(e, bbc1.free)
            ins = e.matmul(bbc1.ps[:, 0:512], s_1r[:], r(5), start=True, stop=True)
            bbc1.ready = pe.inc(ins)
            W(e, bbc2.free)
            ins = e.matmul(bbc2.ps[:, 0:512], s_1r[:], r(4), start=True, stop=True)
            bbc2.ready = pe.inc(ins)
        block.tensor(_)

        def _(e):
            W(e, bbc1.ready)
            W(e, bbc2.ready)
            for ofc in range(EC):
                ins = e.tensor_tensor(tmp(ofc), src(ofc), bbc1.ps[:, 0:512], ALU.mult)
                if ofc == EC - 1:
                    bbc1.free = ev.inc(ins)
            e.wait_ge(ev.h, ev.n)
            for ofc in range(EC):
                ins = e.tensor_tensor(dst(ofc), tmp(ofc), bbc2.ps[:, 0:512], ALU.subtract)
                if ofc == EC - 1:
                    bbc2.free = ev.inc(ins)
        block.vector(_)
        return ev.tok()

    phB2 = []
    s_woT = SBuf(phB2, "s_woT", [128, EC * E])
    s_xo = SBuf(phB2, "s_xo", [128, EC * 512])
    s_xtm2 = SBuf(phB2, "s_xtm2", [128, 4 * E])
    s_t1 = SBuf(phB2, "s_t1", [128, EC * 512])
    s_sqA = SBuf(phB2, "s_sqA", [128, 2 * 512])
    s_rowA = SBuf(phB2, "s_rowA", [1, 6 * 512])

    with nc.named_scope("phaseB2"), nc.Block() as block:
        xtm2_free = [None]
        for rr in range(8):
            tok_holder = []

            def _(e, rr=rr):
                W(e, xtm2_free[0])
                tok_holder.append(dma(e, ld, s_xtm2[:, 0:E],
                                      opw[rr * 128:(rr + 1) * 128, :]))
            block.sync(_)
            jobs = []
            for cfc in range(EC):
                src = s_xtm2[:, cfc * 128:cfc * 128 + 128]
                dst = s_woT[:, cfc * E + rr * 128: cfc * E + rr * 128 + 128]
                jobs.append((src, dst, s_id[:]))
            emit_transposes(block, jobs, [B[0], B[1]], pre_toks=(tok_holder[0],))
            xtm2_free[0] = pe.tok()
        woT_tok = ev.tok()

        for nt in range(4):
            tok_holder = []
            xo_prev = ev.tok()

            def _(e, nt=nt):
                W(e, xtm2_free[0])
                tok_holder.append(dma(
                    e, ld,
                    s_xtm2[:].rearrange("p (t d) -> p t d", d=E),
                    x_own[nt * 512:(nt + 1) * 512, :].rearrange("(t p) d -> p t d", p=128)))
            block.sync(_)
            jobs = []
            for j in range(4 * EC):
                t, ifc = divmod(j, EC)
                srcj = s_xtm2[:, t * E + ifc * 128: t * E + ifc * 128 + 128]
                dstj = s_xo[:, ifc * 512 + t * 128: ifc * 512 + t * 128 + 128]
                jobs.append((srcj, dstj, s_id[:]))
            emit_transposes(block, jobs, [B[2], B[3]],
                            pre_toks=(tok_holder[0],), evac_pre_toks=(xo_prev,))
            xtm2_free[0] = pe.tok()
            xo_tok = ev.tok()

            evac_toks = []
            for ofm in range(EC):
                gb = B[4 + ofm % 2]

                def _(e, ofm=ofm, gb=gb, nt=nt):
                    W(e, gb.free)
                    if nt == 0 and ofm == 0:
                        W(e, woT_tok)
                    for cfc in range(EC):
                        lhs = s_woT[:, cfc * E + ofm * 128: cfc * E + ofm * 128 + 128]
                        rhs = s_ctx[:, cfc * OWN + nt * 512: cfc * OWN + nt * 512 + 512]
                        ins = e.matmul(gb.ps[:], lhs, rhs,
                                       start=(cfc == 0), stop=(cfc == EC - 1))
                    gb.ready = pe.inc(ins)
                block.tensor(_)

                def _(e, ofm=ofm, gb=gb):
                    W(e, gb.ready)
                    ins = e.activation(s_t1[:, ofm * 512: ofm * 512 + 512],
                                       gb.ps[:], AF.Identity,
                                       bias=s_bo[:, ofm:ofm + 1], scale=1.0)
                    gb.free = ex.inc(ins)
                block.scalar(_)
                evac_toks.append(ex.tok())

            def _(e, nt=nt, et=evac_toks[-1]):
                W(e, et)
                W(e, xo_tok)
                for ofc in range(EC):
                    ins = e.tensor_tensor(
                        s_ctx[:, ofc * OWN + nt * 512: ofc * OWN + nt * 512 + 512],
                        s_xo[:, ofc * 512: ofc * 512 + 512],
                        s_t1[:, ofc * 512: ofc * 512 + 512], ALU.add)
                ev.inc(ins)
            block.vector(_)
            add_tok = ev.tok()

            src = lambda ofc, nt=nt: s_ctx[:, ofc * OWN + nt * 512: ofc * OWN + nt * 512 + 512]
            tmp = lambda ofc, nt=nt: s_xo[:, ofc * 512: ofc * 512 + 512]
            ln_tok = emit_ln(block, src, tmp, src, s_sqA, s_rowA, add_tok)

            def _(e, nt=nt, ln_tok=ln_tok):
                W(e, ln_tok)
                sctx = s_ctx[:].rearrange("p (c t) -> p c t", t=OWN)[:, :, nt * 512:(nt + 1) * 512]
                dstx = X1s.rearrange("(c p) t -> p c t", p=128)[:, :, nt * 512:(nt + 1) * 512]
                dma(e, sw, dstx, sctx)
            block.gpsimd(_)

        def _(e):
            dma_drain(e, sw)
        block.gpsimd(_)

    close_scope(phB2)
    close_scope(phBB2)

    phC = []
    s_x1a = SBuf(phC, "s_x1a", [128, EC * 1024], BF16)
    s_x1b = SBuf(phC, "s_x1b", [128, EC * 1024], BF16)
    s_x2 = SBuf(phC, "s_x2", [128, EC * 1024])
    s_hf = SBuf(phC, "s_hf", [128, 4 * 512])
    s_ha = SBuf(phC, "s_ha", [128, 2 * 4 * 512], BF16)
    s_hb = SBuf(phC, "s_hb", [128, 2 * 4 * 512], BF16)
    s_wst = SBuf(phC, "s_wst", [128, 4096])
    s_w1a = SBuf(phC, "s_w1a", [128, 2 * EC * 512], BF16)
    s_w1b = SBuf(phC, "s_w1b", [128, 2 * EC * 512], BF16)
    s_w2a = SBuf(phC, "s_w2a", [128, 2 * 4 * 1024], BF16)
    s_w2b = SBuf(phC, "s_w2b", [128, 2 * 4 * 1024], BF16)
    s_wga = SBuf(phC, "s_wga", [128, EC * EXP], BF16)
    s_wgb = SBuf(phC, "s_wgb", [128, EC * EXP], BF16)

    s_gt = SBuf(phC, "s_gt", [128, 8 * EXP])
    s_ge = SBuf(phC, "s_ge", [128, 8 * EXP])
    s_gst = SBuf(phC, "s_gst", [128, 32])
    s_mf = SBuf(phC, "s_mf", [EXP, 1024])
    s_gl = s_mf
    s_mb = SBuf(phC, "s_mb", [128, 2 * 512])
    s_tf = SBuf(phC, "s_tf", [128, 2 * 512])
    s_sqC = s_hf
    s_rowC = SBuf(phC, "s_rowC", [1, 6 * 512])
    s_ot = SBuf(phC, "s_ot", [128, 1024])

    with nc.named_scope("phaseC"), nc.Block() as block:
        x1h_free = [None]
        x1ch_free = [None]
        x1_tok = [None]
        half_fence = [None]
        tf_free = [None] * 2
        out_free = [None]
        slabbuf_free = [None, None]
        h_free = [None, None]
        hf_free = [None]
        mb_free = [None]

        def _(e):
            ins = e.tensor_copy(s_wga[:], s_wg[:])
            ev.inc(ins)
            e.wait_ge(ev.h, ev.n)
            ins = e.tensor_tensor(s_wgb[:], s_wg[:], s_wga[:], ALU.subtract)
            ev.inc(ins)
        block.vector(_)
        wg_tok = ev.tok()

        for half in range(2):

            x1sp_tok = None
            for chk in range(2):
                def _(e, half=half, chk=chk):
                    if chk == 0 and x1h_free[0] is not None:
                        for tk in x1h_free[0]:
                            W(e, tk)
                    if chk == 1:
                        W(e, x1ch_free[0])
                    src = X1s.rearrange("(c p) t -> p c t", p=128)[
                        :, chk * 4:(chk + 1) * 4, half * 1024:(half + 1) * 1024]
                    dst = s_wst[:].rearrange("p (c t) -> p c t", t=1024)
                    x1_tok[0] = dma(e, ld, dst, src)
                block.sync(_)

                def _(e, half=half, chk=chk):
                    W(e, x1_tok[0])
                    if chk == 0 and half_fence[0] is not None:
                        for tk in half_fence[0]:
                            W(e, tk)
                    ins = e.tensor_copy(s_x1a[:, chk * 4096:(chk + 1) * 4096],
                                        s_wst[:])
                    ev.inc(ins)
                    e.wait_ge(ev.h, ev.n)
                    ins = e.tensor_tensor(s_x1b[:, chk * 4096:(chk + 1) * 4096],
                                          s_wst[:],
                                          s_x1a[:, chk * 4096:(chk + 1) * 4096],
                                          ALU.subtract)
                    ev.inc(ins)
                block.vector(_)
                x1ch_free[0] = ev.tok()
            x1sp_tok = ev.tok()
            stage_free_init = ev.tok()

            gl_toks = []
            for nt2 in range(2):
                gb = B[0]

                def _(e, nt2=nt2, gb=gb):
                    W(e, gb.free)
                    if nt2 == 0:
                        W(e, x1sp_tok)
                        W(e, wg_tok)
                    first = True
                    for dc in range(EC):
                        for (wg_, x_) in ((s_wga, s_x1a), (s_wgb, s_x1a), (s_wga, s_x1b)):
                            ins = e.matmul(gb.ps[0:EXP, 0:512],
                                           wg_[:, dc * EXP:(dc + 1) * EXP],
                                           x_[:, dc * 1024 + nt2 * 512: dc * 1024 + nt2 * 512 + 512],
                                           start=first, stop=(dc == EC - 1 and x_ is s_x1b))
                            first = False
                    gb.ready = pe.inc(ins)
                block.tensor(_)

                def _(e, nt2=nt2, gb=gb):
                    W(e, gb.ready)
                    ins = e.activation(s_gl[:, nt2 * 512: nt2 * 512 + 512],
                                       gb.ps[0:EXP, 0:512], AF.Identity,
                                       bias=s_bg[:], scale=1.0)
                    gb.free = ex.inc(ins)
                block.scalar(_)
                gl_toks.append(ex.tok())

            jobs = []
            for ti in range(8):
                jobs.append((s_gl[0:EXP, ti * 128:(ti + 1) * 128],
                             s_gt[:, ti * EXP:(ti + 1) * EXP], s_id[0:EXP, 0:EXP]))
            gt_tok = emit_transposes(block, jobs, [B[2], B[3]],
                                     pre_toks=(gl_toks[-1],))

            def _(e, gt_tok=gt_tok):
                W(e, gt_tok)
                for ti in range(8):
                    ins = e.reduce_max(s_gst[:, ti * 4: ti * 4 + 1],
                                       s_gt[:, ti * EXP:(ti + 1) * EXP], axis=AXX)
                    ev.inc(ins)
                e.wait_ge(ev.h, ev.n)
                for ti in range(8):
                    ins = e.tensor_scalar(s_gst[:, ti * 4 + 1: ti * 4 + 2],
                                          s_gst[:, ti * 4: ti * 4 + 1],
                                          -1.0, None, ALU.mult)
                    ev.inc(ins)
            block.vector(_)
            gnm_tok = ev.tok()

            def _(e, gnm_tok=gnm_tok):
                W(e, gnm_tok)
                for ti in range(8):
                    ins = e.activation(s_ge[:, ti * EXP: ti * EXP + EXP],
                                       s_gt[:, ti * EXP:(ti + 1) * EXP],
                                       AF.Exp, bias=s_gst[:, ti * 4 + 1: ti * 4 + 2],
                                       scale=1.0,
                                       accum_out=s_gst[:, ti * 4 + 2: ti * 4 + 3])
                    ex.inc(ins)
            block.scalar(_)
            gs_tok = ex.tok()

            def _(e, gs_tok=gs_tok):
                W(e, gs_tok)
                for ti in range(8):
                    ins = e.reciprocal(s_gst[:, ti * 4 + 3: ti * 4 + 4],
                                       s_gst[:, ti * 4 + 2: ti * 4 + 3])
                    ev.inc(ins)
                e.wait_ge(ev.h, ev.n)
                for ti in range(8):
                    ins = e.tensor_scalar(s_gt[:, ti * EXP:(ti + 1) * EXP],
                                          s_gt[:, ti * EXP:(ti + 1) * EXP],
                                          s_gst[:, ti * 4: ti * 4 + 1],
                                          s_gst[:, ti * 4 + 3: ti * 4 + 4],
                                          ALU.is_equal, ALU.mult)
                    ev.inc(ins)
            block.vector(_)
            mt_tok = ev.tok()

            jobs = []
            for ti in range(8):
                jobs.append((s_gt[:, ti * EXP:(ti + 1) * EXP],
                             s_mf[0:EXP, ti * 128:(ti + 1) * 128], s_id[:]))
            mf_tok = emit_transposes(block, jobs, [B[2], B[3]], pre_toks=(mt_tok,))

            gp_prev = gp.tok()
            for nt2 in range(2):
                for om in range(EC):
                    gb = B[4 + om % 2]

                    def _(e, nt2=nt2, om=om, gb=gb, mf_tok=mf_tok):
                        W(e, gb.free)
                        if nt2 == 0 and om == 0:
                            W(e, mf_tok)
                        ins = e.matmul(gb.ps[:], s_b2[0:EXP, om * 128:(om + 1) * 128],
                                       s_mf[0:EXP, nt2 * 512: nt2 * 512 + 512],
                                       start=True, stop=True)
                        gb.ready = pe.inc(ins)
                    block.tensor(_)

                    def _(e, nt2=nt2, om=om, gb=gb, gp_prev=gp_prev):
                        W(e, gb.ready)
                        W(e, gp_prev)
                        ins = e.tensor_copy(
                            s_x2[:, om * 1024 + nt2 * 512: om * 1024 + nt2 * 512 + 512],
                            gb.ps[:])
                        gb.free = ev.inc(ins)
                    block.vector(_)

            slabs = [(eix, hg) for eix in range(EXP) for hg in range(8)]
            stage_tok = [None]
            stage_free = [stage_free_init]
            split_tok = [None, None]
            mb_toks = {}

            def emit_slab_load_split(si):
                buf = si % 2
                eix, hg = slabs[si]
                for part in range(2):
                    def _(e, si=si, part=part, eix=eix, hg=hg):
                        if stage_free[0] is not None:
                            W(e, stage_free[0])
                        if part == 0:
                            sl = w1[eix][:, hg * 512:(hg + 1) * 512].rearrange(
                                "(c p) h -> p c h", p=128)
                            stage_tok[0] = dma(
                                e, ld, s_wst[:].rearrange("p (c h) -> p c h", h=512), sl)
                        else:
                            sl = w2[eix][hg * 512:(hg + 1) * 512, :].rearrange(
                                "(c p) o -> p c o", p=128)
                            stage_tok[0] = dma(
                                e, ld, s_wst[:].rearrange("p (c o) -> p c o", o=E), sl)
                    block.sync(_)

                    def _(e, buf=buf, part=part):
                        W(e, stage_tok[0])
                        if part == 0 and slabbuf_free[buf] is not None:
                            for tk in (slabbuf_free[buf] if isinstance(slabbuf_free[buf], list)
                                       else [slabbuf_free[buf]]):
                                W(e, tk)
                        wa = s_w1a if part == 0 else s_w2a
                        ins = e.tensor_copy(wa[:, buf * 4096:(buf + 1) * 4096], s_wst[:])
                        gp.inc(ins)
                    block.gpsimd(_)
                    hi_tok = gp.tok()

                    def _(e, buf=buf, part=part, hi_tok=hi_tok):
                        W(e, hi_tok)
                        wa = s_w1a if part == 0 else s_w2a
                        wb = s_w1b if part == 0 else s_w2b
                        ins = e.tensor_tensor(wb[:, buf * 4096:(buf + 1) * 4096],
                                              s_wst[:],
                                              wa[:, buf * 4096:(buf + 1) * 4096],
                                              ALU.subtract)
                        t = ev.inc(ins)
                        if part == 1:
                            split_tok[buf] = t
                        stage_free[0] = ev.tok()
                    block.vector(_)

            def emit_mb(eix):
                toks = []
                for nt2 in range(2):
                    gb = B[1]

                    def _(e, eix=eix, nt2=nt2, gb=gb, mf_tok=mf_tok):
                        W(e, gb.free)
                        if eix == 0 and nt2 == 0:
                            W(e, mf_tok)
                        ins = e.matmul(gb.ps[:],
                                       s_sel[0:EXP, eix * 128:(eix + 1) * 128],
                                       s_mf[0:EXP, nt2 * 512: nt2 * 512 + 512],
                                       start=True, stop=True)
                        gb.ready = pe.inc(ins)
                    block.tensor(_)

                    def _(e, nt2=nt2, gb=gb):
                        W(e, gb.ready)
                        W(e, mb_free[0])
                        ins = e.tensor_copy(s_mb[:, nt2 * 512: nt2 * 512 + 512], gb.ps[:])
                        gb.free = ev.inc(ins)
                    block.vector(_)
                    toks.append(ev.tok())
                return toks

            emit_slab_load_split(0)
            for si, (eix, hg) in enumerate(slabs):
                buf = si % 2
                if hg == 0:
                    mb_toks[eix] = emit_mb(eix)
                if si + 1 < len(slabs):
                    emit_slab_load_split(si + 1)
                my_split = split_tok[buf]
                for nt2 in range(2):
                    hbuf = (si * 2 + nt2) % 2
                    h_toks = []
                    for hm in range(4):
                        gb = B[4 + hm % 2]

                        def _(e, nt2=nt2, hm=hm, gb=gb, buf=buf,
                              st_=my_split):
                            W(e, gb.free)
                            if hm == 0 and nt2 == 0:
                                W(e, st_)
                            first = True
                            for dc in range(EC):
                                off = buf * 4096 + dc * 512 + hm * 128
                                for (w_, x_) in ((s_w1a, s_x1a), (s_w1b, s_x1a),
                                                 (s_w1a, s_x1b)):
                                    ins = e.matmul(
                                        gb.ps[:], w_[:, off: off + 128],
                                        x_[:, dc * 1024 + nt2 * 512: dc * 1024 + nt2 * 512 + 512],
                                        start=first,
                                        stop=(dc == EC - 1 and x_ is s_x1b))
                                    first = False
                            gb.ready = pe.inc(ins)
                        block.tensor(_)

                        def _(e, eix=eix, hg=hg, hm=hm, gb=gb, hbuf=hbuf):
                            W(e, gb.ready)
                            if hm == 0:
                                W(e, h_free[hbuf])
                                W(e, hf_free[0])
                            col = eix * HC + hg * 4 + hm
                            ins = e.activation(
                                s_hf[:, hm * 512: hm * 512 + 512], gb.ps[:],
                                AF.Silu, bias=s_b1[:, col:col + 1], scale=1.0)
                            ex.inc(ins)
                            ins = e.activation(
                                s_ha[:, hbuf * 2048 + hm * 512: hbuf * 2048 + hm * 512 + 512],
                                gb.ps[:],
                                AF.Silu, bias=s_b1[:, col:col + 1], scale=1.0)
                            gb.free = ex.inc(ins)
                        block.scalar(_)
                        h_toks.append(ex.tok())

                    def _(e, hbuf=hbuf, ht=h_toks):
                        W(e, ht[-1])
                        for hm in range(4):
                            ins = e.tensor_tensor(
                                s_hb[:, hbuf * 2048 + hm * 512: hbuf * 2048 + hm * 512 + 512],
                                s_hf[:, hm * 512: hm * 512 + 512],
                                s_ha[:, hbuf * 2048 + hm * 512: hbuf * 2048 + hm * 512 + 512],
                                ALU.subtract)
                            ev.inc(ins)
                    block.vector(_)
                    hsp_tok = ev.tok()
                    hf_free[0] = hsp_tok

                    for om in range(EC):
                        gb = B[6 + om % 2]

                        def _(e, om=om, gb=gb, ht=hsp_tok, buf=buf, hbuf=hbuf):
                            W(e, gb.free)
                            if om == 0:
                                W(e, ht)
                            first = True
                            for hc in range(4):
                                woff = buf * 4096 + hc * 1024 + om * 128
                                hoff = hbuf * 2048 + hc * 512
                                for (w_, h_) in ((s_w2a, s_ha), (s_w2b, s_ha),
                                                 (s_w2a, s_hb)):
                                    ins = e.matmul(
                                        gb.ps[:], w_[:, woff: woff + 128],
                                        h_[:, hoff: hoff + 512],
                                        start=first,
                                        stop=(hc == 3 and h_ is s_hb))
                                    first = False
                            gb.ready = pe.inc(ins)
                        block.tensor(_)

                        tslot = om % 2

                        def _(e, om=om, gb=gb, nt2=nt2, tslot=tslot, eix=eix,
                              mbts=mb_toks[eix]):
                            W(e, gb.ready)
                            if om == 0:
                                W(e, mbts[nt2])
                            W(e, tf_free[tslot])
                            ins = e.tensor_tensor(
                                s_tf[:, tslot * 512: tslot * 512 + 512],
                                gb.ps[:], s_mb[:, nt2 * 512: nt2 * 512 + 512],
                                ALU.mult)
                            gb.free = ev.inc(ins)
                        block.vector(_)
                        t_tok = ev.tok()

                        def _(e, om=om, nt2=nt2, tslot=tslot, t_tok=t_tok):
                            W(e, t_tok)
                            ins = e.tensor_add(
                                s_x2[:, om * 1024 + nt2 * 512: om * 1024 + nt2 * 512 + 512],
                                s_x2[:, om * 1024 + nt2 * 512: om * 1024 + nt2 * 512 + 512],
                                s_tf[:, tslot * 512: tslot * 512 + 512])
                            gp.inc(ins)
                        block.gpsimd(_)
                        tf_free[tslot] = gp.tok()
                    h_free[hbuf] = pe.tok()
                slabbuf_free[buf] = pe.tok()
                mb_free[0] = ev.tok()

            folds_tok = gp.tok()

            def _(e, folds_tok=folds_tok):
                W(e, folds_tok)
                for ofc in range(EC):
                    for nt2 in range(2):
                        ins = e.tensor_tensor(
                            s_x2[:, ofc * 1024 + nt2 * 512: ofc * 1024 + nt2 * 512 + 512],
                            s_x2[:, ofc * 1024 + nt2 * 512: ofc * 1024 + nt2 * 512 + 512],
                            s_x1a[:, ofc * 1024 + nt2 * 512: ofc * 1024 + nt2 * 512 + 512],
                            ALU.add)
                        ev.inc(ins)
                e.wait_ge(ev.h, ev.n)
                for ofc in range(EC):
                    for nt2 in range(2):
                        ins = e.tensor_tensor(
                            s_x2[:, ofc * 1024 + nt2 * 512: ofc * 1024 + nt2 * 512 + 512],
                            s_x2[:, ofc * 1024 + nt2 * 512: ofc * 1024 + nt2 * 512 + 512],
                            s_x1b[:, ofc * 1024 + nt2 * 512: ofc * 1024 + nt2 * 512 + 512],
                            ALU.add)
                        ev.inc(ins)
            block.vector(_)
            x2pre_tok = ev.tok()

            for nt2 in range(2):
                src = lambda ofc, nt2=nt2: s_x2[:, ofc * 1024 + nt2 * 512:
                                                ofc * 1024 + nt2 * 512 + 512]
                tmp = lambda ofc, nt2=nt2: s_wst[:, ofc * 512: ofc * 512 + 512]
                ln_tok = emit_ln(block, src, tmp, src, s_sqC, s_rowC, x2pre_tok)

                for tt4 in range(4):
                    jobs = []
                    for ofc in range(EC):
                        srcT = s_x2[:, ofc * 1024 + nt2 * 512 + tt4 * 128:
                                    ofc * 1024 + nt2 * 512 + tt4 * 128 + 128]
                        dstT = s_ot[:, ofc * 128: ofc * 128 + 128]
                        jobs.append((srcT, dstT, s_id[:]))
                    pre = (ln_tok,) if tt4 == 0 else ()
                    ot_tok = emit_transposes(block, jobs, [B[4], B[5]],
                                             pre_toks=pre,
                                             evac_pre_toks=(out_free[0],))

                    def _(e, half=half, nt2=nt2, tt4=tt4, ot_tok=ot_tok):
                        W(e, ot_tok)
                        row = half * 1024 + nt2 * 512 + tt4 * 128
                        out_free[0] = dma(e, sw, out[row:row + 128, :], s_ot[:])
                    block.gpsimd(_)

            x1h_free[0] = (pe.tok(), ev.tok(), ex.tok())
            slabbuf_free[0] = [pe.tok(), ev.tok(), ex.tok()]
            slabbuf_free[1] = [pe.tok(), ev.tok(), ex.tok()]
            half_fence[0] = (pe.tok(), ev.tok(), ex.tok(), gp.tok())

        def _(e):
            dma_drain(e, sw)
        block.gpsimd(_)

    close_scope(phC)
    for b in reversed(B):
        b.cm.__exit__(None, None, None)
    close_scope(live)
    return nc


def _make_consts():
    identf = np.eye(128, dtype=np.float32)
    identbf = np.eye(128, dtype=np.float32).astype(ml_dtypes.bfloat16)
    onescol = np.ones((128, 1), dtype=np.float32)
    onesrow = np.ones((1, 128), dtype=np.float32)
    cvec = np.zeros((128, 4), dtype=np.float32)
    cvec[:, 0] = EPS
    selm = np.zeros((EXP, EXP * 128), dtype=np.float32)
    for e_ in range(EXP):
        selm[e_, e_ * 128:(e_ + 1) * 128] = 1.0
    return identf, identbf, onescol, onesrow, cvec, selm


def _make_mask(par):
    i = np.arange(QS)[:, None]
    j = np.arange(TAIL)[None, :]
    m = np.where(j > par * 512 + i, NEG, 0.0).astype(np.float32)
    return np.ascontiguousarray(m.astype(ml_dtypes.bfloat16))


def _make_maskT(par):
    j = np.arange(TAIL)[:, None]
    i = np.arange(QS)[None, :]
    m = np.where(j > par * 512 + i, NEG, 0.0).astype(np.float32)
    return np.ascontiguousarray(m.astype(ml_dtypes.bfloat16))


def kernel(L0_attn_x, L0_attn_in_proj_w, L0_attn_in_proj_b, L0_attn_out_proj_w,
           L0_attn_out_proj_b, L0_ffn_Wg, L0_ffn_bg, L0_ffn_W1, L0_ffn_b1,
           L0_ffn_W2, L0_ffn_b2):
    x = np.ascontiguousarray(np.asarray(L0_attn_x, dtype=np.float32))
    Bb = x.shape[0]
    identf, identbf, onescol, onesrow, cvec, selm = _make_consts()
    weights = dict(
        ipw=np.ascontiguousarray(np.asarray(L0_attn_in_proj_w, np.float32)),
        ipb=np.ascontiguousarray(np.asarray(L0_attn_in_proj_b, np.float32)),
        opw=np.ascontiguousarray(np.asarray(L0_attn_out_proj_w, np.float32)),
        opb=np.ascontiguousarray(np.asarray(L0_attn_out_proj_b, np.float32)),
        wgw=np.ascontiguousarray(np.asarray(L0_ffn_Wg, np.float32)),
        bgv=np.ascontiguousarray(np.asarray(L0_ffn_bg, np.float32)),
        w1=np.ascontiguousarray(np.asarray(L0_ffn_W1, np.float32)),
        b1=np.ascontiguousarray(np.asarray(L0_ffn_b1, np.float32)),
        w2=np.ascontiguousarray(np.asarray(L0_ffn_W2, np.float32)),
        b2=np.ascontiguousarray(np.asarray(L0_ffn_b2, np.float32)),
        ident=identf, identb=identbf, onescol=onescol, onesrow=onesrow,
        cvec=cvec, selm=selm,
    )
    masks = [_make_mask(0), _make_mask(1)]
    masksT = [_make_maskT(0), _make_maskT(1)]
    in_maps = []
    for c in range(8):
        b, par = c // 2, c % 2
        blocks = [2 * k + par for k in range(NSLOT)]
        xo = np.ascontiguousarray(
            np.concatenate([x[b, a * 512:(a + 1) * 512] for a in blocks], axis=0))
        m = dict(weights)
        m["x_full"] = np.ascontiguousarray(x[b])
        m["x_own"] = xo
        m["mask_in"] = masks[par]
        m["maskT_in"] = masksT[par]
        in_maps.append(m)

    nc = build_program()
    res = run_bass_kernel_spmd(nc, in_maps, core_ids=list(range(8)))

    y = np.empty((Bb, S, E), dtype=np.float32)
    for c in range(8):
        b, par = c // 2, c % 2
        oc = np.asarray(res.results[c]["out"])
        for k in range(NSLOT):
            a = 2 * k + par
            y[b, a * 512:(a + 1) * 512] = oc[k * 512:(k + 1) * 512]
    return y


# version 15
# speedup vs baseline: 1.3811x; 1.0640x over previous
import numpy as np
import ml_dtypes
import concourse.bass as bass
from concourse import library_config
import concourse.mybir as mybir
from concourse.bass_utils import run_bass_kernel_spmd

F32 = mybir.dt.float32
BF16 = mybir.dt.bfloat16
AF = mybir.ActivationFunctionType
ALU = mybir.AluOpType
AXX = mybir.AxisListType.X

NEG = -1e30
EPS = 1e-5

S, E, H, HID, EXP = 4096, 1024, 16, 4096, 8
D = E // H
OWN = S // 2
QS = S // 8
EC = E // 128
HC = HID // 128
TAIL = S // 4
NSLOT = 4
NQT = QS // 128
SC = S // 128
ISQ = 1.0 / float(D) ** 0.5


def ceil_div(a, b):
    return (a + b - 1) // b


class Sem:
    __slots__ = ("h", "n")

    def __init__(self, nc, name):
        self.h = nc.alloc_semaphore(name=name)
        self.n = 0

    def inc(self, instr, k=1):
        instr.then_inc(self.h, k)
        self.n += k
        return (self, self.n)

    def tok(self):
        return (self, self.n)


def W(e, tk):
    if tk is None:
        return
    if isinstance(tk, (list, tuple)) and tk and isinstance(tk[0], (list, tuple)):
        for t in tk:
            W(e, t)
        return
    if tk[1] > 0:
        e.wait_ge(tk[0].h, tk[1])


class DmaPool:
    __slots__ = ("chans", "i")

    def __init__(self, nc, name, n=8):
        self.chans = [Sem(nc, "%s%d" % (name, i)) for i in range(n)]
        self.i = 0


class Bank:
    __slots__ = ("cm", "ps", "ready", "free")

    def __init__(self, nc, i):
        self.cm = nc.psum_tensor("pbank%d" % i, [128, 512], F32)
        self.ps = self.cm.__enter__()
        self.ready = None
        self.free = None


def build_program():
    nc = bass.Bass("TRN2", target_bir_lowering=False)

    x_full = nc.dram_tensor("x_full", [S, E], F32, kind="ExternalInput").ap()
    x_own = nc.dram_tensor("x_own", [OWN, E], F32, kind="ExternalInput").ap()
    mask_in = nc.dram_tensor("mask_in", [QS, TAIL], BF16, kind="ExternalInput").ap()
    maskT_in = nc.dram_tensor("maskT_in", [TAIL, QS], BF16, kind="ExternalInput").ap()
    ipw = nc.dram_tensor("ipw", [3 * E, E], F32, kind="ExternalInput").ap()
    ipb = nc.dram_tensor("ipb", [3 * E], F32, kind="ExternalInput").ap()
    opw = nc.dram_tensor("opw", [E, E], F32, kind="ExternalInput").ap()
    opb = nc.dram_tensor("opb", [E], F32, kind="ExternalInput").ap()
    wgw = nc.dram_tensor("wgw", [E, EXP], F32, kind="ExternalInput").ap()
    bgv = nc.dram_tensor("bgv", [EXP], F32, kind="ExternalInput").ap()
    w1 = nc.dram_tensor("w1", [EXP, E, HID], F32, kind="ExternalInput").ap()
    b1 = nc.dram_tensor("b1", [EXP, HID], F32, kind="ExternalInput").ap()
    w2 = nc.dram_tensor("w2", [EXP, HID, E], F32, kind="ExternalInput").ap()
    b2 = nc.dram_tensor("b2", [EXP, E], F32, kind="ExternalInput").ap()
    ident = nc.dram_tensor("ident", [128, 128], F32, kind="ExternalInput").ap()
    identb = nc.dram_tensor("identb", [128, 128], BF16, kind="ExternalInput").ap()
    onescol = nc.dram_tensor("onescol", [128, 1], F32, kind="ExternalInput").ap()
    onesrow = nc.dram_tensor("onesrow", [1, 128], F32, kind="ExternalInput").ap()
    cvec = nc.dram_tensor("cvec", [128, 4], F32, kind="ExternalInput").ap()
    selm = nc.dram_tensor("selm", [EXP, EXP * 128], F32, kind="ExternalInput").ap()
    out = nc.dram_tensor("out", [OWN, E], F32, kind="ExternalOutput").ap()

    Ks = nc.dram_tensor("Ks", [E, S], F32, kind="Internal").ap()
    Vs = nc.dram_tensor("Vs", [H, S, D], F32, kind="Internal").ap()
    Qs = nc.dram_tensor("Qs", [E, OWN], F32, kind="Internal").ap()
    X1s = nc.dram_tensor("X1s", [E, OWN], F32, kind="Internal").ap()

    live = []

    def SBuf(scope, name, shape, dt=F32):
        cm = nc.sbuf_tensor(name, shape, dt)
        t = cm.__enter__()
        scope.append(cm)
        return t

    def close_scope(scope):
        for cm in reversed(scope):
            cm.__exit__(None, None, None)
        scope.clear()

    s_id = SBuf(live, "s_id", [128, 128])
    s_idb = SBuf(live, "s_idb", [128, 128], BF16)
    s_1c = SBuf(live, "s_1c", [128, 1])
    s_1r = SBuf(live, "s_1r", [1, 128])
    s_cv = SBuf(live, "s_cv", [128, 4])
    s_bqkv = SBuf(live, "s_bqkv", [128, 24])
    s_bo = SBuf(live, "s_bo", [128, EC])
    s_bg = SBuf(live, "s_bg", [EXP, 1])
    s_b1 = SBuf(live, "s_b1", [128, EXP * HC])
    s_b2 = SBuf(live, "s_b2", [EXP, E])
    s_wg = SBuf(live, "s_wg", [128, EC * EXP])
    s_sel = SBuf(live, "s_sel", [EXP, EXP * 128])
    nc.const_aps.register(F32, EPS, s_cv[:, 0:1])
    nc.const_aps.register(F32, 0.0, s_cv[:, 1:2])

    B = [Bank(nc, i) for i in range(8)]
    pe = Sem(nc, "pe")
    ev = Sem(nc, "ev")
    ex = Sem(nc, "ex")
    gp = Sem(nc, "gp")
    ld = DmaPool(nc, "ld", 8)
    sw = DmaPool(nc, "sw", 8)

    def dma(e, pool, dst, src, slow=False):
        ch = pool.chans[pool.i % len(pool.chans)]
        pool.i += 1
        if ch.n:
            e.wait_ge(ch.h, ch.n)
        if slow:
            ins = e.dma_start(dst, src, allow_slow_non_contiguous=True)
        else:
            ins = e.dma_start(dst, src)
        ins.then_inc(ch.h, 16)
        ch.n += 16
        return (ch, ch.n)

    def dma_drain(e, pool):
        for ch in pool.chans:
            if ch.n:
                e.wait_ge(ch.h, ch.n)

    def emit_transposes(block, jobs, banks, evac="vector", pre_toks=(),
                        evac_pre_toks=()):
        nb = len(jobs)
        nbatch = ceil_div(nb, 4)
        sem = ev if evac == "vector" else ex

        for bi in range(nbatch):
            bank = banks[bi % 2]

            def _(e, bi=bi, bank=bank):
                if bi == 0:
                    for tk in pre_toks:
                        W(e, tk)
                W(e, bank.free)
                n4 = min(4, nb - bi * 4)
                for s in range(n4):
                    src, _d, idap = jobs[bi * 4 + s]
                    op_, ow = src.shape[1], src.shape[0]
                    ins = e.transpose(bank.ps[0:op_, s * 128:s * 128 + ow], src, idap)
                bank.ready = pe.inc(ins)
            block.tensor(_)

            def _(e, bi=bi, bank=bank):
                if bi == 0:
                    for tk in evac_pre_toks:
                        W(e, tk)
                W(e, bank.ready)
                n4 = min(4, nb - bi * 4)
                for s in range(n4):
                    src, dst, _i = jobs[bi * 4 + s]
                    op_, ow = src.shape[1], src.shape[0]
                    if evac == "vector":
                        ins = e.tensor_copy(dst, bank.ps[0:op_, s * 128:s * 128 + ow])
                    else:
                        ins = e.copy(dst, bank.ps[0:op_, s * 128:s * 128 + ow])
                bank.free = sem.inc(ins)
            if evac == "vector":
                block.vector(_)
            else:
                block.scalar(_)
        return sem.tok()

    phA = []
    s_wkvT = SBuf(phA, "s_wkvT", [128, EC * 2 * E])
    s_wqT = SBuf(phA, "s_wqT", [128, EC * E])
    s_wtmp = SBuf(phA, "s_wtmp", [128, 2 * E])
    s_xtm = SBuf(phA, "s_xtm", [128, 2 * 4 * E])
    s_xfm = SBuf(phA, "s_xfm", [128, 2 * EC * 512])
    s_kvq = SBuf(phA, "s_kvq", [128, 4 * 512])
    s_vout = SBuf(phA, "s_vout", [128, 2 * 512])

    with nc.named_scope("phaseA"), nc.Block() as block:
        def _(e):
            dma(e, ld, s_id[:], ident)
            dma(e, ld, s_idb[:], identb)
            dma(e, ld, s_1c[:], onescol)
            dma(e, ld, s_1r[:], onesrow)
            dma(e, ld, s_cv[:], cvec)
            dma(e, ld, s_bqkv[:], ipb.rearrange("(c p) -> p c", p=128), slow=True)
            dma(e, ld, s_bo[:], opb.rearrange("(c p) -> p c", p=128), slow=True)
            dma(e, ld, s_bg[:], bgv.rearrange("(e o) -> e o", o=1), slow=True)
            dma(e, ld, s_b1[:].rearrange("p (ee c) -> p ee c", c=HC),
                b1.rearrange("ee (c p) -> p ee c", p=128), slow=True)
            dma(e, ld, s_b2[:], b2)
            dma(e, ld, s_sel[:], selm)
            dma(e, ld, s_wg[:].rearrange("p (c e) -> p c e", e=EXP),
                wgw.rearrange("(c p) e -> p c e", p=128), slow=True)
        block.sync(_)

        strip_tok = [None] * 24
        strip_free = [None, None]

        def _(e):
            for r in range(2):
                strip_tok[r] = dma(e, ld, s_wtmp[:, r * E:(r + 1) * E],
                                   ipw[r * 128:(r + 1) * 128, :])
        block.sync(_)

        for r in range(24):
            jobs = []
            for ifc in range(EC):
                src = s_wtmp[:, (r % 2) * E + ifc * 128:(r % 2) * E + ifc * 128 + 128]
                if r < 8:
                    dst = s_wqT[:, ifc * E + r * 128: ifc * E + r * 128 + 128]
                else:
                    dst = s_wkvT[:, ifc * 2 * E + (r - 8) * 128:
                                 ifc * 2 * E + (r - 8) * 128 + 128]
                jobs.append((src, dst, s_id[:]))
            emit_transposes(block, jobs, [B[0], B[1]], pre_toks=(strip_tok[r],))
            strip_free[r % 2] = pe.tok()

            if r + 2 < 24:
                def _(e, r=r):
                    W(e, strip_free[r % 2])
                    strip_tok[r + 2] = dma(
                        e, ld, s_wtmp[:, (r % 2) * E:(r % 2) * E + E],
                        ipw[(r + 2) * 128:(r + 3) * 128, :])
                block.sync(_)
        wT_tok = ev.tok()

        kvq_free = [None] * 4
        kvq_i = [0]
        vout_free = [None, None]
        vb_i = [0]
        xtm_free = [None, None]
        xfm_free = [None, None]

        def emit_x_block(src_rows, parity):
            tok_holder = []

            def _(e, parity=parity):
                W(e, xtm_free[parity])
                tok_holder.append(dma(
                    e, ld,
                    s_xtm[:, parity * 4 * E: parity * 4 * E + 4 * E].rearrange(
                        "p (t d) -> p t d", d=E),
                    src_rows.rearrange("(t p) d -> p t d", p=128)))
            block.sync(_)

            jobs = []
            for j in range(4 * EC):
                t, ifc = divmod(j, EC)
                src = s_xtm[:, parity * 4 * E + t * E + ifc * 128:
                            parity * 4 * E + t * E + ifc * 128 + 128]
                dst = s_xfm[:, parity * EC * 512 + ifc * 512 + t * 128:
                            parity * EC * 512 + ifc * 512 + t * 128 + 128]
                jobs.append((src, dst, s_id[:]))
            emit_transposes(block, jobs, [B[2], B[3]],
                            pre_toks=(tok_holder[0], xfm_free[parity]))
            xtm_free[parity] = pe.tok()

        def emit_proj(parity, ofm_list, blk):
            for ofm in ofm_list:
                gb = B[4 + (ofm % 2)]
                is_q = ofm < 8

                def _(e, ofm=ofm, gb=gb, is_q=is_q, parity=parity):
                    W(e, gb.free)
                    W(e, wT_tok)
                    W(e, ev.tok())
                    for ifc in range(EC):
                        if is_q:
                            lhs = s_wqT[:, ifc * E + ofm * 128: ifc * E + ofm * 128 + 128]
                        else:
                            lhs = s_wkvT[:, ifc * 2 * E + (ofm - 8) * 128:
                                         ifc * 2 * E + (ofm - 8) * 128 + 128]
                        rhs = s_xfm[:, parity * EC * 512 + ifc * 512:
                                    parity * EC * 512 + ifc * 512 + 512]
                        ins = e.matmul(gb.ps[:], lhs, rhs,
                                       start=(ifc == 0), stop=(ifc == EC - 1))
                    gb.ready = pe.inc(ins)
                block.tensor(_)

                slot = kvq_i[0] % 4
                kvq_i[0] += 1

                def _(e, ofm=ofm, gb=gb, slot=slot):
                    W(e, gb.ready)
                    W(e, kvq_free[slot])
                    ins = e.activation(s_kvq[:, slot * 512:slot * 512 + 512],
                                       gb.ps[:], AF.Identity,
                                       bias=s_bqkv[:, ofm:ofm + 1], scale=1.0)
                    gb.free = ex.inc(ins)
                block.scalar(_)
                evac_tok = ex.tok()

                if is_q or ofm < 16:
                    def _(e, ofm=ofm, slot=slot, evac_tok=evac_tok, blk=blk,
                          is_q=is_q):
                        W(e, evac_tok)
                        if is_q:
                            dst = Qs[ofm * 128:(ofm + 1) * 128,
                                     blk * 512:(blk + 1) * 512]
                        else:
                            dst = Ks[(ofm - 8) * 128:(ofm - 7) * 128,
                                     blk * 512:(blk + 1) * 512]
                        kvq_free[slot] = dma(e, sw, dst,
                                             s_kvq[:, slot * 512:slot * 512 + 512])
                    block.gpsimd(_)
                else:
                    vb = vb_i[0] % 2
                    vb_i[0] += 1
                    vjobs = []
                    for t in range(4):
                        src = s_kvq[:, slot * 512 + t * 128: slot * 512 + t * 128 + 128]
                        dst = s_vout[:, vb * 512 + t * 128: vb * 512 + t * 128 + 128]
                        vjobs.append((src, dst, s_id[:]))
                    vtok = emit_transposes(block, vjobs, [B[6], B[7]],
                                           pre_toks=(evac_tok,),
                                           evac_pre_toks=vout_free[vb] or ())
                    kvq_free[slot] = pe.tok()

                    def _(e, ofm=ofm, blk=blk, vb=vb, vtok=vtok):
                        W(e, vtok)
                        tks = []
                        for hh in range(2):
                            hidx = (ofm - 16) * 2 + hh
                            src = s_vout[:, vb * 512:(vb + 1) * 512].rearrange(
                                "p (t dd) -> p t dd", dd=128)[:, :, hh * 64:(hh + 1) * 64]
                            dst = Vs[hidx, blk * 512:(blk + 1) * 512, :].rearrange(
                                "(t p) d -> p t d", p=128)
                            tks.append(dma(e, sw, dst, src))
                        vout_free[vb] = tuple(tks)
                    block.gpsimd(_)

        for tb in range(8):
            emit_x_block(x_full[tb * 512:(tb + 1) * 512, :], tb % 2)
            emit_proj(tb % 2, list(range(8, 24)), tb)
            xfm_free[tb % 2] = ex.tok()
        for tq in range(4):
            emit_x_block(x_own[tq * 512:(tq + 1) * 512, :], tq % 2)
            emit_proj(tq % 2, list(range(0, 8)), tq)
            xfm_free[tq % 2] = ex.tok()

        def _(e):
            dma_drain(e, sw)
        block.gpsimd(_)

    close_scope(phA)

    phBB2 = []
    s_ctx = SBuf(phBB2, "s_ctx", [128, EC * OWN])

    phB = []
    s_K = SBuf(phB, "s_K", [128, S])
    s_Kh = SBuf(phB, "s_Kh", [128, S], BF16)
    s_Kl = SBuf(phB, "s_Kl", [128, S], BF16)
    s_V = SBuf(phB, "s_V", [128, 2 * SC * 65])
    s_Vh = SBuf(phB, "s_Vh", [128, 2 * SC * 65], BF16)
    s_Vl = SBuf(phB, "s_Vl", [128, 2 * SC * 65], BF16)
    s_Q = SBuf(phB, "s_Q", [128, OWN])
    s_Qh = SBuf(phB, "s_Qh", [128, OWN], BF16)
    s_Ql = SBuf(phB, "s_Ql", [128, OWN], BF16)
    s_maskT = SBuf(phB, "s_maskT", [128, (TAIL // 128) * QS], BF16)
    s_ptf = SBuf(phB, "s_ptf", [128, 8 * 512])
    s_pth = SBuf(phB, "s_pth", [128, 8 * 512], BF16)
    s_ptl = SBuf(phB, "s_ptl", [128, 8 * 512], BF16)
    s_rt = SBuf(phB, "s_rt", [1, 512])
    s_rb = SBuf(phB, "s_rb", [64, 512])

    with nc.named_scope("phaseB"), nc.Block() as block:
        fenceA = (pe.tok(), ev.tok(), ex.tok())
        mask_tok = [None]

        def _(e):
            t2 = dma(e, ld, s_maskT[:].rearrange("p (c q) -> p c q", q=QS),
                     maskT_in.rearrange("(c p) q -> p c q", p=128))
            mask_tok[0] = t2
        block.sync(_)

        split_toks = [None, None]
        head_end = [None, None]
        kvq_toks = [None, None]
        pt_free = [None] * 8
        ptf_free = [None] * 8
        rb_free = [None]
        ctx_banks = [B[4], B[5]]
        bc_bank = B[6]

        def emit_kvq_load(h):
            hb = h % 2

            def _(e, h=h, hb=hb):
                W(e, head_end[hb])
                if h < 2:
                    W(e, fenceA)
                t1 = dma(e, ld, s_K[hb * 64:(hb + 1) * 64, :], Ks[h * D:(h + 1) * D, :])
                vdst = s_V[:].rearrange("p (b c d) -> p b c d", b=2, d=65)[:, hb, :, 0:64]
                t2 = dma(e, ld, vdst, Vs[h].rearrange("(c p) d -> p c d", p=128))
                t3 = dma(e, ld, s_Q[hb * 64:(hb + 1) * 64, :],
                         Qs[h * D:(h + 1) * D, :])
                kvq_toks[hb] = (t1, t2, t3)
            block.sync(_)

            def _(e, hb=hb):
                W(e, kvq_toks[hb])
                vv = s_V[:].rearrange("p (b c d) -> p b c d", b=2, d=65)
                ins = e.tensor_scalar(vv[:, hb, :, 64:65], vv[:, hb, :, 63:64],
                                      0.0, 1.0, ALU.mult, ALU.add)
                ev.inc(ins)
                e.wait_ge(ev.h, ev.n)
                ks = s_K[hb * 64:(hb + 1) * 64, :]
                qs_ = s_Q[hb * 64:(hb + 1) * 64, :]
                vs_ = s_V[:, hb * SC * 65:(hb + 1) * SC * 65]
                ins = e.tensor_copy(s_Kh[hb * 64:(hb + 1) * 64, :], ks)
                ev.inc(ins)
                ins = e.tensor_copy(s_Qh[hb * 64:(hb + 1) * 64, :], qs_)
                ev.inc(ins)
                ins = e.tensor_copy(s_Vh[:, hb * SC * 65:(hb + 1) * SC * 65], vs_)
                ev.inc(ins)
                e.wait_ge(ev.h, ev.n)
                ins = e.tensor_tensor(s_Kl[hb * 64:(hb + 1) * 64, :], ks,
                                      s_Kh[hb * 64:(hb + 1) * 64, :], ALU.subtract)
                ev.inc(ins)
                ins = e.tensor_tensor(s_Ql[hb * 64:(hb + 1) * 64, :], qs_,
                                      s_Qh[hb * 64:(hb + 1) * 64, :], ALU.subtract)
                ev.inc(ins)
                ins = e.tensor_tensor(s_Vl[:, hb * SC * 65:(hb + 1) * SC * 65], vs_,
                                      s_Vh[:, hb * SC * 65:(hb + 1) * SC * 65],
                                      ALU.subtract)
                split_toks[hb] = ev.inc(ins)
            block.vector(_)

        def emit_slot(h, k):
            ek = (k + 1) * TAIL
            nkc128 = ek // 128
            hb = h % 2
            cbank = ctx_banks[k % 2]
            npairs = nkc128 // 2
            pv_jobs = []
            for p in range(npairs):
                kcs = [2 * p, 2 * p + 1]
                banks = [B[kc % 4] for kc in kcs]

                def _(e, kcs=kcs, banks=banks, hb=hb, k=k, h=h, nkc128=nkc128):
                    for b_ in banks:
                        W(e, b_.free)
                    if kcs[0] == 0 and k == 0:
                        W(e, split_toks[hb])
                        if h == 0:
                            W(e, mask_tok[0])
                    for kc, b_ in zip(kcs, banks):
                        qh = s_Qh[hb * 64:(hb + 1) * 64, k * QS: k * QS + 512]
                        ql = s_Ql[hb * 64:(hb + 1) * 64, k * QS: k * QS + 512]
                        kh = s_Kh[hb * 64:(hb + 1) * 64, kc * 128: kc * 128 + 128]
                        kl = s_Kl[hb * 64:(hb + 1) * 64, kc * 128: kc * 128 + 128]
                        tailc = kc >= nkc128 - 8
                        e.matmul(b_.ps[:], kh, qh, start=True, stop=False)
                        e.matmul(b_.ps[:], kh, ql, start=False, stop=False)
                        ins = e.matmul(b_.ps[:], kl, qh, start=False,
                                       stop=not tailc)
                        if tailc:
                            tc = kc - (nkc128 - 8)
                            ins = e.matmul(b_.ps[:], s_idb[:],
                                           s_maskT[:, tc * QS: tc * QS + 512],
                                           start=False, stop=True)
                        b_.ready = pe.inc(ins)
                block.tensor(_)

                for kc, b_ in zip(kcs, banks):
                    ps_ = kc % 8

                    def _(e, kc=kc, b_=b_, ps_=ps_):
                        W(e, b_.ready)
                        W(e, ptf_free[ps_])
                        ins = e.activation(s_ptf[:, ps_ * 512: ps_ * 512 + 512],
                                           b_.ps[:], AF.Exp, bias=0.0, scale=ISQ)
                        b_.free = ex.inc(ins)
                    block.scalar(_)
                    exp_tok = ex.tok()

                    def _(e, ps_=ps_, exp_tok=exp_tok):
                        W(e, exp_tok)
                        W(e, pt_free[ps_])
                        pf = s_ptf[:, ps_ * 512: ps_ * 512 + 512]
                        ph = s_pth[:, ps_ * 512: ps_ * 512 + 512]
                        ins = e.tensor_copy(ph, pf)
                        ev.inc(ins)
                        e.wait_ge(ev.h, ev.n)
                        ins = e.tensor_tensor(s_ptl[:, ps_ * 512: ps_ * 512 + 512],
                                              pf, ph, ALU.subtract)
                        ev.inc(ins)
                    block.vector(_)
                    ptf_free[ps_] = ev.tok()

                split_pt = ev.tok()

                def pv_(e, kcs=kcs, hb=hb, split_pt=split_pt, cbank=cbank,
                        nkc128=nkc128, h=h, k=k):
                    W(e, split_pt)
                    if kcs[0] == 0:
                        W(e, cbank.free)
                    for kc in kcs:
                        ps_ = kc % 8
                        vh = s_Vh[:, hb * SC * 65 + kc * 65: hb * SC * 65 + kc * 65 + 65]
                        vl = s_Vl[:, hb * SC * 65 + kc * 65: hb * SC * 65 + kc * 65 + 65]
                        ph = s_pth[:, ps_ * 512: ps_ * 512 + 512]
                        pl = s_ptl[:, ps_ * 512: ps_ * 512 + 512]
                        last = kc == nkc128 - 1
                        e.matmul(cbank.ps[0:65, 0:512], vh, ph,
                                 start=(kc == 0), stop=False)
                        e.matmul(cbank.ps[0:65, 0:512], vh, pl,
                                 start=False, stop=False)
                        ins = e.matmul(cbank.ps[0:65, 0:512], vl, ph,
                                       start=False, stop=last)
                        t = pe.inc(ins)
                        pt_free[ps_] = pe.tok()
                        if last:
                            cbank.ready = t
                pv_jobs.append(pv_)
                if p >= 2:
                    block.tensor(pv_jobs[p - 2])
            block.tensor(pv_jobs[npairs - 2])
            block.tensor(pv_jobs[npairs - 1])

            def _(e, cbank=cbank):
                W(e, cbank.ready)
                ins = e.tensor_copy(s_rt[0:1, :], cbank.ps[64:65, 0:512])
                ev.inc(ins)
                e.wait_ge(ev.h, ev.n)
                ins = e.reciprocal(s_rt[0:1, :], s_rt[0:1, :])
                ev.inc(ins)
            block.vector(_)
            rt_tok = ev.tok()

            def _(e, rt_tok=rt_tok):
                W(e, rt_tok)
                W(e, bc_bank.free)
                ins = e.matmul(bc_bank.ps[0:64, 0:512], s_1r[0:1, 0:64],
                               s_rt[0:1, :], start=True, stop=True)
                bc_bank.ready = pe.inc(ins)
            block.tensor(_)

            def _(e):
                W(e, bc_bank.ready)
                W(e, rb_free[0])
                ins = e.tensor_copy(s_rb[:], bc_bank.ps[0:64, 0:512])
                bc_bank.free = ev.inc(ins)
            block.vector(_)
            rb_tok = ev.tok()

            def _(e, h=h, k=k, cbank=cbank, rb_tok=rb_tok):
                W(e, cbank.ready)
                W(e, rb_tok)
                ins = e.tensor_tensor(
                    s_ctx[(h % 2) * 64:(h % 2) * 64 + 64,
                          (h // 2) * OWN + k * QS:(h // 2) * OWN + k * QS + QS],
                    cbank.ps[0:64, 0:QS], s_rb[:], ALU.mult)
                cbank.free = ev.inc(ins)
            block.vector(_)
            rb_free[0] = ev.tok()

        for h in range(H):
            emit_kvq_load(h)
            for k in range(NSLOT):
                emit_slot(h, k)
            head_end[h % 2] = pe.tok()

    close_scope(phB)

    def emit_ln(block, src, tmp, dst, s_sq, s_row, pre_tok):
        bsum, bsq, bbc1, bbc2 = B[0], B[1], B[2], B[3]
        sq_toks = []
        sqmm_toks = []

        for ofc in range(EC):
            def _(e, ofc=ofc):
                if ofc == 0:
                    W(e, pre_tok)
                if ofc >= 2:
                    W(e, sqmm_toks[ofc - 2])
                ins = e.activation(s_sq[:, (ofc % 2) * 512:(ofc % 2) * 512 + 512],
                                   src(ofc), AF.Square, bias=0.0, scale=1.0)
                sq_toks.append(ex.inc(ins))
            block.scalar(_)

            def _(e, ofc=ofc):
                if ofc == 0:
                    W(e, pre_tok)
                    W(e, bsum.free)
                    W(e, bsq.free)
                ins = e.matmul(bsum.ps[0:1, 0:512], s_1c[:], src(ofc),
                               start=(ofc == 0), stop=(ofc == EC - 1))
                if ofc == EC - 1:
                    bsum.ready = pe.inc(ins)
                W(e, sq_toks[ofc])
                ins = e.matmul(bsq.ps[0:1, 0:512], s_1c[:],
                               s_sq[:, (ofc % 2) * 512:(ofc % 2) * 512 + 512],
                               start=(ofc == 0), stop=(ofc == EC - 1))
                sqmm_toks.append(pe.inc(ins))
            block.tensor(_)
        bsq.ready = sqmm_toks[-1]

        r = lambda i: s_row[0:1, i * 512:(i + 1) * 512]

        def _(e):
            W(e, bsum.ready)
            ins = e.tensor_copy(r(0), bsum.ps[0:1, 0:512])
            bsum.free = ev.inc(ins)
            W(e, bsq.ready)
            ins = e.tensor_copy(r(1), bsq.ps[0:1, 0:512])
            bsq.free = ev.inc(ins)
            e.wait_ge(ev.h, ev.n)
            ins = e.tensor_scalar(r(2), r(0), 1.0 / E, None, ALU.mult)
            ev.inc(ins)
            ins = e.tensor_scalar(r(3), r(1), 1.0 / E, None, ALU.mult)
            ev.inc(ins)
            e.wait_ge(ev.h, ev.n)
            ins = e.tensor_tensor(r(4), r(2), r(2), ALU.mult)
            ev.inc(ins)
            e.wait_ge(ev.h, ev.n)
            ins = e.tensor_tensor(r(4), r(3), r(4), ALU.subtract)
            ev.inc(ins)
        block.vector(_)
        var_tok = ev.tok()

        def _(e, var_tok=var_tok):
            W(e, var_tok)
            ins = e.activation(r(3), r(4), AF.Sqrt, bias=EPS, scale=1.0)
            ex.inc(ins)
        block.scalar(_)
        sd_tok = ex.tok()

        def _(e, sd_tok=sd_tok):
            W(e, sd_tok)
            ins = e.reciprocal(r(5), r(3))
            ev.inc(ins)
            e.wait_ge(ev.h, ev.n)
            ins = e.tensor_tensor(r(4), r(2), r(5), ALU.mult)
            ev.inc(ins)
        block.vector(_)
        nmr_tok = ev.tok()

        def _(e, nmr_tok=nmr_tok):
            W(e, nmr_tok)
            W(e, bbc1.free)
            ins = e.matmul(bbc1.ps[:, 0:512], s_1r[:], r(5), start=True, stop=True)
            bbc1.ready = pe.inc(ins)
            W(e, bbc2.free)
            ins = e.matmul(bbc2.ps[:, 0:512], s_1r[:], r(4), start=True, stop=True)
            bbc2.ready = pe.inc(ins)
        block.tensor(_)

        def _(e):
            W(e, bbc1.ready)
            W(e, bbc2.ready)
            for ofc in range(EC):
                ins = e.tensor_tensor(tmp(ofc), src(ofc), bbc1.ps[:, 0:512], ALU.mult)
                if ofc == EC - 1:
                    bbc1.free = ev.inc(ins)
            e.wait_ge(ev.h, ev.n)
            for ofc in range(EC):
                ins = e.tensor_tensor(dst(ofc), tmp(ofc), bbc2.ps[:, 0:512], ALU.subtract)
                if ofc == EC - 1:
                    bbc2.free = ev.inc(ins)
        block.vector(_)
        return ev.tok()

    phB2 = []
    s_woT = SBuf(phB2, "s_woT", [128, EC * E])
    s_xo = SBuf(phB2, "s_xo", [128, EC * 512])
    s_xtm2 = SBuf(phB2, "s_xtm2", [128, 4 * E])
    s_t1 = SBuf(phB2, "s_t1", [128, EC * 512])
    s_sqA = SBuf(phB2, "s_sqA", [128, 2 * 512])
    s_rowA = SBuf(phB2, "s_rowA", [1, 6 * 512])

    with nc.named_scope("phaseB2"), nc.Block() as block:
        xtm2_free = [None]
        for rr in range(8):
            tok_holder = []

            def _(e, rr=rr):
                W(e, xtm2_free[0])
                tok_holder.append(dma(e, ld, s_xtm2[:, 0:E],
                                      opw[rr * 128:(rr + 1) * 128, :]))
            block.sync(_)
            jobs = []
            for cfc in range(EC):
                src = s_xtm2[:, cfc * 128:cfc * 128 + 128]
                dst = s_woT[:, cfc * E + rr * 128: cfc * E + rr * 128 + 128]
                jobs.append((src, dst, s_id[:]))
            emit_transposes(block, jobs, [B[0], B[1]], pre_toks=(tok_holder[0],))
            xtm2_free[0] = pe.tok()
        woT_tok = ev.tok()

        for nt in range(4):
            tok_holder = []
            xo_prev = ev.tok()

            def _(e, nt=nt):
                W(e, xtm2_free[0])
                tok_holder.append(dma(
                    e, ld,
                    s_xtm2[:].rearrange("p (t d) -> p t d", d=E),
                    x_own[nt * 512:(nt + 1) * 512, :].rearrange("(t p) d -> p t d", p=128)))
            block.sync(_)
            jobs = []
            for j in range(4 * EC):
                t, ifc = divmod(j, EC)
                srcj = s_xtm2[:, t * E + ifc * 128: t * E + ifc * 128 + 128]
                dstj = s_xo[:, ifc * 512 + t * 128: ifc * 512 + t * 128 + 128]
                jobs.append((srcj, dstj, s_id[:]))
            emit_transposes(block, jobs, [B[2], B[3]],
                            pre_toks=(tok_holder[0],), evac_pre_toks=(xo_prev,))
            xtm2_free[0] = pe.tok()
            xo_tok = ev.tok()

            evac_toks = []
            for ofm in range(EC):
                gb = B[4 + ofm % 2]

                def _(e, ofm=ofm, gb=gb, nt=nt):
                    W(e, gb.free)
                    if nt == 0 and ofm == 0:
                        W(e, woT_tok)
                    for cfc in range(EC):
                        lhs = s_woT[:, cfc * E + ofm * 128: cfc * E + ofm * 128 + 128]
                        rhs = s_ctx[:, cfc * OWN + nt * 512: cfc * OWN + nt * 512 + 512]
                        ins = e.matmul(gb.ps[:], lhs, rhs,
                                       start=(cfc == 0), stop=(cfc == EC - 1))
                    gb.ready = pe.inc(ins)
                block.tensor(_)

                def _(e, ofm=ofm, gb=gb):
                    W(e, gb.ready)
                    ins = e.activation(s_t1[:, ofm * 512: ofm * 512 + 512],
                                       gb.ps[:], AF.Identity,
                                       bias=s_bo[:, ofm:ofm + 1], scale=1.0)
                    gb.free = ex.inc(ins)
                block.scalar(_)
                evac_toks.append(ex.tok())

            def _(e, nt=nt, et=evac_toks[-1]):
                W(e, et)
                W(e, xo_tok)
                for ofc in range(EC):
                    ins = e.tensor_tensor(
                        s_ctx[:, ofc * OWN + nt * 512: ofc * OWN + nt * 512 + 512],
                        s_xo[:, ofc * 512: ofc * 512 + 512],
                        s_t1[:, ofc * 512: ofc * 512 + 512], ALU.add)
                ev.inc(ins)
            block.vector(_)
            add_tok = ev.tok()

            src = lambda ofc, nt=nt: s_ctx[:, ofc * OWN + nt * 512: ofc * OWN + nt * 512 + 512]
            tmp = lambda ofc, nt=nt: s_xo[:, ofc * 512: ofc * 512 + 512]
            ln_tok = emit_ln(block, src, tmp, src, s_sqA, s_rowA, add_tok)

            def _(e, nt=nt, ln_tok=ln_tok):
                W(e, ln_tok)
                sctx = s_ctx[:].rearrange("p (c t) -> p c t", t=OWN)[:, :, nt * 512:(nt + 1) * 512]
                dstx = X1s.rearrange("(c p) t -> p c t", p=128)[:, :, nt * 512:(nt + 1) * 512]
                dma(e, sw, dstx, sctx)
            block.gpsimd(_)

        def _(e):
            dma_drain(e, sw)
        block.gpsimd(_)

    close_scope(phB2)
    close_scope(phBB2)

    phC = []
    s_x1a = SBuf(phC, "s_x1a", [128, EC * 1024], BF16)
    s_x1b = SBuf(phC, "s_x1b", [128, EC * 1024], BF16)
    s_x2 = SBuf(phC, "s_x2", [128, EC * 1024])
    s_hf = SBuf(phC, "s_hf", [128, 4 * 512])
    s_ha = SBuf(phC, "s_ha", [128, 2 * 4 * 512], BF16)
    s_hb = SBuf(phC, "s_hb", [128, 2 * 4 * 512], BF16)
    s_wst = SBuf(phC, "s_wst", [128, 4096])
    s_w1a = SBuf(phC, "s_w1a", [128, 2 * EC * 512], BF16)
    s_w1b = SBuf(phC, "s_w1b", [128, 2 * EC * 512], BF16)
    s_w2a = SBuf(phC, "s_w2a", [128, 2 * 4 * 1024], BF16)
    s_w2b = SBuf(phC, "s_w2b", [128, 2 * 4 * 1024], BF16)
    s_wga = SBuf(phC, "s_wga", [128, EC * EXP], BF16)
    s_wgb = SBuf(phC, "s_wgb", [128, EC * EXP], BF16)

    s_gt = SBuf(phC, "s_gt", [128, 8 * EXP])
    s_ge = SBuf(phC, "s_ge", [128, 8 * EXP])
    s_gst = SBuf(phC, "s_gst", [128, 32])
    s_mf = SBuf(phC, "s_mf", [EXP, 1024])
    s_gl = s_mf
    s_mb = SBuf(phC, "s_mb", [128, 2 * 512])
    s_tf = SBuf(phC, "s_tf", [128, 2 * 512])
    s_sqC = s_hf
    s_rowC = SBuf(phC, "s_rowC", [1, 6 * 512])
    s_ot = SBuf(phC, "s_ot", [128, 1024])

    with nc.named_scope("phaseC"), nc.Block() as block:
        x1h_free = [None]
        x1ch_free = [None]
        x1_tok = [None]
        half_fence = [None]
        tf_free = [None] * 2
        out_free = [None]
        slabbuf_free = [None, None]
        h_free = [None, None]
        hf_free = [None]
        mb_free = [None]

        def _(e):
            ins = e.tensor_copy(s_wga[:], s_wg[:])
            ev.inc(ins)
            e.wait_ge(ev.h, ev.n)
            ins = e.tensor_tensor(s_wgb[:], s_wg[:], s_wga[:], ALU.subtract)
            ev.inc(ins)
        block.vector(_)
        wg_tok = ev.tok()

        for half in range(2):

            x1sp_tok = None
            for chk in range(2):
                def _(e, half=half, chk=chk):
                    if chk == 0 and x1h_free[0] is not None:
                        for tk in x1h_free[0]:
                            W(e, tk)
                    if chk == 1:
                        W(e, x1ch_free[0])
                    src = X1s.rearrange("(c p) t -> p c t", p=128)[
                        :, chk * 4:(chk + 1) * 4, half * 1024:(half + 1) * 1024]
                    dst = s_wst[:].rearrange("p (c t) -> p c t", t=1024)
                    x1_tok[0] = dma(e, ld, dst, src)
                block.sync(_)

                def _(e, half=half, chk=chk):
                    W(e, x1_tok[0])
                    if chk == 0 and half_fence[0] is not None:
                        for tk in half_fence[0]:
                            W(e, tk)
                    ins = e.tensor_copy(s_x1a[:, chk * 4096:(chk + 1) * 4096],
                                        s_wst[:])
                    ev.inc(ins)
                    e.wait_ge(ev.h, ev.n)
                    ins = e.tensor_tensor(s_x1b[:, chk * 4096:(chk + 1) * 4096],
                                          s_wst[:],
                                          s_x1a[:, chk * 4096:(chk + 1) * 4096],
                                          ALU.subtract)
                    ev.inc(ins)
                block.vector(_)
                x1ch_free[0] = ev.tok()
            x1sp_tok = ev.tok()
            stage_free_init = ev.tok()

            gl_toks = []
            for nt2 in range(2):
                gb = B[0]

                def _(e, nt2=nt2, gb=gb):
                    W(e, gb.free)
                    if nt2 == 0:
                        W(e, x1sp_tok)
                        W(e, wg_tok)
                    first = True
                    for dc in range(EC):
                        for (wg_, x_) in ((s_wga, s_x1a), (s_wgb, s_x1a), (s_wga, s_x1b)):
                            ins = e.matmul(gb.ps[0:EXP, 0:512],
                                           wg_[:, dc * EXP:(dc + 1) * EXP],
                                           x_[:, dc * 1024 + nt2 * 512: dc * 1024 + nt2 * 512 + 512],
                                           start=first, stop=(dc == EC - 1 and x_ is s_x1b))
                            first = False
                    gb.ready = pe.inc(ins)
                block.tensor(_)

                def _(e, nt2=nt2, gb=gb):
                    W(e, gb.ready)
                    ins = e.activation(s_gl[:, nt2 * 512: nt2 * 512 + 512],
                                       gb.ps[0:EXP, 0:512], AF.Identity,
                                       bias=s_bg[:], scale=1.0)
                    gb.free = ex.inc(ins)
                block.scalar(_)
                gl_toks.append(ex.tok())

            jobs = []
            for ti in range(8):
                jobs.append((s_gl[0:EXP, ti * 128:(ti + 1) * 128],
                             s_gt[:, ti * EXP:(ti + 1) * EXP], s_id[0:EXP, 0:EXP]))
            gt_tok = emit_transposes(block, jobs, [B[2], B[3]],
                                     pre_toks=(gl_toks[-1],))

            def _(e, gt_tok=gt_tok):
                W(e, gt_tok)
                for ti in range(8):
                    ins = e.reduce_max(s_gst[:, ti * 4: ti * 4 + 1],
                                       s_gt[:, ti * EXP:(ti + 1) * EXP], axis=AXX)
                    ev.inc(ins)
                e.wait_ge(ev.h, ev.n)
                for ti in range(8):
                    ins = e.tensor_scalar(s_gst[:, ti * 4 + 1: ti * 4 + 2],
                                          s_gst[:, ti * 4: ti * 4 + 1],
                                          -1.0, None, ALU.mult)
                    ev.inc(ins)
            block.vector(_)
            gnm_tok = ev.tok()

            def _(e, gnm_tok=gnm_tok):
                W(e, gnm_tok)
                for ti in range(8):
                    ins = e.activation(s_ge[:, ti * EXP: ti * EXP + EXP],
                                       s_gt[:, ti * EXP:(ti + 1) * EXP],
                                       AF.Exp, bias=s_gst[:, ti * 4 + 1: ti * 4 + 2],
                                       scale=1.0,
                                       accum_out=s_gst[:, ti * 4 + 2: ti * 4 + 3])
                    ex.inc(ins)
            block.scalar(_)
            gs_tok = ex.tok()

            def _(e, gs_tok=gs_tok):
                W(e, gs_tok)
                for ti in range(8):
                    ins = e.reciprocal(s_gst[:, ti * 4 + 3: ti * 4 + 4],
                                       s_gst[:, ti * 4 + 2: ti * 4 + 3])
                    ev.inc(ins)
                e.wait_ge(ev.h, ev.n)
                for ti in range(8):
                    ins = e.tensor_scalar(s_gt[:, ti * EXP:(ti + 1) * EXP],
                                          s_gt[:, ti * EXP:(ti + 1) * EXP],
                                          s_gst[:, ti * 4: ti * 4 + 1],
                                          s_gst[:, ti * 4 + 3: ti * 4 + 4],
                                          ALU.is_equal, ALU.mult)
                    ev.inc(ins)
            block.vector(_)
            mt_tok = ev.tok()

            jobs = []
            for ti in range(8):
                jobs.append((s_gt[:, ti * EXP:(ti + 1) * EXP],
                             s_mf[0:EXP, ti * 128:(ti + 1) * 128], s_id[:]))
            mf_tok = emit_transposes(block, jobs, [B[2], B[3]], pre_toks=(mt_tok,))

            gp_prev = gp.tok()
            for nt2 in range(2):
                for om in range(EC):
                    gb = B[4 + om % 2]

                    def _(e, nt2=nt2, om=om, gb=gb, mf_tok=mf_tok):
                        W(e, gb.free)
                        if nt2 == 0 and om == 0:
                            W(e, mf_tok)
                        ins = e.matmul(gb.ps[:], s_b2[0:EXP, om * 128:(om + 1) * 128],
                                       s_mf[0:EXP, nt2 * 512: nt2 * 512 + 512],
                                       start=True, stop=True)
                        gb.ready = pe.inc(ins)
                    block.tensor(_)

                    def _(e, nt2=nt2, om=om, gb=gb, gp_prev=gp_prev):
                        W(e, gb.ready)
                        W(e, gp_prev)
                        ins = e.tensor_copy(
                            s_x2[:, om * 1024 + nt2 * 512: om * 1024 + nt2 * 512 + 512],
                            gb.ps[:])
                        gb.free = ev.inc(ins)
                    block.vector(_)

            slabs = [(eix, hg) for eix in range(EXP) for hg in range(8)]
            stage_tok = [None]
            stage_free = [stage_free_init]
            split_tok = [None, None]
            mb_toks = {}

            def emit_slab_load_split(si):
                buf = si % 2
                eix, hg = slabs[si]
                for part in range(2):
                    def _(e, si=si, part=part, eix=eix, hg=hg):
                        if stage_free[0] is not None:
                            W(e, stage_free[0])
                        if part == 0:
                            sl = w1[eix][:, hg * 512:(hg + 1) * 512].rearrange(
                                "(c p) h -> p c h", p=128)
                            stage_tok[0] = dma(
                                e, ld, s_wst[:].rearrange("p (c h) -> p c h", h=512), sl)
                        else:
                            sl = w2[eix][hg * 512:(hg + 1) * 512, :].rearrange(
                                "(c p) o -> p c o", p=128)
                            stage_tok[0] = dma(
                                e, ld, s_wst[:].rearrange("p (c o) -> p c o", o=E), sl)
                    block.sync(_)

                    def _(e, buf=buf, part=part):
                        W(e, stage_tok[0])
                        if part == 0 and slabbuf_free[buf] is not None:
                            for tk in (slabbuf_free[buf] if isinstance(slabbuf_free[buf], list)
                                       else [slabbuf_free[buf]]):
                                W(e, tk)
                        wa = s_w1a if part == 0 else s_w2a
                        ins = e.tensor_copy(wa[:, buf * 4096:(buf + 1) * 4096], s_wst[:])
                        gp.inc(ins)
                    block.gpsimd(_)
                    hi_tok = gp.tok()

                    def _(e, buf=buf, part=part, hi_tok=hi_tok):
                        W(e, hi_tok)
                        wa = s_w1a if part == 0 else s_w2a
                        wb = s_w1b if part == 0 else s_w2b
                        ins = e.tensor_tensor(wb[:, buf * 4096:(buf + 1) * 4096],
                                              s_wst[:],
                                              wa[:, buf * 4096:(buf + 1) * 4096],
                                              ALU.subtract)
                        t = ev.inc(ins)
                        if part == 1:
                            split_tok[buf] = t
                        stage_free[0] = ev.tok()
                    block.vector(_)

            def emit_mb(eix):
                toks = []
                for nt2 in range(2):
                    gb = B[1]

                    def _(e, eix=eix, nt2=nt2, gb=gb, mf_tok=mf_tok):
                        W(e, gb.free)
                        if eix == 0 and nt2 == 0:
                            W(e, mf_tok)
                        ins = e.matmul(gb.ps[:],
                                       s_sel[0:EXP, eix * 128:(eix + 1) * 128],
                                       s_mf[0:EXP, nt2 * 512: nt2 * 512 + 512],
                                       start=True, stop=True)
                        gb.ready = pe.inc(ins)
                    block.tensor(_)

                    def _(e, nt2=nt2, gb=gb):
                        W(e, gb.ready)
                        W(e, mb_free[0])
                        ins = e.tensor_copy(s_mb[:, nt2 * 512: nt2 * 512 + 512], gb.ps[:])
                        gb.free = ev.inc(ins)
                    block.vector(_)
                    toks.append(ev.tok())
                return toks

            emit_slab_load_split(0)
            for si, (eix, hg) in enumerate(slabs):
                buf = si % 2
                if hg == 0:
                    mb_toks[eix] = emit_mb(eix)
                if si + 1 < len(slabs):
                    emit_slab_load_split(si + 1)
                my_split = split_tok[buf]
                for nt2 in range(2):
                    hbuf = (si * 2 + nt2) % 2
                    h_toks = []
                    for hm in range(4):
                        gb = B[4 + hm % 2]

                        def _(e, nt2=nt2, hm=hm, gb=gb, buf=buf,
                              st_=my_split):
                            W(e, gb.free)
                            if hm == 0 and nt2 == 0:
                                W(e, st_)
                            first = True
                            for dc in range(EC):
                                off = buf * 4096 + dc * 512 + hm * 128
                                for (w_, x_) in ((s_w1a, s_x1a), (s_w1b, s_x1a),
                                                 (s_w1a, s_x1b)):
                                    ins = e.matmul(
                                        gb.ps[:], w_[:, off: off + 128],
                                        x_[:, dc * 1024 + nt2 * 512: dc * 1024 + nt2 * 512 + 512],
                                        start=first,
                                        stop=(dc == EC - 1 and x_ is s_x1b))
                                    first = False
                            gb.ready = pe.inc(ins)
                        block.tensor(_)

                        def _(e, eix=eix, hg=hg, hm=hm, gb=gb, hbuf=hbuf):
                            W(e, gb.ready)
                            if hm == 0:
                                W(e, h_free[hbuf])
                                W(e, hf_free[0])
                            col = eix * HC + hg * 4 + hm
                            ins = e.activation(
                                s_hf[:, hm * 512: hm * 512 + 512], gb.ps[:],
                                AF.Silu, bias=s_b1[:, col:col + 1], scale=1.0)
                            ex.inc(ins)
                            ins = e.activation(
                                s_ha[:, hbuf * 2048 + hm * 512: hbuf * 2048 + hm * 512 + 512],
                                gb.ps[:],
                                AF.Silu, bias=s_b1[:, col:col + 1], scale=1.0)
                            gb.free = ex.inc(ins)
                        block.scalar(_)
                        h_toks.append(ex.tok())

                    def _(e, hbuf=hbuf, ht=h_toks):
                        W(e, ht[-1])
                        for hm in range(4):
                            ins = e.tensor_tensor(
                                s_hb[:, hbuf * 2048 + hm * 512: hbuf * 2048 + hm * 512 + 512],
                                s_hf[:, hm * 512: hm * 512 + 512],
                                s_ha[:, hbuf * 2048 + hm * 512: hbuf * 2048 + hm * 512 + 512],
                                ALU.subtract)
                            ev.inc(ins)
                    block.vector(_)
                    hsp_tok = ev.tok()
                    hf_free[0] = hsp_tok

                    for om in range(EC):
                        gb = B[6 + om % 2]

                        def _(e, om=om, gb=gb, ht=hsp_tok, buf=buf, hbuf=hbuf):
                            W(e, gb.free)
                            if om == 0:
                                W(e, ht)
                            first = True
                            for hc in range(4):
                                woff = buf * 4096 + hc * 1024 + om * 128
                                hoff = hbuf * 2048 + hc * 512
                                for (w_, h_) in ((s_w2a, s_ha), (s_w2b, s_ha),
                                                 (s_w2a, s_hb)):
                                    ins = e.matmul(
                                        gb.ps[:], w_[:, woff: woff + 128],
                                        h_[:, hoff: hoff + 512],
                                        start=first,
                                        stop=(hc == 3 and h_ is s_hb))
                                    first = False
                            gb.ready = pe.inc(ins)
                        block.tensor(_)

                        tslot = om % 2

                        def _(e, om=om, gb=gb, nt2=nt2, tslot=tslot, eix=eix,
                              mbts=mb_toks[eix]):
                            W(e, gb.ready)
                            if om == 0:
                                W(e, mbts[nt2])
                            W(e, tf_free[tslot])
                            ins = e.tensor_tensor(
                                s_tf[:, tslot * 512: tslot * 512 + 512],
                                gb.ps[:], s_mb[:, nt2 * 512: nt2 * 512 + 512],
                                ALU.mult)
                            gb.free = ev.inc(ins)
                        block.vector(_)
                        t_tok = ev.tok()

                        def _(e, om=om, nt2=nt2, tslot=tslot, t_tok=t_tok):
                            W(e, t_tok)
                            ins = e.tensor_add(
                                s_x2[:, om * 1024 + nt2 * 512: om * 1024 + nt2 * 512 + 512],
                                s_x2[:, om * 1024 + nt2 * 512: om * 1024 + nt2 * 512 + 512],
                                s_tf[:, tslot * 512: tslot * 512 + 512])
                            gp.inc(ins)
                        block.gpsimd(_)
                        tf_free[tslot] = gp.tok()
                    h_free[hbuf] = pe.tok()
                slabbuf_free[buf] = pe.tok()
                mb_free[0] = ev.tok()

            folds_tok = gp.tok()

            def _(e, folds_tok=folds_tok):
                W(e, folds_tok)
                for ofc in range(EC):
                    for nt2 in range(2):
                        ins = e.tensor_tensor(
                            s_x2[:, ofc * 1024 + nt2 * 512: ofc * 1024 + nt2 * 512 + 512],
                            s_x2[:, ofc * 1024 + nt2 * 512: ofc * 1024 + nt2 * 512 + 512],
                            s_x1a[:, ofc * 1024 + nt2 * 512: ofc * 1024 + nt2 * 512 + 512],
                            ALU.add)
                        ev.inc(ins)
                e.wait_ge(ev.h, ev.n)
                for ofc in range(EC):
                    for nt2 in range(2):
                        ins = e.tensor_tensor(
                            s_x2[:, ofc * 1024 + nt2 * 512: ofc * 1024 + nt2 * 512 + 512],
                            s_x2[:, ofc * 1024 + nt2 * 512: ofc * 1024 + nt2 * 512 + 512],
                            s_x1b[:, ofc * 1024 + nt2 * 512: ofc * 1024 + nt2 * 512 + 512],
                            ALU.add)
                        ev.inc(ins)
            block.vector(_)
            x2pre_tok = ev.tok()

            for nt2 in range(2):
                src = lambda ofc, nt2=nt2: s_x2[:, ofc * 1024 + nt2 * 512:
                                                ofc * 1024 + nt2 * 512 + 512]
                tmp = lambda ofc, nt2=nt2: s_wst[:, ofc * 512: ofc * 512 + 512]
                ln_tok = emit_ln(block, src, tmp, src, s_sqC, s_rowC, x2pre_tok)

                for tt4 in range(4):
                    jobs = []
                    for ofc in range(EC):
                        srcT = s_x2[:, ofc * 1024 + nt2 * 512 + tt4 * 128:
                                    ofc * 1024 + nt2 * 512 + tt4 * 128 + 128]
                        dstT = s_ot[:, ofc * 128: ofc * 128 + 128]
                        jobs.append((srcT, dstT, s_id[:]))
                    pre = (ln_tok,) if tt4 == 0 else ()
                    ot_tok = emit_transposes(block, jobs, [B[4], B[5]],
                                             pre_toks=pre,
                                             evac_pre_toks=(out_free[0],))

                    def _(e, half=half, nt2=nt2, tt4=tt4, ot_tok=ot_tok):
                        W(e, ot_tok)
                        row = half * 1024 + nt2 * 512 + tt4 * 128
                        out_free[0] = dma(e, sw, out[row:row + 128, :], s_ot[:])
                    block.gpsimd(_)

            x1h_free[0] = (pe.tok(), ev.tok(), ex.tok())
            slabbuf_free[0] = [pe.tok(), ev.tok(), ex.tok()]
            slabbuf_free[1] = [pe.tok(), ev.tok(), ex.tok()]
            half_fence[0] = (pe.tok(), ev.tok(), ex.tok(), gp.tok())

        def _(e):
            dma_drain(e, sw)
        block.gpsimd(_)

    close_scope(phC)
    for b in reversed(B):
        b.cm.__exit__(None, None, None)
    close_scope(live)
    return nc


def _make_consts():
    identf = np.eye(128, dtype=np.float32)
    identbf = np.eye(128, dtype=np.float32).astype(ml_dtypes.bfloat16)
    onescol = np.ones((128, 1), dtype=np.float32)
    onesrow = np.ones((1, 128), dtype=np.float32)
    cvec = np.zeros((128, 4), dtype=np.float32)
    cvec[:, 0] = EPS
    selm = np.zeros((EXP, EXP * 128), dtype=np.float32)
    for e_ in range(EXP):
        selm[e_, e_ * 128:(e_ + 1) * 128] = 1.0
    return identf, identbf, onescol, onesrow, cvec, selm


def _make_mask(par):
    i = np.arange(QS)[:, None]
    j = np.arange(TAIL)[None, :]
    m = np.where(j > par * 512 + i, NEG, 0.0).astype(np.float32)
    return np.ascontiguousarray(m.astype(ml_dtypes.bfloat16))


def _make_maskT(par):
    j = np.arange(TAIL)[:, None]
    i = np.arange(QS)[None, :]
    m = np.where(j > par * 512 + i, NEG, 0.0).astype(np.float32)
    return np.ascontiguousarray(m.astype(ml_dtypes.bfloat16))


def kernel(L0_attn_x, L0_attn_in_proj_w, L0_attn_in_proj_b, L0_attn_out_proj_w,
           L0_attn_out_proj_b, L0_ffn_Wg, L0_ffn_bg, L0_ffn_W1, L0_ffn_b1,
           L0_ffn_W2, L0_ffn_b2):
    x = np.ascontiguousarray(np.asarray(L0_attn_x, dtype=np.float32))
    Bb = x.shape[0]
    identf, identbf, onescol, onesrow, cvec, selm = _make_consts()
    weights = dict(
        ipw=np.ascontiguousarray(np.asarray(L0_attn_in_proj_w, np.float32)),
        ipb=np.ascontiguousarray(np.asarray(L0_attn_in_proj_b, np.float32)),
        opw=np.ascontiguousarray(np.asarray(L0_attn_out_proj_w, np.float32)),
        opb=np.ascontiguousarray(np.asarray(L0_attn_out_proj_b, np.float32)),
        wgw=np.ascontiguousarray(np.asarray(L0_ffn_Wg, np.float32)),
        bgv=np.ascontiguousarray(np.asarray(L0_ffn_bg, np.float32)),
        w1=np.ascontiguousarray(np.asarray(L0_ffn_W1, np.float32)),
        b1=np.ascontiguousarray(np.asarray(L0_ffn_b1, np.float32)),
        w2=np.ascontiguousarray(np.asarray(L0_ffn_W2, np.float32)),
        b2=np.ascontiguousarray(np.asarray(L0_ffn_b2, np.float32)),
        ident=identf, identb=identbf, onescol=onescol, onesrow=onesrow,
        cvec=cvec, selm=selm,
    )
    masks = [_make_mask(0), _make_mask(1)]
    masksT = [_make_maskT(0), _make_maskT(1)]
    in_maps = []
    for c in range(8):
        b, par = c // 2, c % 2
        blocks = [2 * k + par for k in range(NSLOT)]
        xo = np.ascontiguousarray(
            np.concatenate([x[b, a * 512:(a + 1) * 512] for a in blocks], axis=0))
        m = dict(weights)
        m["x_full"] = np.ascontiguousarray(x[b])
        m["x_own"] = xo
        m["mask_in"] = masks[par]
        m["maskT_in"] = masksT[par]
        in_maps.append(m)

    nc = build_program()
    res = run_bass_kernel_spmd(nc, in_maps, core_ids=list(range(8)))

    y = np.empty((Bb, S, E), dtype=np.float32)
    for c in range(8):
        b, par = c // 2, c % 2
        oc = np.asarray(res.results[c]["out"])
        for k in range(NSLOT):
            a = 2 * k + par
            y[b, a * 512:(a + 1) * 512] = oc[k * 512:(k + 1) * 512]
    return y


# version 16
# speedup vs baseline: 1.5142x; 1.0964x over previous
import numpy as np
import ml_dtypes
import concourse.bass as bass
from concourse import library_config
import concourse.mybir as mybir
from concourse.bass_utils import run_bass_kernel_spmd

F32 = mybir.dt.float32
BF16 = mybir.dt.bfloat16
AF = mybir.ActivationFunctionType
ALU = mybir.AluOpType
AXX = mybir.AxisListType.X

NEG = -1e30
EPS = 1e-5

S, E, H, HID, EXP = 4096, 1024, 16, 4096, 8
D = E // H
OWN = S // 2
QS = S // 8
EC = E // 128
HC = HID // 128
TAIL = S // 4
NSLOT = 4
NQT = QS // 128
SC = S // 128
ISQ = 1.0 / float(D) ** 0.5


def ceil_div(a, b):
    return (a + b - 1) // b


class Sem:
    __slots__ = ("h", "n")

    def __init__(self, nc, name):
        self.h = nc.alloc_semaphore(name=name)
        self.n = 0

    def inc(self, instr, k=1):
        instr.then_inc(self.h, k)
        self.n += k
        return (self, self.n)

    def tok(self):
        return (self, self.n)


def W(e, tk):
    if tk is None:
        return
    if isinstance(tk, (list, tuple)) and tk and isinstance(tk[0], (list, tuple)):
        for t in tk:
            W(e, t)
        return
    if tk[1] > 0:
        e.wait_ge(tk[0].h, tk[1])


class DmaPool:
    __slots__ = ("chans", "i")

    def __init__(self, nc, name, n=8):
        self.chans = [Sem(nc, "%s%d" % (name, i)) for i in range(n)]
        self.i = 0


class Bank:
    __slots__ = ("cm", "ps", "ready", "free")

    def __init__(self, nc, i):
        self.cm = nc.psum_tensor("pbank%d" % i, [128, 512], F32)
        self.ps = self.cm.__enter__()
        self.ready = None
        self.free = None


def build_program():
    nc = bass.Bass("TRN2", target_bir_lowering=False)

    x_full = nc.dram_tensor("x_full", [S, E], F32, kind="ExternalInput").ap()
    x_own = nc.dram_tensor("x_own", [OWN, E], F32, kind="ExternalInput").ap()
    mask_in = nc.dram_tensor("mask_in", [QS, TAIL], BF16, kind="ExternalInput").ap()
    maskT_in = nc.dram_tensor("maskT_in", [TAIL, QS], BF16, kind="ExternalInput").ap()
    ipw = nc.dram_tensor("ipw", [3 * E, E], F32, kind="ExternalInput").ap()
    ipb = nc.dram_tensor("ipb", [3 * E], F32, kind="ExternalInput").ap()
    opw = nc.dram_tensor("opw", [E, E], F32, kind="ExternalInput").ap()
    opb = nc.dram_tensor("opb", [E], F32, kind="ExternalInput").ap()
    wgw = nc.dram_tensor("wgw", [E, EXP], F32, kind="ExternalInput").ap()
    bgv = nc.dram_tensor("bgv", [EXP], F32, kind="ExternalInput").ap()
    w1 = nc.dram_tensor("w1", [EXP, E, HID], F32, kind="ExternalInput").ap()
    b1 = nc.dram_tensor("b1", [EXP, HID], F32, kind="ExternalInput").ap()
    w2 = nc.dram_tensor("w2", [EXP, HID, E], F32, kind="ExternalInput").ap()
    b2 = nc.dram_tensor("b2", [EXP, E], F32, kind="ExternalInput").ap()
    ident = nc.dram_tensor("ident", [128, 128], F32, kind="ExternalInput").ap()
    identb = nc.dram_tensor("identb", [128, 128], BF16, kind="ExternalInput").ap()
    onescol = nc.dram_tensor("onescol", [128, 1], F32, kind="ExternalInput").ap()
    onesrow = nc.dram_tensor("onesrow", [1, 128], F32, kind="ExternalInput").ap()
    cvec = nc.dram_tensor("cvec", [128, 4], F32, kind="ExternalInput").ap()
    selm = nc.dram_tensor("selm", [EXP, EXP * 128], F32, kind="ExternalInput").ap()
    out = nc.dram_tensor("out", [OWN, E], F32, kind="ExternalOutput").ap()

    Ks = nc.dram_tensor("Ks", [E, S], F32, kind="Internal").ap()
    Vs = nc.dram_tensor("Vs", [H, S, D], F32, kind="Internal").ap()
    Qs = nc.dram_tensor("Qs", [E, OWN], F32, kind="Internal").ap()
    X1s = nc.dram_tensor("X1s", [E, OWN], F32, kind="Internal").ap()

    live = []

    def SBuf(scope, name, shape, dt=F32):
        cm = nc.sbuf_tensor(name, shape, dt)
        t = cm.__enter__()
        scope.append(cm)
        return t

    def close_scope(scope):
        for cm in reversed(scope):
            cm.__exit__(None, None, None)
        scope.clear()

    s_id = SBuf(live, "s_id", [128, 128])
    s_idb = SBuf(live, "s_idb", [128, 128], BF16)
    s_1c = SBuf(live, "s_1c", [128, 1])
    s_1r = SBuf(live, "s_1r", [1, 128])
    s_cv = SBuf(live, "s_cv", [128, 4])
    s_bqkv = SBuf(live, "s_bqkv", [128, 24])
    s_bo = SBuf(live, "s_bo", [128, EC])
    s_bg = SBuf(live, "s_bg", [EXP, 1])
    s_b1 = SBuf(live, "s_b1", [128, EXP * HC])
    s_b2 = SBuf(live, "s_b2", [EXP, E])
    s_wg = SBuf(live, "s_wg", [128, EC * EXP])
    s_sel = SBuf(live, "s_sel", [EXP, EXP * 128])
    nc.const_aps.register(F32, EPS, s_cv[:, 0:1])
    nc.const_aps.register(F32, 0.0, s_cv[:, 1:2])

    B = [Bank(nc, i) for i in range(8)]
    pe = Sem(nc, "pe")
    ev = Sem(nc, "ev")
    ex = Sem(nc, "ex")
    gp = Sem(nc, "gp")
    ld = DmaPool(nc, "ld", 8)
    sw = DmaPool(nc, "sw", 8)

    def dma(e, pool, dst, src, slow=False):
        ch = pool.chans[pool.i % len(pool.chans)]
        pool.i += 1
        if ch.n:
            e.wait_ge(ch.h, ch.n)
        if slow:
            ins = e.dma_start(dst, src, allow_slow_non_contiguous=True)
        else:
            ins = e.dma_start(dst, src)
        ins.then_inc(ch.h, 16)
        ch.n += 16
        return (ch, ch.n)

    def dma_drain(e, pool):
        for ch in pool.chans:
            if ch.n:
                e.wait_ge(ch.h, ch.n)

    def emit_transposes(block, jobs, banks, evac="vector", pre_toks=(),
                        evac_pre_toks=()):
        nb = len(jobs)
        nbatch = ceil_div(nb, 4)
        sem = ev if evac == "vector" else ex

        for bi in range(nbatch):
            bank = banks[bi % 2]

            def _(e, bi=bi, bank=bank):
                if bi == 0:
                    for tk in pre_toks:
                        W(e, tk)
                W(e, bank.free)
                n4 = min(4, nb - bi * 4)
                for s in range(n4):
                    src, _d, idap = jobs[bi * 4 + s]
                    op_, ow = src.shape[1], src.shape[0]
                    ins = e.transpose(bank.ps[0:op_, s * 128:s * 128 + ow], src, idap)
                bank.ready = pe.inc(ins)
            block.tensor(_)

            def _(e, bi=bi, bank=bank):
                if bi == 0:
                    for tk in evac_pre_toks:
                        W(e, tk)
                W(e, bank.ready)
                n4 = min(4, nb - bi * 4)
                for s in range(n4):
                    src, dst, _i = jobs[bi * 4 + s]
                    op_, ow = src.shape[1], src.shape[0]
                    if evac == "vector":
                        ins = e.tensor_copy(dst, bank.ps[0:op_, s * 128:s * 128 + ow])
                    else:
                        ins = e.copy(dst, bank.ps[0:op_, s * 128:s * 128 + ow])
                bank.free = sem.inc(ins)
            if evac == "vector":
                block.vector(_)
            else:
                block.scalar(_)
        return sem.tok()

    phA = []
    s_wkvT = SBuf(phA, "s_wkvT", [128, EC * 2 * E])
    s_wqT = SBuf(phA, "s_wqT", [128, EC * E])
    s_wtmp = SBuf(phA, "s_wtmp", [128, 2 * E])
    s_xtm = SBuf(phA, "s_xtm", [128, 2 * 4 * E])
    s_xfm = SBuf(phA, "s_xfm", [128, 2 * EC * 512])
    s_kvq = SBuf(phA, "s_kvq", [128, 4 * 512])
    s_vout = SBuf(phA, "s_vout", [128, 2 * 512])

    with nc.named_scope("phaseA"), nc.Block() as block:
        def _(e):
            dma(e, ld, s_id[:], ident)
            dma(e, ld, s_idb[:], identb)
            dma(e, ld, s_1c[:], onescol)
            dma(e, ld, s_1r[:], onesrow)
            dma(e, ld, s_cv[:], cvec)
            dma(e, ld, s_bqkv[:], ipb.rearrange("(c p) -> p c", p=128), slow=True)
            dma(e, ld, s_bo[:], opb.rearrange("(c p) -> p c", p=128), slow=True)
            dma(e, ld, s_bg[:], bgv.rearrange("(e o) -> e o", o=1), slow=True)
            dma(e, ld, s_b1[:].rearrange("p (ee c) -> p ee c", c=HC),
                b1.rearrange("ee (c p) -> p ee c", p=128), slow=True)
            dma(e, ld, s_b2[:], b2)
            dma(e, ld, s_sel[:], selm)
            dma(e, ld, s_wg[:].rearrange("p (c e) -> p c e", e=EXP),
                wgw.rearrange("(c p) e -> p c e", p=128), slow=True)
        block.sync(_)

        strip_tok = [None] * 24
        strip_free = [None, None]

        def _(e):
            for r in range(2):
                strip_tok[r] = dma(e, ld, s_wtmp[:, r * E:(r + 1) * E],
                                   ipw[r * 128:(r + 1) * 128, :])
        block.sync(_)

        for r in range(24):
            jobs = []
            for ifc in range(EC):
                src = s_wtmp[:, (r % 2) * E + ifc * 128:(r % 2) * E + ifc * 128 + 128]
                if r < 8:
                    dst = s_wqT[:, ifc * E + r * 128: ifc * E + r * 128 + 128]
                else:
                    dst = s_wkvT[:, ifc * 2 * E + (r - 8) * 128:
                                 ifc * 2 * E + (r - 8) * 128 + 128]
                jobs.append((src, dst, s_id[:]))
            emit_transposes(block, jobs, [B[0], B[1]], pre_toks=(strip_tok[r],))
            strip_free[r % 2] = pe.tok()

            if r + 2 < 24:
                def _(e, r=r):
                    W(e, strip_free[r % 2])
                    strip_tok[r + 2] = dma(
                        e, ld, s_wtmp[:, (r % 2) * E:(r % 2) * E + E],
                        ipw[(r + 2) * 128:(r + 3) * 128, :])
                block.sync(_)
        wT_tok = ev.tok()

        kvq_free = [None] * 4
        kvq_i = [0]
        vout_free = [None, None]
        vb_i = [0]
        xtm_free = [None, None]
        xfm_free = [None, None]

        def emit_x_block(src_rows, parity):
            tok_holder = []

            def _(e, parity=parity):
                W(e, xtm_free[parity])
                tok_holder.append(dma(
                    e, ld,
                    s_xtm[:, parity * 4 * E: parity * 4 * E + 4 * E].rearrange(
                        "p (t d) -> p t d", d=E),
                    src_rows.rearrange("(t p) d -> p t d", p=128)))
            block.sync(_)

            jobs = []
            for j in range(4 * EC):
                t, ifc = divmod(j, EC)
                src = s_xtm[:, parity * 4 * E + t * E + ifc * 128:
                            parity * 4 * E + t * E + ifc * 128 + 128]
                dst = s_xfm[:, parity * EC * 512 + ifc * 512 + t * 128:
                            parity * EC * 512 + ifc * 512 + t * 128 + 128]
                jobs.append((src, dst, s_id[:]))
            emit_transposes(block, jobs, [B[2], B[3]],
                            pre_toks=(tok_holder[0], xfm_free[parity]))
            xtm_free[parity] = pe.tok()

        def emit_proj(parity, ofm_list, blk):
            for ofm in ofm_list:
                gb = B[4 + (ofm % 2)]
                is_q = ofm < 8

                def _(e, ofm=ofm, gb=gb, is_q=is_q, parity=parity):
                    W(e, gb.free)
                    W(e, wT_tok)
                    W(e, ev.tok())
                    for ifc in range(EC):
                        if is_q:
                            lhs = s_wqT[:, ifc * E + ofm * 128: ifc * E + ofm * 128 + 128]
                        else:
                            lhs = s_wkvT[:, ifc * 2 * E + (ofm - 8) * 128:
                                         ifc * 2 * E + (ofm - 8) * 128 + 128]
                        rhs = s_xfm[:, parity * EC * 512 + ifc * 512:
                                    parity * EC * 512 + ifc * 512 + 512]
                        ins = e.matmul(gb.ps[:], lhs, rhs,
                                       start=(ifc == 0), stop=(ifc == EC - 1))
                    gb.ready = pe.inc(ins)
                block.tensor(_)

                slot = kvq_i[0] % 4
                kvq_i[0] += 1

                def _(e, ofm=ofm, gb=gb, slot=slot):
                    W(e, gb.ready)
                    W(e, kvq_free[slot])
                    ins = e.activation(s_kvq[:, slot * 512:slot * 512 + 512],
                                       gb.ps[:], AF.Identity,
                                       bias=s_bqkv[:, ofm:ofm + 1], scale=1.0)
                    gb.free = ex.inc(ins)
                block.scalar(_)
                evac_tok = ex.tok()

                if is_q or ofm < 16:
                    def _(e, ofm=ofm, slot=slot, evac_tok=evac_tok, blk=blk,
                          is_q=is_q):
                        W(e, evac_tok)
                        if is_q:
                            dst = Qs[ofm * 128:(ofm + 1) * 128,
                                     blk * 512:(blk + 1) * 512]
                        else:
                            dst = Ks[(ofm - 8) * 128:(ofm - 7) * 128,
                                     blk * 512:(blk + 1) * 512]
                        kvq_free[slot] = dma(e, sw, dst,
                                             s_kvq[:, slot * 512:slot * 512 + 512])
                    block.gpsimd(_)
                else:
                    vb = vb_i[0] % 2
                    vb_i[0] += 1
                    vjobs = []
                    for t in range(4):
                        src = s_kvq[:, slot * 512 + t * 128: slot * 512 + t * 128 + 128]
                        dst = s_vout[:, vb * 512 + t * 128: vb * 512 + t * 128 + 128]
                        vjobs.append((src, dst, s_id[:]))
                    vtok = emit_transposes(block, vjobs, [B[6], B[7]],
                                           pre_toks=(evac_tok,),
                                           evac_pre_toks=vout_free[vb] or ())
                    kvq_free[slot] = pe.tok()

                    def _(e, ofm=ofm, blk=blk, vb=vb, vtok=vtok):
                        W(e, vtok)
                        tks = []
                        for hh in range(2):
                            hidx = (ofm - 16) * 2 + hh
                            src = s_vout[:, vb * 512:(vb + 1) * 512].rearrange(
                                "p (t dd) -> p t dd", dd=128)[:, :, hh * 64:(hh + 1) * 64]
                            dst = Vs[hidx, blk * 512:(blk + 1) * 512, :].rearrange(
                                "(t p) d -> p t d", p=128)
                            tks.append(dma(e, sw, dst, src))
                        vout_free[vb] = tuple(tks)
                    block.gpsimd(_)

        for tb in range(8):
            emit_x_block(x_full[tb * 512:(tb + 1) * 512, :], tb % 2)
            emit_proj(tb % 2, list(range(8, 24)), tb)
            xfm_free[tb % 2] = ex.tok()
        for tq in range(4):
            emit_x_block(x_own[tq * 512:(tq + 1) * 512, :], tq % 2)
            emit_proj(tq % 2, list(range(0, 8)), tq)
            xfm_free[tq % 2] = ex.tok()

        def _(e):
            dma_drain(e, sw)
        block.gpsimd(_)

    close_scope(phA)

    phBB2 = []
    s_ctx = SBuf(phBB2, "s_ctx", [128, EC * OWN])

    phB = []
    s_K = SBuf(phB, "s_K", [128, S])
    s_Kh = SBuf(phB, "s_Kh", [128, S], BF16)
    s_Kl = SBuf(phB, "s_Kl", [128, S], BF16)
    s_V = SBuf(phB, "s_V", [128, 2 * SC * 65])
    s_Vh = SBuf(phB, "s_Vh", [128, 2 * SC * 65], BF16)
    s_Vl = SBuf(phB, "s_Vl", [128, 2 * SC * 65], BF16)
    s_Q = SBuf(phB, "s_Q", [128, OWN])
    s_Qh = SBuf(phB, "s_Qh", [128, OWN], BF16)
    s_Ql = SBuf(phB, "s_Ql", [128, OWN], BF16)
    s_maskT = SBuf(phB, "s_maskT", [128, (TAIL // 128) * QS], BF16)
    s_ptf = SBuf(phB, "s_ptf", [128, 8 * 512])
    s_pth = SBuf(phB, "s_pth", [128, 8 * 512], BF16)
    s_ptl = SBuf(phB, "s_ptl", [128, 8 * 512], BF16)
    s_rt = SBuf(phB, "s_rt", [1, 512])
    s_rb = SBuf(phB, "s_rb", [64, 512])

    with nc.named_scope("phaseB"), nc.Block() as block:
        fenceA = (pe.tok(), ev.tok(), ex.tok())
        mask_tok = [None]

        def _(e):
            t2 = dma(e, ld, s_maskT[:].rearrange("p (c q) -> p c q", q=QS),
                     maskT_in.rearrange("(c p) q -> p c q", p=128))
            mask_tok[0] = t2
        block.sync(_)

        split_toks = [None, None]
        head_end = [None, None]
        kvq_toks = [None, None]
        pt_free = [None] * 8
        ptf_free = [None] * 8
        rb_free = [None]
        ctx_banks = [B[4], B[5]]
        bc_bank = B[6]

        def emit_kvq_load(h):
            hb = h % 2

            def _(e, h=h, hb=hb):
                W(e, head_end[hb])
                if h < 2:
                    W(e, fenceA)
                t1 = dma(e, ld, s_K[hb * 64:(hb + 1) * 64, :], Ks[h * D:(h + 1) * D, :])
                vdst = s_V[:].rearrange("p (b c d) -> p b c d", b=2, d=65)[:, hb, :, 0:64]
                t2 = dma(e, ld, vdst, Vs[h].rearrange("(c p) d -> p c d", p=128))
                t3 = dma(e, ld, s_Q[hb * 64:(hb + 1) * 64, :],
                         Qs[h * D:(h + 1) * D, :])
                kvq_toks[hb] = (t1, t2, t3)
            block.sync(_)

            def _(e, hb=hb):
                W(e, kvq_toks[hb])
                vv = s_V[:].rearrange("p (b c d) -> p b c d", b=2, d=65)
                ins = e.tensor_scalar(vv[:, hb, :, 64:65], vv[:, hb, :, 63:64],
                                      0.0, 1.0, ALU.mult, ALU.add)
                ev.inc(ins)
                e.wait_ge(ev.h, ev.n)
                ks = s_K[hb * 64:(hb + 1) * 64, :]
                qs_ = s_Q[hb * 64:(hb + 1) * 64, :]
                vs_ = s_V[:, hb * SC * 65:(hb + 1) * SC * 65]
                ins = e.tensor_copy(s_Kh[hb * 64:(hb + 1) * 64, :], ks)
                ev.inc(ins)
                ins = e.tensor_copy(s_Qh[hb * 64:(hb + 1) * 64, :], qs_)
                ev.inc(ins)
                ins = e.tensor_copy(s_Vh[:, hb * SC * 65:(hb + 1) * SC * 65], vs_)
                ev.inc(ins)
                e.wait_ge(ev.h, ev.n)
                ins = e.tensor_tensor(s_Kl[hb * 64:(hb + 1) * 64, :], ks,
                                      s_Kh[hb * 64:(hb + 1) * 64, :], ALU.subtract)
                ev.inc(ins)
                ins = e.tensor_tensor(s_Ql[hb * 64:(hb + 1) * 64, :], qs_,
                                      s_Qh[hb * 64:(hb + 1) * 64, :], ALU.subtract)
                ev.inc(ins)
                ins = e.tensor_tensor(s_Vl[:, hb * SC * 65:(hb + 1) * SC * 65], vs_,
                                      s_Vh[:, hb * SC * 65:(hb + 1) * SC * 65],
                                      ALU.subtract)
                split_toks[hb] = ev.inc(ins)
            block.vector(_)

        def emit_slot(h, k):
            ek = (k + 1) * TAIL
            nkc128 = ek // 128
            hb = h % 2
            cbank = ctx_banks[k % 2]
            npairs = nkc128 // 2
            pv_jobs = []
            for p in range(npairs):
                kcs = [2 * p, 2 * p + 1]
                banks = [B[kc % 4] for kc in kcs]

                def _(e, kcs=kcs, banks=banks, hb=hb, k=k, h=h, nkc128=nkc128):
                    for b_ in banks:
                        W(e, b_.free)
                    if kcs[0] == 0 and k == 0:
                        W(e, split_toks[hb])
                        if h == 0:
                            W(e, mask_tok[0])
                    for kc, b_ in zip(kcs, banks):
                        qh = s_Qh[hb * 64:(hb + 1) * 64, k * QS: k * QS + 512]
                        ql = s_Ql[hb * 64:(hb + 1) * 64, k * QS: k * QS + 512]
                        kh = s_Kh[hb * 64:(hb + 1) * 64, kc * 128: kc * 128 + 128]
                        kl = s_Kl[hb * 64:(hb + 1) * 64, kc * 128: kc * 128 + 128]
                        tailc = kc >= nkc128 - 8
                        e.matmul(b_.ps[:], kh, qh, start=True, stop=False)
                        e.matmul(b_.ps[:], kh, ql, start=False, stop=False)
                        ins = e.matmul(b_.ps[:], kl, qh, start=False,
                                       stop=not tailc)
                        if tailc:
                            tc = kc - (nkc128 - 8)
                            ins = e.matmul(b_.ps[:], s_idb[:],
                                           s_maskT[:, tc * QS: tc * QS + 512],
                                           start=False, stop=True)
                        b_.ready = pe.inc(ins)
                block.tensor(_)

                for kc, b_ in zip(kcs, banks):
                    ps_ = kc % 8

                    def _(e, kc=kc, b_=b_, ps_=ps_):
                        W(e, b_.ready)
                        W(e, ptf_free[ps_])
                        ins = e.activation(s_ptf[:, ps_ * 512: ps_ * 512 + 512],
                                           b_.ps[:], AF.Exp, bias=0.0, scale=ISQ)
                        b_.free = ex.inc(ins)
                    block.scalar(_)
                    exp_tok = ex.tok()

                    def _(e, ps_=ps_, exp_tok=exp_tok):
                        W(e, exp_tok)
                        W(e, pt_free[ps_])
                        pf = s_ptf[:, ps_ * 512: ps_ * 512 + 512]
                        ph = s_pth[:, ps_ * 512: ps_ * 512 + 512]
                        ins = e.tensor_copy(ph, pf)
                        ev.inc(ins)
                        e.wait_ge(ev.h, ev.n)
                        ins = e.tensor_tensor(s_ptl[:, ps_ * 512: ps_ * 512 + 512],
                                              pf, ph, ALU.subtract)
                        ev.inc(ins)
                    block.vector(_)
                    ptf_free[ps_] = ev.tok()

                split_pt = ev.tok()

                def pv_(e, kcs=kcs, hb=hb, split_pt=split_pt, cbank=cbank,
                        nkc128=nkc128, h=h, k=k):
                    W(e, split_pt)
                    if kcs[0] == 0:
                        W(e, cbank.free)
                    for kc in kcs:
                        ps_ = kc % 8
                        vh = s_Vh[:, hb * SC * 65 + kc * 65: hb * SC * 65 + kc * 65 + 65]
                        vl = s_Vl[:, hb * SC * 65 + kc * 65: hb * SC * 65 + kc * 65 + 65]
                        ph = s_pth[:, ps_ * 512: ps_ * 512 + 512]
                        pl = s_ptl[:, ps_ * 512: ps_ * 512 + 512]
                        last = kc == nkc128 - 1
                        e.matmul(cbank.ps[0:65, 0:512], vh, ph,
                                 start=(kc == 0), stop=False)
                        e.matmul(cbank.ps[0:65, 0:512], vh, pl,
                                 start=False, stop=False)
                        ins = e.matmul(cbank.ps[0:65, 0:512], vl, ph,
                                       start=False, stop=last)
                        t = pe.inc(ins)
                        pt_free[ps_] = pe.tok()
                        if last:
                            cbank.ready = t
                pv_jobs.append(pv_)
                if p >= 2:
                    block.tensor(pv_jobs[p - 2])
            block.tensor(pv_jobs[npairs - 2])
            block.tensor(pv_jobs[npairs - 1])

            def _(e, cbank=cbank):
                W(e, cbank.ready)
                ins = e.tensor_copy(s_rt[0:1, :], cbank.ps[64:65, 0:512])
                ev.inc(ins)
                e.wait_ge(ev.h, ev.n)
                ins = e.reciprocal(s_rt[0:1, :], s_rt[0:1, :])
                ev.inc(ins)
            block.vector(_)
            rt_tok = ev.tok()

            def _(e, rt_tok=rt_tok):
                W(e, rt_tok)
                W(e, bc_bank.free)
                ins = e.matmul(bc_bank.ps[0:64, 0:512], s_1r[0:1, 0:64],
                               s_rt[0:1, :], start=True, stop=True)
                bc_bank.ready = pe.inc(ins)
            block.tensor(_)

            def _(e):
                W(e, bc_bank.ready)
                W(e, rb_free[0])
                ins = e.tensor_copy(s_rb[:], bc_bank.ps[0:64, 0:512])
                bc_bank.free = ev.inc(ins)
            block.vector(_)
            rb_tok = ev.tok()

            def _(e, h=h, k=k, cbank=cbank, rb_tok=rb_tok):
                W(e, cbank.ready)
                W(e, rb_tok)
                ins = e.tensor_tensor(
                    s_ctx[(h % 2) * 64:(h % 2) * 64 + 64,
                          (h // 2) * OWN + k * QS:(h // 2) * OWN + k * QS + QS],
                    cbank.ps[0:64, 0:QS], s_rb[:], ALU.mult)
                cbank.free = ev.inc(ins)
            block.vector(_)
            rb_free[0] = ev.tok()

        for h in range(H):
            emit_kvq_load(h)
            for k in range(NSLOT):
                emit_slot(h, k)
            head_end[h % 2] = pe.tok()

    close_scope(phB)

    def emit_ln(block, src, tmp, dst, s_sq, s_row, pre_tok):
        bsum, bsq, bbc1, bbc2 = B[0], B[1], B[2], B[3]
        sq_toks = []
        sqmm_toks = []

        for ofc in range(EC):
            def _(e, ofc=ofc):
                if ofc == 0:
                    W(e, pre_tok)
                if ofc >= 2:
                    W(e, sqmm_toks[ofc - 2])
                ins = e.activation(s_sq[:, (ofc % 2) * 512:(ofc % 2) * 512 + 512],
                                   src(ofc), AF.Square, bias=0.0, scale=1.0)
                sq_toks.append(ex.inc(ins))
            block.scalar(_)

            def _(e, ofc=ofc):
                if ofc == 0:
                    W(e, pre_tok)
                    W(e, bsum.free)
                    W(e, bsq.free)
                ins = e.matmul(bsum.ps[0:1, 0:512], s_1c[:], src(ofc),
                               start=(ofc == 0), stop=(ofc == EC - 1))
                if ofc == EC - 1:
                    bsum.ready = pe.inc(ins)
                W(e, sq_toks[ofc])
                ins = e.matmul(bsq.ps[0:1, 0:512], s_1c[:],
                               s_sq[:, (ofc % 2) * 512:(ofc % 2) * 512 + 512],
                               start=(ofc == 0), stop=(ofc == EC - 1))
                sqmm_toks.append(pe.inc(ins))
            block.tensor(_)
        bsq.ready = sqmm_toks[-1]

        r = lambda i: s_row[0:1, i * 512:(i + 1) * 512]

        def _(e):
            W(e, bsum.ready)
            ins = e.tensor_copy(r(0), bsum.ps[0:1, 0:512])
            bsum.free = ev.inc(ins)
            W(e, bsq.ready)
            ins = e.tensor_copy(r(1), bsq.ps[0:1, 0:512])
            bsq.free = ev.inc(ins)
            e.wait_ge(ev.h, ev.n)
            ins = e.tensor_scalar(r(2), r(0), 1.0 / E, None, ALU.mult)
            ev.inc(ins)
            ins = e.tensor_scalar(r(3), r(1), 1.0 / E, None, ALU.mult)
            ev.inc(ins)
            e.wait_ge(ev.h, ev.n)
            ins = e.tensor_tensor(r(4), r(2), r(2), ALU.mult)
            ev.inc(ins)
            e.wait_ge(ev.h, ev.n)
            ins = e.tensor_tensor(r(4), r(3), r(4), ALU.subtract)
            ev.inc(ins)
        block.vector(_)
        var_tok = ev.tok()

        def _(e, var_tok=var_tok):
            W(e, var_tok)
            ins = e.activation(r(3), r(4), AF.Sqrt, bias=EPS, scale=1.0)
            ex.inc(ins)
        block.scalar(_)
        sd_tok = ex.tok()

        def _(e, sd_tok=sd_tok):
            W(e, sd_tok)
            ins = e.reciprocal(r(5), r(3))
            ev.inc(ins)
            e.wait_ge(ev.h, ev.n)
            ins = e.tensor_tensor(r(4), r(2), r(5), ALU.mult)
            ev.inc(ins)
        block.vector(_)
        nmr_tok = ev.tok()

        def _(e, nmr_tok=nmr_tok):
            W(e, nmr_tok)
            W(e, bbc1.free)
            ins = e.matmul(bbc1.ps[:, 0:512], s_1r[:], r(5), start=True, stop=True)
            bbc1.ready = pe.inc(ins)
            W(e, bbc2.free)
            ins = e.matmul(bbc2.ps[:, 0:512], s_1r[:], r(4), start=True, stop=True)
            bbc2.ready = pe.inc(ins)
        block.tensor(_)

        def _(e):
            W(e, bbc1.ready)
            W(e, bbc2.ready)
            for ofc in range(EC):
                ins = e.tensor_tensor(tmp(ofc), src(ofc), bbc1.ps[:, 0:512], ALU.mult)
                if ofc == EC - 1:
                    bbc1.free = ev.inc(ins)
            e.wait_ge(ev.h, ev.n)
            for ofc in range(EC):
                ins = e.tensor_tensor(dst(ofc), tmp(ofc), bbc2.ps[:, 0:512], ALU.subtract)
                if ofc == EC - 1:
                    bbc2.free = ev.inc(ins)
        block.vector(_)
        return ev.tok()

    phB2 = []
    s_woT = SBuf(phB2, "s_woT", [128, EC * E])
    s_xo = SBuf(phB2, "s_xo", [128, EC * 512])
    s_xtm2 = SBuf(phB2, "s_xtm2", [128, 4 * E])
    s_t1 = SBuf(phB2, "s_t1", [128, EC * 512])
    s_sqA = SBuf(phB2, "s_sqA", [128, 2 * 512])
    s_rowA = SBuf(phB2, "s_rowA", [1, 6 * 512])

    with nc.named_scope("phaseB2"), nc.Block() as block:
        xtm2_free = [None]
        for rr in range(8):
            tok_holder = []

            def _(e, rr=rr):
                W(e, xtm2_free[0])
                tok_holder.append(dma(e, ld, s_xtm2[:, 0:E],
                                      opw[rr * 128:(rr + 1) * 128, :]))
            block.sync(_)
            jobs = []
            for cfc in range(EC):
                src = s_xtm2[:, cfc * 128:cfc * 128 + 128]
                dst = s_woT[:, cfc * E + rr * 128: cfc * E + rr * 128 + 128]
                jobs.append((src, dst, s_id[:]))
            emit_transposes(block, jobs, [B[0], B[1]], pre_toks=(tok_holder[0],))
            xtm2_free[0] = pe.tok()
        woT_tok = ev.tok()

        for nt in range(4):
            tok_holder = []
            xo_prev = ev.tok()

            def _(e, nt=nt):
                W(e, xtm2_free[0])
                tok_holder.append(dma(
                    e, ld,
                    s_xtm2[:].rearrange("p (t d) -> p t d", d=E),
                    x_own[nt * 512:(nt + 1) * 512, :].rearrange("(t p) d -> p t d", p=128)))
            block.sync(_)
            jobs = []
            for j in range(4 * EC):
                t, ifc = divmod(j, EC)
                srcj = s_xtm2[:, t * E + ifc * 128: t * E + ifc * 128 + 128]
                dstj = s_xo[:, ifc * 512 + t * 128: ifc * 512 + t * 128 + 128]
                jobs.append((srcj, dstj, s_id[:]))
            emit_transposes(block, jobs, [B[2], B[3]],
                            pre_toks=(tok_holder[0],), evac_pre_toks=(xo_prev,))
            xtm2_free[0] = pe.tok()
            xo_tok = ev.tok()

            evac_toks = []
            for ofm in range(EC):
                gb = B[4 + ofm % 2]

                def _(e, ofm=ofm, gb=gb, nt=nt):
                    W(e, gb.free)
                    if nt == 0 and ofm == 0:
                        W(e, woT_tok)
                    for cfc in range(EC):
                        lhs = s_woT[:, cfc * E + ofm * 128: cfc * E + ofm * 128 + 128]
                        rhs = s_ctx[:, cfc * OWN + nt * 512: cfc * OWN + nt * 512 + 512]
                        ins = e.matmul(gb.ps[:], lhs, rhs,
                                       start=(cfc == 0), stop=(cfc == EC - 1))
                    gb.ready = pe.inc(ins)
                block.tensor(_)

                def _(e, ofm=ofm, gb=gb):
                    W(e, gb.ready)
                    ins = e.activation(s_t1[:, ofm * 512: ofm * 512 + 512],
                                       gb.ps[:], AF.Identity,
                                       bias=s_bo[:, ofm:ofm + 1], scale=1.0)
                    gb.free = ex.inc(ins)
                block.scalar(_)
                evac_toks.append(ex.tok())

            def _(e, nt=nt, et=evac_toks[-1]):
                W(e, et)
                W(e, xo_tok)
                for ofc in range(EC):
                    ins = e.tensor_tensor(
                        s_ctx[:, ofc * OWN + nt * 512: ofc * OWN + nt * 512 + 512],
                        s_xo[:, ofc * 512: ofc * 512 + 512],
                        s_t1[:, ofc * 512: ofc * 512 + 512], ALU.add)
                ev.inc(ins)
            block.vector(_)
            add_tok = ev.tok()

            src = lambda ofc, nt=nt: s_ctx[:, ofc * OWN + nt * 512: ofc * OWN + nt * 512 + 512]
            tmp = lambda ofc, nt=nt: s_xo[:, ofc * 512: ofc * 512 + 512]
            ln_tok = emit_ln(block, src, tmp, src, s_sqA, s_rowA, add_tok)

            def _(e, nt=nt, ln_tok=ln_tok):
                W(e, ln_tok)
                sctx = s_ctx[:].rearrange("p (c t) -> p c t", t=OWN)[:, :, nt * 512:(nt + 1) * 512]
                dstx = X1s.rearrange("(c p) t -> p c t", p=128)[:, :, nt * 512:(nt + 1) * 512]
                dma(e, sw, dstx, sctx)
            block.gpsimd(_)

        def _(e):
            dma_drain(e, sw)
        block.gpsimd(_)

    close_scope(phB2)
    close_scope(phBB2)

    phC = []
    s_x1a = SBuf(phC, "s_x1a", [128, EC * 1024], BF16)
    s_x1b = SBuf(phC, "s_x1b", [128, EC * 1024], BF16)
    s_x2 = SBuf(phC, "s_x2", [128, EC * 1024])
    s_hf = SBuf(phC, "s_hf", [128, 4 * 512])
    s_ha = SBuf(phC, "s_ha", [128, 2 * 4 * 512], BF16)
    s_hb = SBuf(phC, "s_hb", [128, 2 * 4 * 512], BF16)
    s_wst = SBuf(phC, "s_wst", [128, 4096])
    s_w1a = SBuf(phC, "s_w1a", [128, 2 * EC * 512], BF16)
    s_w1b = SBuf(phC, "s_w1b", [128, 2 * EC * 512], BF16)
    s_w2a = SBuf(phC, "s_w2a", [128, 2 * 4 * 1024], BF16)
    s_w2b = SBuf(phC, "s_w2b", [128, 2 * 4 * 1024], BF16)
    s_wga = SBuf(phC, "s_wga", [128, EC * EXP], BF16)
    s_wgb = SBuf(phC, "s_wgb", [128, EC * EXP], BF16)

    s_gt = SBuf(phC, "s_gt", [128, 8 * EXP])
    s_ge = SBuf(phC, "s_ge", [128, 8 * EXP])
    s_gst = SBuf(phC, "s_gst", [128, 32])
    s_mf = SBuf(phC, "s_mf", [EXP, 1024])
    s_gl = s_mf
    s_mb = SBuf(phC, "s_mb", [128, 2 * 512])
    s_tf = SBuf(phC, "s_tf", [128, 2 * 512])
    s_sqC = s_hf
    s_rowC = SBuf(phC, "s_rowC", [1, 6 * 512])
    s_ot = SBuf(phC, "s_ot", [128, 1024])

    with nc.named_scope("phaseC"), nc.Block() as block:
        x1h_free = [None]
        x1ch_free = [None]
        x1_tok = [None]
        half_fence = [None]
        tf_free = [None] * 2
        out_free = [None]
        slabbuf_free = [None, None]
        h_free = [None, None]
        hf_free = [None]
        mb_free = [None]

        def _(e):
            ins = e.tensor_copy(s_wga[:], s_wg[:])
            ev.inc(ins)
            e.wait_ge(ev.h, ev.n)
            ins = e.tensor_tensor(s_wgb[:], s_wg[:], s_wga[:], ALU.subtract)
            ev.inc(ins)
        block.vector(_)
        wg_tok = ev.tok()

        for half in range(2):

            x1sp_tok = None
            for chk in range(2):
                def _(e, half=half, chk=chk):
                    if chk == 0 and x1h_free[0] is not None:
                        for tk in x1h_free[0]:
                            W(e, tk)
                    if chk == 1:
                        W(e, x1ch_free[0])
                    src = X1s.rearrange("(c p) t -> p c t", p=128)[
                        :, chk * 4:(chk + 1) * 4, half * 1024:(half + 1) * 1024]
                    dst = s_wst[:].rearrange("p (c t) -> p c t", t=1024)
                    x1_tok[0] = dma(e, ld, dst, src)
                block.sync(_)

                def _(e, half=half, chk=chk):
                    W(e, x1_tok[0])
                    if chk == 0 and half_fence[0] is not None:
                        for tk in half_fence[0]:
                            W(e, tk)
                    ins = e.tensor_copy(s_x1a[:, chk * 4096:(chk + 1) * 4096],
                                        s_wst[:])
                    ev.inc(ins)
                    e.wait_ge(ev.h, ev.n)
                    ins = e.tensor_tensor(s_x1b[:, chk * 4096:(chk + 1) * 4096],
                                          s_wst[:],
                                          s_x1a[:, chk * 4096:(chk + 1) * 4096],
                                          ALU.subtract)
                    ev.inc(ins)
                block.vector(_)
                x1ch_free[0] = ev.tok()
            x1sp_tok = ev.tok()
            stage_free_init = ev.tok()

            gl_toks = []
            for nt2 in range(2):
                gb = B[0]

                def _(e, nt2=nt2, gb=gb):
                    W(e, gb.free)
                    if nt2 == 0:
                        W(e, x1sp_tok)
                        W(e, wg_tok)
                    first = True
                    for dc in range(EC):
                        for (wg_, x_) in ((s_wga, s_x1a), (s_wgb, s_x1a), (s_wga, s_x1b)):
                            ins = e.matmul(gb.ps[0:EXP, 0:512],
                                           wg_[:, dc * EXP:(dc + 1) * EXP],
                                           x_[:, dc * 1024 + nt2 * 512: dc * 1024 + nt2 * 512 + 512],
                                           start=first, stop=(dc == EC - 1 and x_ is s_x1b))
                            first = False
                    gb.ready = pe.inc(ins)
                block.tensor(_)

                def _(e, nt2=nt2, gb=gb):
                    W(e, gb.ready)
                    ins = e.activation(s_gl[:, nt2 * 512: nt2 * 512 + 512],
                                       gb.ps[0:EXP, 0:512], AF.Identity,
                                       bias=s_bg[:], scale=1.0)
                    gb.free = ex.inc(ins)
                block.scalar(_)
                gl_toks.append(ex.tok())

            jobs = []
            for ti in range(8):
                jobs.append((s_gl[0:EXP, ti * 128:(ti + 1) * 128],
                             s_gt[:, ti * EXP:(ti + 1) * EXP], s_id[0:EXP, 0:EXP]))
            gt_tok = emit_transposes(block, jobs, [B[2], B[3]],
                                     pre_toks=(gl_toks[-1],))

            def _(e, gt_tok=gt_tok):
                W(e, gt_tok)
                for ti in range(8):
                    ins = e.reduce_max(s_gst[:, ti * 4: ti * 4 + 1],
                                       s_gt[:, ti * EXP:(ti + 1) * EXP], axis=AXX)
                    ev.inc(ins)
                e.wait_ge(ev.h, ev.n)
                for ti in range(8):
                    ins = e.tensor_scalar(s_gst[:, ti * 4 + 1: ti * 4 + 2],
                                          s_gst[:, ti * 4: ti * 4 + 1],
                                          -1.0, None, ALU.mult)
                    ev.inc(ins)
            block.vector(_)
            gnm_tok = ev.tok()

            def _(e, gnm_tok=gnm_tok):
                W(e, gnm_tok)
                for ti in range(8):
                    ins = e.activation(s_ge[:, ti * EXP: ti * EXP + EXP],
                                       s_gt[:, ti * EXP:(ti + 1) * EXP],
                                       AF.Exp, bias=s_gst[:, ti * 4 + 1: ti * 4 + 2],
                                       scale=1.0,
                                       accum_out=s_gst[:, ti * 4 + 2: ti * 4 + 3])
                    ex.inc(ins)
            block.scalar(_)
            gs_tok = ex.tok()

            def _(e, gs_tok=gs_tok):
                W(e, gs_tok)
                for ti in range(8):
                    ins = e.reciprocal(s_gst[:, ti * 4 + 3: ti * 4 + 4],
                                       s_gst[:, ti * 4 + 2: ti * 4 + 3])
                    ev.inc(ins)
                e.wait_ge(ev.h, ev.n)
                for ti in range(8):
                    ins = e.tensor_scalar(s_gt[:, ti * EXP:(ti + 1) * EXP],
                                          s_gt[:, ti * EXP:(ti + 1) * EXP],
                                          s_gst[:, ti * 4: ti * 4 + 1],
                                          s_gst[:, ti * 4 + 3: ti * 4 + 4],
                                          ALU.is_equal, ALU.mult)
                    ev.inc(ins)
            block.vector(_)
            mt_tok = ev.tok()

            jobs = []
            for ti in range(8):
                jobs.append((s_gt[:, ti * EXP:(ti + 1) * EXP],
                             s_mf[0:EXP, ti * 128:(ti + 1) * 128], s_id[:]))
            mf_tok = emit_transposes(block, jobs, [B[2], B[3]], pre_toks=(mt_tok,))

            gp_prev = gp.tok()
            for nt2 in range(2):
                for om in range(EC):
                    gb = B[4 + om % 2]

                    def _(e, nt2=nt2, om=om, gb=gb, mf_tok=mf_tok):
                        W(e, gb.free)
                        if nt2 == 0 and om == 0:
                            W(e, mf_tok)
                        ins = e.matmul(gb.ps[:], s_b2[0:EXP, om * 128:(om + 1) * 128],
                                       s_mf[0:EXP, nt2 * 512: nt2 * 512 + 512],
                                       start=True, stop=True)
                        gb.ready = pe.inc(ins)
                    block.tensor(_)

                    def _(e, nt2=nt2, om=om, gb=gb, gp_prev=gp_prev):
                        W(e, gb.ready)
                        W(e, gp_prev)
                        ins = e.tensor_copy(
                            s_x2[:, om * 1024 + nt2 * 512: om * 1024 + nt2 * 512 + 512],
                            gb.ps[:])
                        gb.free = ev.inc(ins)
                    block.vector(_)

            slabs = [(eix, hg) for eix in range(EXP) for hg in range(8)]
            stage_tok = [None, None]
            stage_free = [stage_free_init, stage_free_init]
            split_tok = [None, None]
            mb_toks = {}

            def emit_slab_load_split(si):
                buf = si % 2
                eix, hg = slabs[si]
                for ci in range(4):
                    part, sub = ci // 2, ci % 2
                    rs = ci % 2
                    stg = s_wst[:, rs * 2048:(rs + 1) * 2048]

                    def _(e, part=part, sub=sub, rs=rs, eix=eix, hg=hg, stg=stg):
                        if stage_free[rs] is not None:
                            W(e, stage_free[rs])
                        if part == 0:
                            sl = w1[eix][sub * 512:(sub + 1) * 512,
                                         hg * 512:(hg + 1) * 512].rearrange(
                                "(c p) h -> p c h", p=128)
                            stage_tok[rs] = dma(
                                e, ld, stg.rearrange("p (c h) -> p c h", h=512), sl)
                        else:
                            sl = w2[eix][hg * 512 + sub * 256:
                                         hg * 512 + (sub + 1) * 256, :].rearrange(
                                "(c p) o -> p c o", p=128)
                            stage_tok[rs] = dma(
                                e, ld, stg.rearrange("p (c o) -> p c o", o=E), sl)
                    block.sync(_)

                    def _(e, buf=buf, part=part, sub=sub, rs=rs, stg=stg):
                        W(e, stage_tok[rs])
                        if part == 0 and sub == 0 and slabbuf_free[buf] is not None:
                            for tk in (slabbuf_free[buf] if isinstance(slabbuf_free[buf], list)
                                       else [slabbuf_free[buf]]):
                                W(e, tk)
                        wa = s_w1a if part == 0 else s_w2a
                        wb = s_w1b if part == 0 else s_w2b
                        off = buf * 4096 + sub * 2048
                        ins = e.tensor_copy(wa[:, off: off + 2048], stg)
                        ev.inc(ins)
                        e.wait_ge(ev.h, ev.n)
                        ins = e.tensor_tensor(wb[:, off: off + 2048], stg,
                                              wa[:, off: off + 2048],
                                              ALU.subtract)
                        t = ev.inc(ins)
                        if part == 1 and sub == 1:
                            split_tok[buf] = t
                        stage_free[rs] = ev.tok()
                    block.vector(_)

            def emit_mb(eix):
                toks = []
                for nt2 in range(2):
                    gb = B[1]

                    def _(e, eix=eix, nt2=nt2, gb=gb, mf_tok=mf_tok):
                        W(e, gb.free)
                        if eix == 0 and nt2 == 0:
                            W(e, mf_tok)
                        ins = e.matmul(gb.ps[:],
                                       s_sel[0:EXP, eix * 128:(eix + 1) * 128],
                                       s_mf[0:EXP, nt2 * 512: nt2 * 512 + 512],
                                       start=True, stop=True)
                        gb.ready = pe.inc(ins)
                    block.tensor(_)

                    def _(e, nt2=nt2, gb=gb):
                        W(e, gb.ready)
                        W(e, mb_free[0])
                        ins = e.tensor_copy(s_mb[:, nt2 * 512: nt2 * 512 + 512], gb.ps[:])
                        gb.free = ev.inc(ins)
                    block.vector(_)
                    toks.append(ev.tok())
                return toks

            emit_slab_load_split(0)
            for si, (eix, hg) in enumerate(slabs):
                buf = si % 2
                if hg == 0:
                    mb_toks[eix] = emit_mb(eix)
                if si + 1 < len(slabs):
                    emit_slab_load_split(si + 1)
                my_split = split_tok[buf]
                for nt2 in range(2):
                    hbuf = (si * 2 + nt2) % 2
                    h_toks = []
                    for hm in range(4):
                        gb = B[4 + hm % 2]

                        def _(e, nt2=nt2, hm=hm, gb=gb, buf=buf,
                              st_=my_split):
                            W(e, gb.free)
                            if hm == 0 and nt2 == 0:
                                W(e, st_)
                            first = True
                            for dc in range(EC):
                                off = buf * 4096 + dc * 512 + hm * 128
                                for (w_, x_) in ((s_w1a, s_x1a), (s_w1b, s_x1a),
                                                 (s_w1a, s_x1b)):
                                    ins = e.matmul(
                                        gb.ps[:], w_[:, off: off + 128],
                                        x_[:, dc * 1024 + nt2 * 512: dc * 1024 + nt2 * 512 + 512],
                                        start=first,
                                        stop=(dc == EC - 1 and x_ is s_x1b))
                                    first = False
                            gb.ready = pe.inc(ins)
                        block.tensor(_)

                        def _(e, eix=eix, hg=hg, hm=hm, gb=gb, hbuf=hbuf):
                            W(e, gb.ready)
                            if hm == 0:
                                W(e, h_free[hbuf])
                                W(e, hf_free[0])
                            col = eix * HC + hg * 4 + hm
                            ins = e.activation(
                                s_hf[:, hm * 512: hm * 512 + 512], gb.ps[:],
                                AF.Silu, bias=s_b1[:, col:col + 1], scale=1.0)
                            gb.free = ex.inc(ins)
                        block.scalar(_)
                        h_toks.append(ex.tok())

                    def _(e, hbuf=hbuf, ht=h_toks):
                        W(e, ht[-1])
                        for hm in range(4):
                            ins = e.tensor_copy(
                                s_ha[:, hbuf * 2048 + hm * 512: hbuf * 2048 + hm * 512 + 512],
                                s_hf[:, hm * 512: hm * 512 + 512])
                            ev.inc(ins)
                        e.wait_ge(ev.h, ev.n)
                        for hm in range(4):
                            ins = e.tensor_tensor(
                                s_hb[:, hbuf * 2048 + hm * 512: hbuf * 2048 + hm * 512 + 512],
                                s_hf[:, hm * 512: hm * 512 + 512],
                                s_ha[:, hbuf * 2048 + hm * 512: hbuf * 2048 + hm * 512 + 512],
                                ALU.subtract)
                            ev.inc(ins)
                    block.vector(_)
                    hsp_tok = ev.tok()
                    hf_free[0] = hsp_tok

                    for om in range(EC):
                        gb = B[6 + om % 2]

                        def _(e, om=om, gb=gb, ht=hsp_tok, buf=buf, hbuf=hbuf):
                            W(e, gb.free)
                            if om == 0:
                                W(e, ht)
                            first = True
                            for hc in range(4):
                                woff = buf * 4096 + hc * 1024 + om * 128
                                hoff = hbuf * 2048 + hc * 512
                                for (w_, h_) in ((s_w2a, s_ha), (s_w2b, s_ha),
                                                 (s_w2a, s_hb)):
                                    ins = e.matmul(
                                        gb.ps[:], w_[:, woff: woff + 128],
                                        h_[:, hoff: hoff + 512],
                                        start=first,
                                        stop=(hc == 3 and h_ is s_hb))
                                    first = False
                            gb.ready = pe.inc(ins)
                        block.tensor(_)

                        tslot = om % 2

                        def _(e, om=om, gb=gb, nt2=nt2, tslot=tslot, eix=eix,
                              mbts=mb_toks[eix]):
                            W(e, gb.ready)
                            if om == 0:
                                W(e, mbts[nt2])
                            W(e, tf_free[tslot])
                            ins = e.tensor_tensor(
                                s_tf[:, tslot * 512: tslot * 512 + 512],
                                gb.ps[:], s_mb[:, nt2 * 512: nt2 * 512 + 512],
                                ALU.mult)
                            gb.free = ev.inc(ins)
                        block.vector(_)
                        t_tok = ev.tok()

                        def _(e, om=om, nt2=nt2, tslot=tslot, t_tok=t_tok):
                            W(e, t_tok)
                            ins = e.tensor_add(
                                s_x2[:, om * 1024 + nt2 * 512: om * 1024 + nt2 * 512 + 512],
                                s_x2[:, om * 1024 + nt2 * 512: om * 1024 + nt2 * 512 + 512],
                                s_tf[:, tslot * 512: tslot * 512 + 512])
                            gp.inc(ins)
                        block.gpsimd(_)
                        tf_free[tslot] = gp.tok()
                    h_free[hbuf] = pe.tok()
                slabbuf_free[buf] = pe.tok()
                mb_free[0] = ev.tok()

            folds_tok = gp.tok()

            def _(e, folds_tok=folds_tok):
                W(e, folds_tok)
                for ofc in range(EC):
                    for nt2 in range(2):
                        ins = e.tensor_tensor(
                            s_x2[:, ofc * 1024 + nt2 * 512: ofc * 1024 + nt2 * 512 + 512],
                            s_x2[:, ofc * 1024 + nt2 * 512: ofc * 1024 + nt2 * 512 + 512],
                            s_x1a[:, ofc * 1024 + nt2 * 512: ofc * 1024 + nt2 * 512 + 512],
                            ALU.add)
                        ev.inc(ins)
                e.wait_ge(ev.h, ev.n)
                for ofc in range(EC):
                    for nt2 in range(2):
                        ins = e.tensor_tensor(
                            s_x2[:, ofc * 1024 + nt2 * 512: ofc * 1024 + nt2 * 512 + 512],
                            s_x2[:, ofc * 1024 + nt2 * 512: ofc * 1024 + nt2 * 512 + 512],
                            s_x1b[:, ofc * 1024 + nt2 * 512: ofc * 1024 + nt2 * 512 + 512],
                            ALU.add)
                        ev.inc(ins)
            block.vector(_)
            x2pre_tok = ev.tok()

            for nt2 in range(2):
                src = lambda ofc, nt2=nt2: s_x2[:, ofc * 1024 + nt2 * 512:
                                                ofc * 1024 + nt2 * 512 + 512]
                tmp = lambda ofc, nt2=nt2: s_wst[:, ofc * 512: ofc * 512 + 512]
                ln_tok = emit_ln(block, src, tmp, src, s_sqC, s_rowC, x2pre_tok)

                for tt4 in range(4):
                    jobs = []
                    for ofc in range(EC):
                        srcT = s_x2[:, ofc * 1024 + nt2 * 512 + tt4 * 128:
                                    ofc * 1024 + nt2 * 512 + tt4 * 128 + 128]
                        dstT = s_ot[:, ofc * 128: ofc * 128 + 128]
                        jobs.append((srcT, dstT, s_id[:]))
                    pre = (ln_tok,) if tt4 == 0 else ()
                    ot_tok = emit_transposes(block, jobs, [B[4], B[5]],
                                             pre_toks=pre,
                                             evac_pre_toks=(out_free[0],))

                    def _(e, half=half, nt2=nt2, tt4=tt4, ot_tok=ot_tok):
                        W(e, ot_tok)
                        row = half * 1024 + nt2 * 512 + tt4 * 128
                        out_free[0] = dma(e, sw, out[row:row + 128, :], s_ot[:])
                    block.gpsimd(_)

            x1h_free[0] = (pe.tok(), ev.tok(), ex.tok())
            slabbuf_free[0] = [pe.tok(), ev.tok(), ex.tok()]
            slabbuf_free[1] = [pe.tok(), ev.tok(), ex.tok()]
            half_fence[0] = (pe.tok(), ev.tok(), ex.tok(), gp.tok())

        def _(e):
            dma_drain(e, sw)
        block.gpsimd(_)

    close_scope(phC)
    for b in reversed(B):
        b.cm.__exit__(None, None, None)
    close_scope(live)
    return nc


def _make_consts():
    identf = np.eye(128, dtype=np.float32)
    identbf = np.eye(128, dtype=np.float32).astype(ml_dtypes.bfloat16)
    onescol = np.ones((128, 1), dtype=np.float32)
    onesrow = np.ones((1, 128), dtype=np.float32)
    cvec = np.zeros((128, 4), dtype=np.float32)
    cvec[:, 0] = EPS
    selm = np.zeros((EXP, EXP * 128), dtype=np.float32)
    for e_ in range(EXP):
        selm[e_, e_ * 128:(e_ + 1) * 128] = 1.0
    return identf, identbf, onescol, onesrow, cvec, selm


def _make_mask(par):
    i = np.arange(QS)[:, None]
    j = np.arange(TAIL)[None, :]
    m = np.where(j > par * 512 + i, NEG, 0.0).astype(np.float32)
    return np.ascontiguousarray(m.astype(ml_dtypes.bfloat16))


def _make_maskT(par):
    j = np.arange(TAIL)[:, None]
    i = np.arange(QS)[None, :]
    m = np.where(j > par * 512 + i, NEG, 0.0).astype(np.float32)
    return np.ascontiguousarray(m.astype(ml_dtypes.bfloat16))


def kernel(L0_attn_x, L0_attn_in_proj_w, L0_attn_in_proj_b, L0_attn_out_proj_w,
           L0_attn_out_proj_b, L0_ffn_Wg, L0_ffn_bg, L0_ffn_W1, L0_ffn_b1,
           L0_ffn_W2, L0_ffn_b2):
    x = np.ascontiguousarray(np.asarray(L0_attn_x, dtype=np.float32))
    Bb = x.shape[0]
    identf, identbf, onescol, onesrow, cvec, selm = _make_consts()
    weights = dict(
        ipw=np.ascontiguousarray(np.asarray(L0_attn_in_proj_w, np.float32)),
        ipb=np.ascontiguousarray(np.asarray(L0_attn_in_proj_b, np.float32)),
        opw=np.ascontiguousarray(np.asarray(L0_attn_out_proj_w, np.float32)),
        opb=np.ascontiguousarray(np.asarray(L0_attn_out_proj_b, np.float32)),
        wgw=np.ascontiguousarray(np.asarray(L0_ffn_Wg, np.float32)),
        bgv=np.ascontiguousarray(np.asarray(L0_ffn_bg, np.float32)),
        w1=np.ascontiguousarray(np.asarray(L0_ffn_W1, np.float32)),
        b1=np.ascontiguousarray(np.asarray(L0_ffn_b1, np.float32)),
        w2=np.ascontiguousarray(np.asarray(L0_ffn_W2, np.float32)),
        b2=np.ascontiguousarray(np.asarray(L0_ffn_b2, np.float32)),
        ident=identf, identb=identbf, onescol=onescol, onesrow=onesrow,
        cvec=cvec, selm=selm,
    )
    masks = [_make_mask(0), _make_mask(1)]
    masksT = [_make_maskT(0), _make_maskT(1)]
    in_maps = []
    for c in range(8):
        b, par = c // 2, c % 2
        blocks = [2 * k + par for k in range(NSLOT)]
        xo = np.ascontiguousarray(
            np.concatenate([x[b, a * 512:(a + 1) * 512] for a in blocks], axis=0))
        m = dict(weights)
        m["x_full"] = np.ascontiguousarray(x[b])
        m["x_own"] = xo
        m["mask_in"] = masks[par]
        m["maskT_in"] = masksT[par]
        in_maps.append(m)

    nc = build_program()
    res = run_bass_kernel_spmd(nc, in_maps, core_ids=list(range(8)))

    y = np.empty((Bb, S, E), dtype=np.float32)
    for c in range(8):
        b, par = c // 2, c % 2
        oc = np.asarray(res.results[c]["out"])
        for k in range(NSLOT):
            a = 2 * k + par
            y[b, a * 512:(a + 1) * 512] = oc[k * 512:(k + 1) * 512]
    return y


# version 17 (final)
# speedup vs baseline: 2.9803x; 1.9682x over previous
import numpy as np
import ml_dtypes
import concourse.bass as bass
from concourse import library_config
import concourse.mybir as mybir
from concourse.bass_utils import run_bass_kernel_spmd

F32 = mybir.dt.float32
BF16 = mybir.dt.bfloat16
U32 = mybir.dt.uint32
IOA = bass.IndirectOffsetOnAxis
AF = mybir.ActivationFunctionType
ALU = mybir.AluOpType
AXX = mybir.AxisListType.X

NEG = -1e30
EPS = 1e-5

S, E, H, HID, EXP = 4096, 1024, 16, 4096, 8
D = E // H
OWN = S // 2
QS = S // 8
EC = E // 128
HC = HID // 128
TAIL = S // 4
NSLOT = 4
NQT = QS // 128
SC = S // 128
ISQ = 1.0 / float(D) ** 0.5


def ceil_div(a, b):
    return (a + b - 1) // b


class Sem:
    __slots__ = ("h", "n")

    def __init__(self, nc, name):
        self.h = nc.alloc_semaphore(name=name)
        self.n = 0

    def inc(self, instr, k=1):
        instr.then_inc(self.h, k)
        self.n += k
        return (self, self.n)

    def tok(self):
        return (self, self.n)


def W(e, tk):
    if tk is None:
        return
    if isinstance(tk, (list, tuple)) and tk and isinstance(tk[0], (list, tuple)):
        for t in tk:
            W(e, t)
        return
    if tk[1] > 0:
        h = tk[0].h if hasattr(tk[0], "h") else tk[0]
        e.wait_ge(h, tk[1])


class DmaPool:
    __slots__ = ("chans", "i")

    def __init__(self, nc, name, n=8):
        self.chans = [Sem(nc, "%s%d" % (name, i)) for i in range(n)]
        self.i = 0


class Bank:
    __slots__ = ("cm", "ps", "ready", "free")

    def __init__(self, nc, i):
        self.cm = nc.psum_tensor("pbank%d" % i, [128, 512], F32)
        self.ps = self.cm.__enter__()
        self.ready = None
        self.free = None


def build_program():
    nc = bass.Bass("TRN2", target_bir_lowering=False)

    x_full = nc.dram_tensor("x_full", [S, E], F32, kind="ExternalInput").ap()
    x_own = nc.dram_tensor("x_own", [OWN, E], F32, kind="ExternalInput").ap()
    mask_in = nc.dram_tensor("mask_in", [QS, TAIL], BF16, kind="ExternalInput").ap()
    maskT_in = nc.dram_tensor("maskT_in", [TAIL, QS], BF16, kind="ExternalInput").ap()
    ipw = nc.dram_tensor("ipw", [3 * E, E], F32, kind="ExternalInput").ap()
    ipb = nc.dram_tensor("ipb", [3 * E], F32, kind="ExternalInput").ap()
    opw = nc.dram_tensor("opw", [E, E], F32, kind="ExternalInput").ap()
    opb = nc.dram_tensor("opb", [E], F32, kind="ExternalInput").ap()
    wgw = nc.dram_tensor("wgw", [E, EXP], F32, kind="ExternalInput").ap()
    bgv = nc.dram_tensor("bgv", [EXP], F32, kind="ExternalInput").ap()
    w1 = nc.dram_tensor("w1", [EXP, E, HID], F32, kind="ExternalInput").ap()
    b1 = nc.dram_tensor("b1", [EXP, HID], F32, kind="ExternalInput").ap()
    w2 = nc.dram_tensor("w2", [EXP, HID, E], F32, kind="ExternalInput").ap()
    b2 = nc.dram_tensor("b2", [EXP, E], F32, kind="ExternalInput").ap()
    ident = nc.dram_tensor("ident", [128, 128], F32, kind="ExternalInput").ap()
    identb = nc.dram_tensor("identb", [128, 128], BF16, kind="ExternalInput").ap()
    onescol = nc.dram_tensor("onescol", [128, 1], F32, kind="ExternalInput").ap()
    onesrow = nc.dram_tensor("onesrow", [1, 128], F32, kind="ExternalInput").ap()
    cvec = nc.dram_tensor("cvec", [128, 4], F32, kind="ExternalInput").ap()
    selm = nc.dram_tensor("selm", [EXP, EXP * 128], F32, kind="ExternalInput").ap()
    out = nc.dram_tensor("out", [OWN, E], F32, kind="ExternalOutput").ap()

    Ks = nc.dram_tensor("Ks", [E, S], F32, kind="Internal").ap()
    Vs = nc.dram_tensor("Vs", [H, S, D], F32, kind="Internal").ap()
    Qs = nc.dram_tensor("Qs", [E, OWN], F32, kind="Internal").ap()
    X1s = nc.dram_tensor("X1s", [E, OWN], F32, kind="Internal").ap()
    X1tok = nc.dram_tensor("X1tok", [2560, E], F32, kind="Internal").ap()
    Ytok = nc.dram_tensor("Ytok", [2560, E], F32, kind="Internal").ap()
    IdxH = nc.dram_tensor("IdxH", [EXP * 2560, 4], U32, kind="Internal").ap()
    u128 = nc.dram_tensor("u128", [128, 128], BF16, kind="ExternalInput").ap()
    tid4 = nc.dram_tensor("tid4", [128, 64], U32, kind="ExternalInput").ap()
    dmpc = nc.dram_tensor("dmpc", [128, 16], F32, kind="ExternalInput").ap()
    pflc = nc.dram_tensor("pflc", [128, 16], U32, kind="ExternalInput").ap()
    pfic = nc.dram_tensor("pfic", [128, 4], U32, kind="ExternalInput").ap()

    live = []

    def SBuf(scope, name, shape, dt=F32):
        cm = nc.sbuf_tensor(name, shape, dt)
        t = cm.__enter__()
        scope.append(cm)
        return t

    def close_scope(scope):
        for cm in reversed(scope):
            cm.__exit__(None, None, None)
        scope.clear()

    s_id = SBuf(live, "s_id", [128, 128])
    s_idb = SBuf(live, "s_idb", [128, 128], BF16)
    s_1c = SBuf(live, "s_1c", [128, 1])
    s_1r = SBuf(live, "s_1r", [1, 128])
    s_cv = SBuf(live, "s_cv", [128, 4])
    s_bqkv = SBuf(live, "s_bqkv", [128, 24])
    s_bo = SBuf(live, "s_bo", [128, EC])
    s_bg = SBuf(live, "s_bg", [EXP, 1])
    s_b1 = SBuf(live, "s_b1", [128, EXP * HC])
    s_b2 = SBuf(live, "s_b2", [EXP, E])
    s_wg = SBuf(live, "s_wg", [128, EC * EXP])
    s_gl = SBuf(live, "s_gl", [EXP, OWN])
    s_sel = SBuf(live, "s_sel", [EXP, EXP * 128])
    nc.const_aps.register(F32, EPS, s_cv[:, 0:1])
    nc.const_aps.register(F32, 0.0, s_cv[:, 1:2])

    B = [Bank(nc, i) for i in range(8)]
    pe = Sem(nc, "pe")
    ev = Sem(nc, "ev")
    ex = Sem(nc, "ex")
    gp = Sem(nc, "gp")
    ld = DmaPool(nc, "ld", 8)
    sw = DmaPool(nc, "sw", 8)

    def dma(e, pool, dst, src, slow=False):
        ch = pool.chans[pool.i % len(pool.chans)]
        pool.i += 1
        if ch.n:
            e.wait_ge(ch.h, ch.n)
        if slow:
            ins = e.dma_start(dst, src, allow_slow_non_contiguous=True)
        else:
            ins = e.dma_start(dst, src)
        ins.then_inc(ch.h, 16)
        ch.n += 16
        return (ch, ch.n)

    def dma_drain(e, pool):
        for ch in pool.chans:
            if ch.n:
                e.wait_ge(ch.h, ch.n)

    def emit_transposes(block, jobs, banks, evac="vector", pre_toks=(),
                        evac_pre_toks=()):
        nb = len(jobs)
        nbatch = ceil_div(nb, 4)
        sem = ev if evac == "vector" else ex

        for bi in range(nbatch):
            bank = banks[bi % 2]

            def _(e, bi=bi, bank=bank):
                if bi == 0:
                    for tk in pre_toks:
                        W(e, tk)
                W(e, bank.free)
                n4 = min(4, nb - bi * 4)
                for s in range(n4):
                    src, _d, idap = jobs[bi * 4 + s]
                    op_, ow = src.shape[1], src.shape[0]
                    ins = e.transpose(bank.ps[0:op_, s * 128:s * 128 + ow], src, idap)
                bank.ready = pe.inc(ins)
            block.tensor(_)

            def _(e, bi=bi, bank=bank):
                if bi == 0:
                    for tk in evac_pre_toks:
                        W(e, tk)
                W(e, bank.ready)
                n4 = min(4, nb - bi * 4)
                for s in range(n4):
                    src, dst, _i = jobs[bi * 4 + s]
                    op_, ow = src.shape[1], src.shape[0]
                    if evac == "vector":
                        ins = e.tensor_copy(dst, bank.ps[0:op_, s * 128:s * 128 + ow])
                    else:
                        ins = e.copy(dst, bank.ps[0:op_, s * 128:s * 128 + ow])
                bank.free = sem.inc(ins)
            if evac == "vector":
                block.vector(_)
            else:
                block.scalar(_)
        return sem.tok()

    phA = []
    s_wkvT = SBuf(phA, "s_wkvT", [128, EC * 2 * E])
    s_wqT = SBuf(phA, "s_wqT", [128, EC * E])
    s_wtmp = SBuf(phA, "s_wtmp", [128, 2 * E])
    s_xtm = SBuf(phA, "s_xtm", [128, 2 * 4 * E])
    s_xfm = SBuf(phA, "s_xfm", [128, 2 * EC * 512])
    s_kvq = SBuf(phA, "s_kvq", [128, 4 * 512])
    s_vout = SBuf(phA, "s_vout", [128, 2 * 512])

    with nc.named_scope("phaseA"), nc.Block() as block:
        def _(e):
            dma(e, ld, s_id[:], ident)
            dma(e, ld, s_idb[:], identb)
            dma(e, ld, s_1c[:], onescol)
            dma(e, ld, s_1r[:], onesrow)
            dma(e, ld, s_cv[:], cvec)
            dma(e, ld, s_bqkv[:], ipb.rearrange("(c p) -> p c", p=128), slow=True)
            dma(e, ld, s_bo[:], opb.rearrange("(c p) -> p c", p=128), slow=True)
            dma(e, ld, s_bg[:], bgv.rearrange("(e o) -> e o", o=1), slow=True)
            dma(e, ld, s_b1[:].rearrange("p (ee c) -> p ee c", c=HC),
                b1.rearrange("ee (c p) -> p ee c", p=128), slow=True)
            dma(e, ld, s_b2[:], b2)
            dma(e, ld, s_sel[:], selm)
            dma(e, ld, s_wg[:].rearrange("p (c e) -> p c e", e=EXP),
                wgw.rearrange("(c p) e -> p c e", p=128), slow=True)
        block.sync(_)

        strip_tok = [None] * 24
        strip_free = [None, None]

        def _(e):
            for r in range(2):
                strip_tok[r] = dma(e, ld, s_wtmp[:, r * E:(r + 1) * E],
                                   ipw[r * 128:(r + 1) * 128, :])
        block.sync(_)

        for r in range(24):
            jobs = []
            for ifc in range(EC):
                src = s_wtmp[:, (r % 2) * E + ifc * 128:(r % 2) * E + ifc * 128 + 128]
                if r < 8:
                    dst = s_wqT[:, ifc * E + r * 128: ifc * E + r * 128 + 128]
                else:
                    dst = s_wkvT[:, ifc * 2 * E + (r - 8) * 128:
                                 ifc * 2 * E + (r - 8) * 128 + 128]
                jobs.append((src, dst, s_id[:]))
            emit_transposes(block, jobs, [B[0], B[1]], pre_toks=(strip_tok[r],))
            strip_free[r % 2] = pe.tok()

            if r + 2 < 24:
                def _(e, r=r):
                    W(e, strip_free[r % 2])
                    strip_tok[r + 2] = dma(
                        e, ld, s_wtmp[:, (r % 2) * E:(r % 2) * E + E],
                        ipw[(r + 2) * 128:(r + 3) * 128, :])
                block.sync(_)
        wT_tok = ev.tok()

        kvq_free = [None] * 4
        kvq_i = [0]
        vout_free = [None, None]
        vb_i = [0]
        xtm_free = [None, None]
        xfm_free = [None, None]

        def emit_x_block(src_rows, parity):
            tok_holder = []

            def _(e, parity=parity):
                W(e, xtm_free[parity])
                tok_holder.append(dma(
                    e, ld,
                    s_xtm[:, parity * 4 * E: parity * 4 * E + 4 * E].rearrange(
                        "p (t d) -> p t d", d=E),
                    src_rows.rearrange("(t p) d -> p t d", p=128)))
            block.sync(_)

            jobs = []
            for j in range(4 * EC):
                t, ifc = divmod(j, EC)
                src = s_xtm[:, parity * 4 * E + t * E + ifc * 128:
                            parity * 4 * E + t * E + ifc * 128 + 128]
                dst = s_xfm[:, parity * EC * 512 + ifc * 512 + t * 128:
                            parity * EC * 512 + ifc * 512 + t * 128 + 128]
                jobs.append((src, dst, s_id[:]))
            emit_transposes(block, jobs, [B[2], B[3]],
                            pre_toks=(tok_holder[0], xfm_free[parity]))
            xtm_free[parity] = pe.tok()

        def emit_proj(parity, ofm_list, blk):
            for ofm in ofm_list:
                gb = B[4 + (ofm % 2)]
                is_q = ofm < 8

                def _(e, ofm=ofm, gb=gb, is_q=is_q, parity=parity):
                    W(e, gb.free)
                    W(e, wT_tok)
                    W(e, ev.tok())
                    for ifc in range(EC):
                        if is_q:
                            lhs = s_wqT[:, ifc * E + ofm * 128: ifc * E + ofm * 128 + 128]
                        else:
                            lhs = s_wkvT[:, ifc * 2 * E + (ofm - 8) * 128:
                                         ifc * 2 * E + (ofm - 8) * 128 + 128]
                        rhs = s_xfm[:, parity * EC * 512 + ifc * 512:
                                    parity * EC * 512 + ifc * 512 + 512]
                        ins = e.matmul(gb.ps[:], lhs, rhs,
                                       start=(ifc == 0), stop=(ifc == EC - 1))
                    gb.ready = pe.inc(ins)
                block.tensor(_)

                slot = kvq_i[0] % 4
                kvq_i[0] += 1

                def _(e, ofm=ofm, gb=gb, slot=slot):
                    W(e, gb.ready)
                    W(e, kvq_free[slot])
                    ins = e.activation(s_kvq[:, slot * 512:slot * 512 + 512],
                                       gb.ps[:], AF.Identity,
                                       bias=s_bqkv[:, ofm:ofm + 1], scale=1.0)
                    gb.free = ex.inc(ins)
                block.scalar(_)
                evac_tok = ex.tok()

                if is_q or ofm < 16:
                    def _(e, ofm=ofm, slot=slot, evac_tok=evac_tok, blk=blk,
                          is_q=is_q):
                        W(e, evac_tok)
                        if is_q:
                            dst = Qs[ofm * 128:(ofm + 1) * 128,
                                     blk * 512:(blk + 1) * 512]
                        else:
                            dst = Ks[(ofm - 8) * 128:(ofm - 7) * 128,
                                     blk * 512:(blk + 1) * 512]
                        kvq_free[slot] = dma(e, sw, dst,
                                             s_kvq[:, slot * 512:slot * 512 + 512])
                    block.gpsimd(_)
                else:
                    vb = vb_i[0] % 2
                    vb_i[0] += 1
                    vjobs = []
                    for t in range(4):
                        src = s_kvq[:, slot * 512 + t * 128: slot * 512 + t * 128 + 128]
                        dst = s_vout[:, vb * 512 + t * 128: vb * 512 + t * 128 + 128]
                        vjobs.append((src, dst, s_id[:]))
                    vtok = emit_transposes(block, vjobs, [B[6], B[7]],
                                           pre_toks=(evac_tok,),
                                           evac_pre_toks=vout_free[vb] or ())
                    kvq_free[slot] = pe.tok()

                    def _(e, ofm=ofm, blk=blk, vb=vb, vtok=vtok):
                        W(e, vtok)
                        tks = []
                        for hh in range(2):
                            hidx = (ofm - 16) * 2 + hh
                            src = s_vout[:, vb * 512:(vb + 1) * 512].rearrange(
                                "p (t dd) -> p t dd", dd=128)[:, :, hh * 64:(hh + 1) * 64]
                            dst = Vs[hidx, blk * 512:(blk + 1) * 512, :].rearrange(
                                "(t p) d -> p t d", p=128)
                            tks.append(dma(e, sw, dst, src))
                        vout_free[vb] = tuple(tks)
                    block.gpsimd(_)

        for tb in range(8):
            emit_x_block(x_full[tb * 512:(tb + 1) * 512, :], tb % 2)
            emit_proj(tb % 2, list(range(8, 24)), tb)
            xfm_free[tb % 2] = ex.tok()
        for tq in range(4):
            emit_x_block(x_own[tq * 512:(tq + 1) * 512, :], tq % 2)
            emit_proj(tq % 2, list(range(0, 8)), tq)
            xfm_free[tq % 2] = ex.tok()

        def _(e):
            dma_drain(e, sw)
        block.gpsimd(_)

    close_scope(phA)

    phBB2 = []
    s_ctx = SBuf(phBB2, "s_ctx", [128, EC * OWN])

    phB = []
    s_K = SBuf(phB, "s_K", [128, S])
    s_Kh = SBuf(phB, "s_Kh", [128, S], BF16)
    s_Kl = SBuf(phB, "s_Kl", [128, S], BF16)
    s_V = SBuf(phB, "s_V", [128, 2 * SC * 65])
    s_Vh = SBuf(phB, "s_Vh", [128, 2 * SC * 65], BF16)
    s_Vl = SBuf(phB, "s_Vl", [128, 2 * SC * 65], BF16)
    s_Q = SBuf(phB, "s_Q", [128, OWN])
    s_Qh = SBuf(phB, "s_Qh", [128, OWN], BF16)
    s_Ql = SBuf(phB, "s_Ql", [128, OWN], BF16)
    s_maskT = SBuf(phB, "s_maskT", [128, (TAIL // 128) * QS], BF16)
    s_ptf = SBuf(phB, "s_ptf", [128, 8 * 512])
    s_pth = SBuf(phB, "s_pth", [128, 8 * 512], BF16)
    s_ptl = SBuf(phB, "s_ptl", [128, 8 * 512], BF16)
    s_rt = SBuf(phB, "s_rt", [1, 512])
    s_rb = SBuf(phB, "s_rb", [64, 512])

    with nc.named_scope("phaseB"), nc.Block() as block:
        fenceA = (pe.tok(), ev.tok(), ex.tok())
        mask_tok = [None]

        def _(e):
            t2 = dma(e, ld, s_maskT[:].rearrange("p (c q) -> p c q", q=QS),
                     maskT_in.rearrange("(c p) q -> p c q", p=128))
            mask_tok[0] = t2
        block.sync(_)

        split_toks = [None, None]
        head_end = [None, None]
        kvq_toks = [None, None]
        pt_free = [None] * 8
        ptf_free = [None] * 8
        rb_free = [None]
        ctx_banks = [B[4], B[5]]
        bc_bank = B[6]

        def emit_kvq_load(h):
            hb = h % 2

            def _(e, h=h, hb=hb):
                W(e, head_end[hb])
                if h < 2:
                    W(e, fenceA)
                t1 = dma(e, ld, s_K[hb * 64:(hb + 1) * 64, :], Ks[h * D:(h + 1) * D, :])
                vdst = s_V[:].rearrange("p (b c d) -> p b c d", b=2, d=65)[:, hb, :, 0:64]
                t2 = dma(e, ld, vdst, Vs[h].rearrange("(c p) d -> p c d", p=128))
                t3 = dma(e, ld, s_Q[hb * 64:(hb + 1) * 64, :],
                         Qs[h * D:(h + 1) * D, :])
                kvq_toks[hb] = (t1, t2, t3)
            block.sync(_)

            def _(e, hb=hb):
                W(e, kvq_toks[hb])
                vv = s_V[:].rearrange("p (b c d) -> p b c d", b=2, d=65)
                ins = e.tensor_scalar(vv[:, hb, :, 64:65], vv[:, hb, :, 63:64],
                                      0.0, 1.0, ALU.mult, ALU.add)
                ev.inc(ins)
                e.wait_ge(ev.h, ev.n)
                ks = s_K[hb * 64:(hb + 1) * 64, :]
                qs_ = s_Q[hb * 64:(hb + 1) * 64, :]
                vs_ = s_V[:, hb * SC * 65:(hb + 1) * SC * 65]
                ins = e.tensor_copy(s_Kh[hb * 64:(hb + 1) * 64, :], ks)
                ev.inc(ins)
                ins = e.tensor_copy(s_Qh[hb * 64:(hb + 1) * 64, :], qs_)
                ev.inc(ins)
                ins = e.tensor_copy(s_Vh[:, hb * SC * 65:(hb + 1) * SC * 65], vs_)
                ev.inc(ins)
                e.wait_ge(ev.h, ev.n)
                ins = e.tensor_tensor(s_Kl[hb * 64:(hb + 1) * 64, :], ks,
                                      s_Kh[hb * 64:(hb + 1) * 64, :], ALU.subtract)
                ev.inc(ins)
                ins = e.tensor_tensor(s_Ql[hb * 64:(hb + 1) * 64, :], qs_,
                                      s_Qh[hb * 64:(hb + 1) * 64, :], ALU.subtract)
                ev.inc(ins)
                ins = e.tensor_tensor(s_Vl[:, hb * SC * 65:(hb + 1) * SC * 65], vs_,
                                      s_Vh[:, hb * SC * 65:(hb + 1) * SC * 65],
                                      ALU.subtract)
                split_toks[hb] = ev.inc(ins)
            block.vector(_)

        def emit_slot(h, k):
            ek = (k + 1) * TAIL
            nkc128 = ek // 128
            hb = h % 2
            cbank = ctx_banks[k % 2]
            npairs = nkc128 // 2
            pv_jobs = []
            for p in range(npairs):
                kcs = [2 * p, 2 * p + 1]
                banks = [B[kc % 4] for kc in kcs]

                def _(e, kcs=kcs, banks=banks, hb=hb, k=k, h=h, nkc128=nkc128):
                    for b_ in banks:
                        W(e, b_.free)
                    if kcs[0] == 0 and k == 0:
                        W(e, split_toks[hb])
                        if h == 0:
                            W(e, mask_tok[0])
                    for kc, b_ in zip(kcs, banks):
                        qh = s_Qh[hb * 64:(hb + 1) * 64, k * QS: k * QS + 512]
                        ql = s_Ql[hb * 64:(hb + 1) * 64, k * QS: k * QS + 512]
                        kh = s_Kh[hb * 64:(hb + 1) * 64, kc * 128: kc * 128 + 128]
                        kl = s_Kl[hb * 64:(hb + 1) * 64, kc * 128: kc * 128 + 128]
                        tailc = kc >= nkc128 - 8
                        e.matmul(b_.ps[:], kh, qh, start=True, stop=False)
                        e.matmul(b_.ps[:], kh, ql, start=False, stop=False)
                        ins = e.matmul(b_.ps[:], kl, qh, start=False,
                                       stop=not tailc)
                        if tailc:
                            tc = kc - (nkc128 - 8)
                            ins = e.matmul(b_.ps[:], s_idb[:],
                                           s_maskT[:, tc * QS: tc * QS + 512],
                                           start=False, stop=True)
                        b_.ready = pe.inc(ins)
                block.tensor(_)

                for kc, b_ in zip(kcs, banks):
                    ps_ = kc % 8

                    def _(e, kc=kc, b_=b_, ps_=ps_):
                        W(e, b_.ready)
                        W(e, ptf_free[ps_])
                        ins = e.activation(s_ptf[:, ps_ * 512: ps_ * 512 + 512],
                                           b_.ps[:], AF.Exp, bias=0.0, scale=ISQ)
                        b_.free = ex.inc(ins)
                    block.scalar(_)
                    exp_tok = ex.tok()

                    def _(e, ps_=ps_, exp_tok=exp_tok):
                        W(e, exp_tok)
                        W(e, pt_free[ps_])
                        pf = s_ptf[:, ps_ * 512: ps_ * 512 + 512]
                        ph = s_pth[:, ps_ * 512: ps_ * 512 + 512]
                        ins = e.tensor_copy(ph, pf)
                        ev.inc(ins)
                        e.wait_ge(ev.h, ev.n)
                        ins = e.tensor_tensor(s_ptl[:, ps_ * 512: ps_ * 512 + 512],
                                              pf, ph, ALU.subtract)
                        ev.inc(ins)
                    block.vector(_)
                    ptf_free[ps_] = ev.tok()

                split_pt = ev.tok()

                def pv_(e, kcs=kcs, hb=hb, split_pt=split_pt, cbank=cbank,
                        nkc128=nkc128, h=h, k=k):
                    W(e, split_pt)
                    if kcs[0] == 0:
                        W(e, cbank.free)
                    for kc in kcs:
                        ps_ = kc % 8
                        vh = s_Vh[:, hb * SC * 65 + kc * 65: hb * SC * 65 + kc * 65 + 65]
                        vl = s_Vl[:, hb * SC * 65 + kc * 65: hb * SC * 65 + kc * 65 + 65]
                        ph = s_pth[:, ps_ * 512: ps_ * 512 + 512]
                        pl = s_ptl[:, ps_ * 512: ps_ * 512 + 512]
                        last = kc == nkc128 - 1
                        e.matmul(cbank.ps[0:65, 0:512], vh, ph,
                                 start=(kc == 0), stop=False)
                        e.matmul(cbank.ps[0:65, 0:512], vh, pl,
                                 start=False, stop=False)
                        ins = e.matmul(cbank.ps[0:65, 0:512], vl, ph,
                                       start=False, stop=last)
                        t = pe.inc(ins)
                        pt_free[ps_] = pe.tok()
                        if last:
                            cbank.ready = t
                pv_jobs.append(pv_)
                if p >= 2:
                    block.tensor(pv_jobs[p - 2])
            block.tensor(pv_jobs[npairs - 2])
            block.tensor(pv_jobs[npairs - 1])

            def _(e, cbank=cbank):
                W(e, cbank.ready)
                ins = e.tensor_copy(s_rt[0:1, :], cbank.ps[64:65, 0:512])
                ev.inc(ins)
                e.wait_ge(ev.h, ev.n)
                ins = e.reciprocal(s_rt[0:1, :], s_rt[0:1, :])
                ev.inc(ins)
            block.vector(_)
            rt_tok = ev.tok()

            def _(e, rt_tok=rt_tok):
                W(e, rt_tok)
                W(e, bc_bank.free)
                ins = e.matmul(bc_bank.ps[0:64, 0:512], s_1r[0:1, 0:64],
                               s_rt[0:1, :], start=True, stop=True)
                bc_bank.ready = pe.inc(ins)
            block.tensor(_)

            def _(e):
                W(e, bc_bank.ready)
                W(e, rb_free[0])
                ins = e.tensor_copy(s_rb[:], bc_bank.ps[0:64, 0:512])
                bc_bank.free = ev.inc(ins)
            block.vector(_)
            rb_tok = ev.tok()

            def _(e, h=h, k=k, cbank=cbank, rb_tok=rb_tok):
                W(e, cbank.ready)
                W(e, rb_tok)
                ins = e.tensor_tensor(
                    s_ctx[(h % 2) * 64:(h % 2) * 64 + 64,
                          (h // 2) * OWN + k * QS:(h // 2) * OWN + k * QS + QS],
                    cbank.ps[0:64, 0:QS], s_rb[:], ALU.mult)
                cbank.free = ev.inc(ins)
            block.vector(_)
            rb_free[0] = ev.tok()

        for h in range(H):
            emit_kvq_load(h)
            for k in range(NSLOT):
                emit_slot(h, k)
            head_end[h % 2] = pe.tok()

    close_scope(phB)

    def emit_ln(block, src, tmp, dst, s_sq, s_row, pre_tok):
        bsum, bsq, bbc1, bbc2 = B[0], B[1], B[2], B[3]
        sq_toks = []
        sqmm_toks = []

        for ofc in range(EC):
            def _(e, ofc=ofc):
                if ofc == 0:
                    W(e, pre_tok)
                if ofc >= 2:
                    W(e, sqmm_toks[ofc - 2])
                ins = e.activation(s_sq[:, (ofc % 2) * 512:(ofc % 2) * 512 + 512],
                                   src(ofc), AF.Square, bias=0.0, scale=1.0)
                sq_toks.append(ex.inc(ins))
            block.scalar(_)

            def _(e, ofc=ofc):
                if ofc == 0:
                    W(e, pre_tok)
                    W(e, bsum.free)
                    W(e, bsq.free)
                ins = e.matmul(bsum.ps[0:1, 0:512], s_1c[:], src(ofc),
                               start=(ofc == 0), stop=(ofc == EC - 1))
                if ofc == EC - 1:
                    bsum.ready = pe.inc(ins)
                W(e, sq_toks[ofc])
                ins = e.matmul(bsq.ps[0:1, 0:512], s_1c[:],
                               s_sq[:, (ofc % 2) * 512:(ofc % 2) * 512 + 512],
                               start=(ofc == 0), stop=(ofc == EC - 1))
                sqmm_toks.append(pe.inc(ins))
            block.tensor(_)
        bsq.ready = sqmm_toks[-1]

        r = lambda i: s_row[0:1, i * 512:(i + 1) * 512]

        def _(e):
            W(e, bsum.ready)
            ins = e.tensor_copy(r(0), bsum.ps[0:1, 0:512])
            bsum.free = ev.inc(ins)
            W(e, bsq.ready)
            ins = e.tensor_copy(r(1), bsq.ps[0:1, 0:512])
            bsq.free = ev.inc(ins)
            e.wait_ge(ev.h, ev.n)
            ins = e.tensor_scalar(r(2), r(0), 1.0 / E, None, ALU.mult)
            ev.inc(ins)
            ins = e.tensor_scalar(r(3), r(1), 1.0 / E, None, ALU.mult)
            ev.inc(ins)
            e.wait_ge(ev.h, ev.n)
            ins = e.tensor_tensor(r(4), r(2), r(2), ALU.mult)
            ev.inc(ins)
            e.wait_ge(ev.h, ev.n)
            ins = e.tensor_tensor(r(4), r(3), r(4), ALU.subtract)
            ev.inc(ins)
        block.vector(_)
        var_tok = ev.tok()

        def _(e, var_tok=var_tok):
            W(e, var_tok)
            ins = e.activation(r(3), r(4), AF.Sqrt, bias=EPS, scale=1.0)
            ex.inc(ins)
        block.scalar(_)
        sd_tok = ex.tok()

        def _(e, sd_tok=sd_tok):
            W(e, sd_tok)
            ins = e.reciprocal(r(5), r(3))
            ev.inc(ins)
            e.wait_ge(ev.h, ev.n)
            ins = e.tensor_tensor(r(4), r(2), r(5), ALU.mult)
            ev.inc(ins)
        block.vector(_)
        nmr_tok = ev.tok()

        def _(e, nmr_tok=nmr_tok):
            W(e, nmr_tok)
            W(e, bbc1.free)
            ins = e.matmul(bbc1.ps[:, 0:512], s_1r[:], r(5), start=True, stop=True)
            bbc1.ready = pe.inc(ins)
            W(e, bbc2.free)
            ins = e.matmul(bbc2.ps[:, 0:512], s_1r[:], r(4), start=True, stop=True)
            bbc2.ready = pe.inc(ins)
        block.tensor(_)

        def _(e):
            W(e, bbc1.ready)
            W(e, bbc2.ready)
            for ofc in range(EC):
                ins = e.tensor_tensor(tmp(ofc), src(ofc), bbc1.ps[:, 0:512], ALU.mult)
                if ofc == EC - 1:
                    bbc1.free = ev.inc(ins)
            e.wait_ge(ev.h, ev.n)
            for ofc in range(EC):
                ins = e.tensor_tensor(dst(ofc), tmp(ofc), bbc2.ps[:, 0:512], ALU.subtract)
                if ofc == EC - 1:
                    bbc2.free = ev.inc(ins)
        block.vector(_)
        return ev.tok()

    phB2 = []
    s_woT = SBuf(phB2, "s_woT", [128, EC * E])
    s_xo = SBuf(phB2, "s_xo", [128, EC * 512])
    s_xtm2 = SBuf(phB2, "s_xtm2", [128, 4 * E])
    s_t1 = SBuf(phB2, "s_t1", [128, EC * 512])
    s_sqA = SBuf(phB2, "s_sqA", [128, 2 * 512])
    s_rowA = SBuf(phB2, "s_rowA", [1, 6 * 512])
    s_xtk = SBuf(phB2, "s_xtk", [128, 2 * E])

    with nc.named_scope("phaseB2"), nc.Block() as block:
        xtm2_free = [None]
        for rr in range(8):
            tok_holder = []

            def _(e, rr=rr):
                W(e, xtm2_free[0])
                tok_holder.append(dma(e, ld, s_xtm2[:, 0:E],
                                      opw[rr * 128:(rr + 1) * 128, :]))
            block.sync(_)
            jobs = []
            for cfc in range(EC):
                src = s_xtm2[:, cfc * 128:cfc * 128 + 128]
                dst = s_woT[:, cfc * E + rr * 128: cfc * E + rr * 128 + 128]
                jobs.append((src, dst, s_id[:]))
            emit_transposes(block, jobs, [B[0], B[1]], pre_toks=(tok_holder[0],))
            xtm2_free[0] = pe.tok()
        woT_tok = ev.tok()

        xtk_free = [None, None]
        x1dump_tok = [None]
        for nt in range(4):
            tok_holder = []
            xo_prev = ev.tok()

            def _(e, nt=nt):
                W(e, xtm2_free[0])
                tok_holder.append(dma(
                    e, ld,
                    s_xtm2[:].rearrange("p (t d) -> p t d", d=E),
                    x_own[nt * 512:(nt + 1) * 512, :].rearrange("(t p) d -> p t d", p=128)))
            block.sync(_)
            jobs = []
            for j in range(4 * EC):
                t, ifc = divmod(j, EC)
                srcj = s_xtm2[:, t * E + ifc * 128: t * E + ifc * 128 + 128]
                dstj = s_xo[:, ifc * 512 + t * 128: ifc * 512 + t * 128 + 128]
                jobs.append((srcj, dstj, s_id[:]))
            emit_transposes(block, jobs, [B[2], B[3]],
                            pre_toks=(tok_holder[0],), evac_pre_toks=(xo_prev,))
            xtm2_free[0] = pe.tok()
            xo_tok = ev.tok()

            evac_toks = []
            for ofm in range(EC):
                gb = B[4 + ofm % 2]

                def _(e, ofm=ofm, gb=gb, nt=nt):
                    W(e, gb.free)
                    if nt == 0 and ofm == 0:
                        W(e, woT_tok)
                    for cfc in range(EC):
                        lhs = s_woT[:, cfc * E + ofm * 128: cfc * E + ofm * 128 + 128]
                        rhs = s_ctx[:, cfc * OWN + nt * 512: cfc * OWN + nt * 512 + 512]
                        ins = e.matmul(gb.ps[:], lhs, rhs,
                                       start=(cfc == 0), stop=(cfc == EC - 1))
                    gb.ready = pe.inc(ins)
                block.tensor(_)

                def _(e, ofm=ofm, gb=gb):
                    W(e, gb.ready)
                    ins = e.activation(s_t1[:, ofm * 512: ofm * 512 + 512],
                                       gb.ps[:], AF.Identity,
                                       bias=s_bo[:, ofm:ofm + 1], scale=1.0)
                    gb.free = ex.inc(ins)
                block.scalar(_)
                evac_toks.append(ex.tok())

            def _(e, nt=nt, et=evac_toks[-1]):
                W(e, et)
                W(e, xo_tok)
                for ofc in range(EC):
                    ins = e.tensor_tensor(
                        s_ctx[:, ofc * OWN + nt * 512: ofc * OWN + nt * 512 + 512],
                        s_xo[:, ofc * 512: ofc * 512 + 512],
                        s_t1[:, ofc * 512: ofc * 512 + 512], ALU.add)
                ev.inc(ins)
            block.vector(_)
            add_tok = ev.tok()

            src = lambda ofc, nt=nt: s_ctx[:, ofc * OWN + nt * 512: ofc * OWN + nt * 512 + 512]
            tmp = lambda ofc, nt=nt: s_xo[:, ofc * 512: ofc * 512 + 512]
            ln_tok = emit_ln(block, src, tmp, src, s_sqA, s_rowA, add_tok)

            def _(e, nt=nt, ln_tok=ln_tok):
                W(e, B[6].free)
                W(e, ln_tok)
                for cfc in range(EC):
                    ins = e.matmul(
                        B[6].ps[0:EXP, 0:512],
                        s_wg[:, cfc * EXP:(cfc + 1) * EXP],
                        s_ctx[:, cfc * OWN + nt * 512: cfc * OWN + nt * 512 + 512],
                        start=(cfc == 0), stop=(cfc == EC - 1))
                B[6].ready = pe.inc(ins)
            block.tensor(_)

            def _(e, nt=nt):
                W(e, B[6].ready)
                ins = e.activation(s_gl[0:EXP, nt * 512: nt * 512 + 512],
                                   B[6].ps[0:EXP, 0:512], AF.Identity,
                                   bias=s_bg[:], scale=1.0)
                B[6].free = ex.inc(ins)
            block.scalar(_)

            for tq in range(4):
                jobs = []
                for fc in range(EC):
                    srcj = s_ctx[:, fc * OWN + nt * 512 + tq * 128:
                                 fc * OWN + nt * 512 + tq * 128 + 128]
                    dstj = s_xtk[:, (tq % 2) * E + fc * 128:
                                 (tq % 2) * E + fc * 128 + 128]
                    jobs.append((srcj, dstj, s_id[:]))
                pre = (ln_tok,) if tq == 0 else ()
                xt_tok = emit_transposes(block, jobs, [B[0], B[1]],
                                         pre_toks=pre,
                                         evac_pre_toks=(xtk_free[tq % 2],))

                def _(e, nt=nt, tq=tq, xt_tok=xt_tok):
                    W(e, xt_tok)
                    row = nt * 512 + tq * 128
                    xtk_free[tq % 2] = dma(
                        e, sw, X1tok[row:row + 128, :],
                        s_xtk[:, (tq % 2) * E:(tq % 2) * E + E])
                    if nt == 3 and tq == 3:
                        toks = [xtk_free[tq % 2]]
                        for k in range(4):
                            toks.append(dma(
                                e, sw, X1tok[2048 + k * 128: 2176 + k * 128, :],
                                s_xtk[:, (tq % 2) * E:(tq % 2) * E + E]))
                        x1dump_tok[0] = tuple(toks)
                block.gpsimd(_)

        def _(e):
            dma_drain(e, sw)
        block.gpsimd(_)

    close_scope(phB2)
    close_scope(phBB2)

    phC = []
    CAP = 512
    s_gt = SBuf(phC, "s_gt", [128, 16 * EXP])
    s_gst = SBuf(phC, "s_gst", [128, 64])
    s_mf = SBuf(phC, "s_mf", [EXP, OWN])
    s_row1 = SBuf(phC, "s_row1", [1, OWN])
    s_u = SBuf(phC, "s_u", [128, 128], BF16)
    s_tid = SBuf(phC, "s_tid", [128, 64], U32)
    s_dmp = SBuf(phC, "s_dmp", [128, 16])
    s_pfl = SBuf(phC, "s_pfl", [128, 16], U32)
    s_pfi = SBuf(phC, "s_pfi", [128, 4], U32)
    s_i16 = SBuf(phC, "s_i16", [128, 4 * 16])
    s_slot = SBuf(phC, "s_slot", [128, 16], U32)
    s_indc = SBuf(phC, "s_indc", [128, 128])
    s_indcb = SBuf(phC, "s_indcb", [128, 128], BF16)
    s_idxall = SBuf(phC, "s_idxall", [128, 16 * EXP], U32)
    s_ws = SBuf(phC, "s_ws", [128, 16])
    s_xg = SBuf(phC, "s_xg", [128, 4 * E])
    s_xf = SBuf(phC, "s_xf", [128, EC * CAP])
    s_xfh = SBuf(phC, "s_xfh", [128, EC * CAP], BF16)
    s_xfl = SBuf(phC, "s_xfl", [128, EC * CAP], BF16)
    s_hf = SBuf(phC, "s_hf", [128, 4 * 512])
    s_ha = SBuf(phC, "s_ha", [128, 2 * 4 * 512], BF16)
    s_hb = SBuf(phC, "s_hb", [128, 2 * 4 * 512], BF16)
    s_wst = SBuf(phC, "s_wst", [128, 4096])
    s_w1a = SBuf(phC, "s_w1a", [128, 2 * EC * 512], BF16)
    s_w1b = SBuf(phC, "s_w1b", [128, 2 * EC * 512], BF16)
    s_w2a = SBuf(phC, "s_w2a", [128, 2 * 4 * 1024], BF16)
    s_w2b = SBuf(phC, "s_w2b", [128, 2 * 4 * 1024], BF16)
    s_yT = SBuf(phC, "s_yT", [128, 4 * E])
    s_cells = SBuf(phC, "s_cells", [128, 32])

    s_b2r = s_row1

    with nc.named_scope("phaseC"), nc.Block() as block:
        fB2 = (pe.tok(), ev.tok(), ex.tok())
        cst_tok = [None]

        def _(e):
            W(e, x1dump_tok[0])
            t1 = dma(e, ld, s_u[:], u128)
            t2 = dma(e, ld, s_tid[:], tid4)
            t3 = dma(e, ld, s_dmp[:], dmpc)
            t4 = dma(e, ld, s_pfl[:], pflc)
            t5 = dma(e, ld, s_pfi[:], pfic)
            cst_tok[0] = (t1, t2, t3, t4, t5)
        block.sync(_)
        consts_tok = cst_tok[0]

        jobs = []
        for ti in range(16):
            jobs.append((s_gl[0:EXP, ti * 128:(ti + 1) * 128],
                         s_gt[:, ti * EXP:(ti + 1) * EXP], s_id[0:EXP, 0:EXP]))
        gt_tok = emit_transposes(block, jobs, [B[2], B[3]], pre_toks=fB2)

        def _(e, gt_tok=gt_tok):
            W(e, gt_tok)
            for ti in range(16):
                ins = e.reduce_max(s_gst[:, ti * 4: ti * 4 + 1],
                                   s_gt[:, ti * EXP:(ti + 1) * EXP], axis=AXX)
                ev.inc(ins)
            e.wait_ge(ev.h, ev.n)
            for ti in range(16):
                ins = e.tensor_scalar(s_gst[:, ti * 4 + 1: ti * 4 + 2],
                                      s_gst[:, ti * 4: ti * 4 + 1],
                                      -1.0, None, ALU.mult)
                ev.inc(ins)
        block.vector(_)
        gnm_tok = ev.tok()

        def _(e, gnm_tok=gnm_tok):
            W(e, gnm_tok)
            for ti in range(16):
                ins = e.activation(s_gt[:, ti * EXP: ti * EXP + EXP],
                                   s_gt[:, ti * EXP:(ti + 1) * EXP],
                                   AF.Exp, bias=s_gst[:, ti * 4 + 1: ti * 4 + 2],
                                   scale=1.0,
                                   accum_out=s_gst[:, ti * 4 + 2: ti * 4 + 3])
                ex.inc(ins)
        block.scalar(_)
        gs_tok = ex.tok()

        def _(e, gs_tok=gs_tok):
            W(e, gs_tok)
            for ti in range(16):
                ins = e.reciprocal(s_gst[:, ti * 4 + 3: ti * 4 + 4],
                                   s_gst[:, ti * 4 + 2: ti * 4 + 3])
                ev.inc(ins)
            e.wait_ge(ev.h, ev.n)
            for ti in range(16):
                ins = e.tensor_scalar(s_gt[:, ti * EXP:(ti + 1) * EXP],
                                      s_gt[:, ti * EXP:(ti + 1) * EXP],
                                      1.0, s_gst[:, ti * 4 + 3: ti * 4 + 4],
                                      ALU.is_equal, ALU.mult)
                ev.inc(ins)
        block.vector(_)
        mt_tok = ev.tok()

        jobs = []
        for ti in range(16):
            jobs.append((s_gt[:, ti * EXP:(ti + 1) * EXP],
                         s_mf[0:EXP, ti * 128:(ti + 1) * 128], s_id[:]))
        mf_tok = emit_transposes(block, jobs, [B[2], B[3]], pre_toks=(mt_tok,))

        for c in range(4):
            wb = B[c % 2]

            def _(e, c=c, wb=wb, mf_tok=mf_tok):
                W(e, wb.free)
                if c == 0:
                    W(e, mf_tok)
                ins = e.matmul(wb.ps[0:1, 0:512], s_1c[0:EXP, 0:1],
                               s_mf[0:EXP, c * 512:(c + 1) * 512],
                               start=True, stop=True)
                wb.ready = pe.inc(ins)
            block.tensor(_)

            def _(e, c=c, wb=wb):
                W(e, wb.ready)
                ins = e.tensor_copy(s_row1[0:1, c * 512:(c + 1) * 512],
                                    wb.ps[0:1, 0:512])
                wb.free = ev.inc(ins)
            block.vector(_)

        def _(e):
            ins = e.tensor_scalar(s_mf[0:EXP, :], s_mf[0:EXP, :], 0.0, None,
                                  ALU.is_gt)
            ev.inc(ins)
        block.vector(_)
        ws_ind_tok = ev.tok()

        jobs = []
        for c in range(16):
            jobs.append((s_mf[0:EXP, c * 128:(c + 1) * 128],
                         s_indc[:, c * EXP:(c + 1) * EXP], s_id[0:EXP, 0:EXP]))
        for c in range(16):
            jobs.append((s_row1[0:1, c * 128:(c + 1) * 128],
                         s_ws[:, c:c + 1], s_id[0:1, 0:1]))
        red_tok = emit_transposes(block, jobs, [B[2], B[3]],
                                  pre_toks=(ws_ind_tok,))

        def _(e, t=red_tok):
            W(e, t)
            ins = e.tensor_copy(s_indcb[:], s_indc[:])
            ev.inc(ins)
        block.vector(_)
        indb_tok = ev.tok()
        ws_tok = indb_tok

        idx_toks = []
        slot_tok = [None]
        slot_free = [None]
        row1_free = [None]
        for eix in range(EXP):
            for tc0 in range(0, 16, 2):
                def _(e, eix=eix, tc0=tc0, t=indb_tok):
                    for tc in (tc0, tc0 + 1):
                        b_ = B[tc % 2]
                        W(e, b_.free)
                        if tc == 0:
                            W(e, t)
                            W(e, row1_free[0])
                            if eix == 0:
                                W(e, consts_tok)
                        ins = e.matmul(b_.ps[0:1, 0:128],
                                       s_indcb[:, tc * EXP + eix: tc * EXP + eix + 1],
                                       s_u[:], start=True, stop=True)
                        b_.ready = pe.inc(ins)
                block.tensor(_)

                def _(e, tc0=tc0):
                    for tc in (tc0, tc0 + 1):
                        b_ = B[tc % 2]
                        W(e, b_.ready)
                        ins = e.tensor_copy(s_row1[0:1, tc * 128:(tc + 1) * 128],
                                            b_.ps[0:1, 0:128])
                        b_.free = ev.inc(ins)
                block.vector(_)
            pfx_tok = ev.tok()

            def _(e, t=pfx_tok):
                W(e, t)
                c = lambda i: s_cells[0:1, i:i + 1]
                ins = e.tensor_scalar(c(0), s_row1[0:1, 0:1], 0.0, None,
                                      ALU.mult)
                ev.inc(ins)
                for tc in range(1, 16):
                    e.wait_ge(ev.h, ev.n)
                    ins = e.tensor_tensor(c(tc), c(tc - 1),
                                          s_row1[0:1, tc * 128 - 1: tc * 128],
                                          ALU.add)
                    ev.inc(ins)
                e.wait_ge(ev.h, ev.n)
                for tc in range(16):
                    ins = e.tensor_scalar(s_row1[0:1, tc * 128:(tc + 1) * 128],
                                          s_row1[0:1, tc * 128:(tc + 1) * 128],
                                          c(tc), -1.0, ALU.add, ALU.add)
                    ev.inc(ins)
            block.vector(_)
            posg_tok = ev.tok()

            jobs = []
            for tc in range(16):
                jobs.append((s_row1[0:1, tc * 128:(tc + 1) * 128],
                             s_i16[:, 16 + tc: 17 + tc], s_id[0:1, 0:1]))
            posc_tok = emit_transposes(block, jobs, [B[2], B[3]],
                                       pre_toks=(posg_tok,),
                                       evac_pre_toks=(slot_tok[0],))
            row1_free[0] = pe.tok()

            def _(e, eix=eix, t=posc_tok):
                W(e, t)
                W(e, slot_free[0])
                indv = s_indc[:].rearrange("p (c e) -> p c e", e=EXP)[:, :, eix]
                ins = e.tensor_tensor(s_i16[:, 48:64], s_i16[:, 16:32],
                                      indv, ALU.mult)
                ev.inc(ins)
                ins = e.tensor_tensor(s_i16[:, 32:48], s_dmp[:],
                                      indv, ALU.mult)
                ev.inc(ins)
                e.wait_ge(ev.h, ev.n)
                ins = e.tensor_tensor(s_i16[:, 32:48], s_dmp[:],
                                      s_i16[:, 32:48], ALU.subtract)
                ev.inc(ins)
                e.wait_ge(ev.h, ev.n)
                ins = e.tensor_tensor(s_i16[:, 48:64], s_i16[:, 48:64],
                                      s_i16[:, 32:48], ALU.add)
                ev.inc(ins)
                e.wait_ge(ev.h, ev.n)
                ins = e.tensor_copy(s_slot[:], s_i16[:, 48:64])
                ev.inc(ins)
            block.vector(_)
            sl_tok = ev.tok()
            slot_tok[0] = sl_tok

            def _(e, eix=eix, t=sl_tok):
                if eix == 0:
                    W(e, cst_tok[0])
                for j in range(4):
                    if sw.chans[0].n:
                        e.wait_ge(sw.chans[0].h, sw.chans[0].n)
                    ins = e.indirect_dma_start(
                        IdxH, IOA(s_pfi[:, j:j + 1], 0),
                        s_pfl[:, j * 4:(j + 1) * 4],
                        None, element_offset=eix * 2560 * 4)
                    sw.chans[0].n += 16
                    ins.then_inc(sw.chans[0].h, 16)
                W(e, t)
                for tc in range(16):
                    e.wait_ge(sw.chans[0].h, sw.chans[0].n)
                    ins = e.indirect_dma_start(
                        IdxH, IOA(s_slot[:, tc:tc + 1], 0),
                        s_tid[:, tc * 4:(tc + 1) * 4],
                        None, element_offset=eix * 2560 * 4)
                    sw.chans[0].n += 16
                    ins.then_inc(sw.chans[0].h, 16)
                e.wait_ge(sw.chans[0].h, sw.chans[0].n)
                t1 = dma(e, sw, s_idxall[:, eix * 16:(eix + 1) * 16],
                         IdxH[eix * 2560: eix * 2560 + CAP, :].rearrange(
                             "(p c) x -> p (c x)", p=128))
                idx_toks.append(t1)
                slot_free[0] = (sw.chans[0].h, sw.chans[0].n)
            block.gpsimd(_)

        slabs = [(eix, hg) for eix in range(EXP) for hg in range(8)]
        stage_tok = [None, None]
        stage_free = [None, None]
        split_tok = [None, None]
        slabbuf_free = [None, None]
        h_free = [None, None]
        hf_free = [None]
        xg_free = [None]
        xg_tok = [None]
        xf_free = [None]
        xsp_free = [None]
        xsp_tok = [None]
        yT_free = [None]
        b2r_free = [None]
        scat_toks = []

        def emit_slab_load_split(si):
            buf = si % 2
            eix, hg = slabs[si]
            for ci in range(4):
                part, sub = ci // 2, ci % 2
                rs = ci % 2
                stg = s_wst[:, rs * 2048:(rs + 1) * 2048]

                def _(e, part=part, sub=sub, rs=rs, eix=eix, hg=hg, stg=stg):
                    if stage_free[rs] is not None:
                        W(e, stage_free[rs])
                    if part == 0:
                        sl = w1[eix][sub * 512:(sub + 1) * 512,
                                     hg * 512:(hg + 1) * 512].rearrange(
                            "(c p) h -> p c h", p=128)
                        stage_tok[rs] = dma(
                            e, ld, stg.rearrange("p (c h) -> p c h", h=512), sl)
                    else:
                        sl = w2[eix][hg * 512 + sub * 256:
                                     hg * 512 + (sub + 1) * 256, :].rearrange(
                            "(c p) o -> p c o", p=128)
                        stage_tok[rs] = dma(
                            e, ld, stg.rearrange("p (c o) -> p c o", o=E), sl)
                block.sync(_)

                def _(e, buf=buf, part=part, sub=sub, rs=rs, stg=stg):
                    W(e, stage_tok[rs])
                    if part == 0 and sub == 0 and slabbuf_free[buf] is not None:
                        for tk in (slabbuf_free[buf]
                                   if isinstance(slabbuf_free[buf], list)
                                   else [slabbuf_free[buf]]):
                            W(e, tk)
                    wa = s_w1a if part == 0 else s_w2a
                    wb = s_w1b if part == 0 else s_w2b
                    off = buf * 4096 + sub * 2048
                    ins = e.tensor_copy(wa[:, off: off + 2048], stg)
                    ev.inc(ins)
                    e.wait_ge(ev.h, ev.n)
                    ins = e.tensor_tensor(wb[:, off: off + 2048], stg,
                                          wa[:, off: off + 2048],
                                          ALU.subtract)
                    t = ev.inc(ins)
                    if part == 1 and sub == 1:
                        split_tok[buf] = t
                    stage_free[rs] = ev.tok()
                block.vector(_)

        def emit_gather(eix):
            def _(e, eix=eix):
                W(e, idx_toks[eix])
                W(e, xg_free[0])
                for c in range(4):
                    if sw.chans[1].n:
                        e.wait_ge(sw.chans[1].h, sw.chans[1].n)
                    ins = e.indirect_dma_start(
                        s_xg[:, c * E:(c + 1) * E],
                        None, X1tok,
                        IOA(s_idxall[:, eix * 16 + c * 4: eix * 16 + c * 4 + 1], 0))
                    sw.chans[1].n += 16
                    ins.then_inc(sw.chans[1].h, 16)
                xg_tok[0] = (sw.chans[1].h, sw.chans[1].n)
            block.gpsimd(_)

        def emit_xsplit(eix):
            jobs = []
            for c in range(4):
                for fj in range(EC):
                    srcj = s_xg[:, c * E + fj * 128: c * E + fj * 128 + 128]
                    dstj = s_xf[:, fj * CAP + c * 128: fj * CAP + c * 128 + 128]
                    jobs.append((srcj, dstj, s_id[:]))
            xf_tok = emit_transposes(block, jobs, [B[2], B[3]],
                                     pre_toks=(xg_tok[0],),
                                     evac_pre_toks=(xf_free[0],))
            xg_free[0] = pe.tok()

            def _(e, eix=eix, t=xf_tok):
                W(e, t)
                W(e, xsp_free[0])
                ins = e.tensor_copy(s_xfh[:], s_xf[:])
                ev.inc(ins)
                e.wait_ge(ev.h, ev.n)
                ins = e.tensor_tensor(s_xfl[:], s_xf[:], s_xfh[:], ALU.subtract)
                ev.inc(ins)
            block.vector(_)
            xf_free[0] = ev.tok()
            xsp_tok[0] = ev.tok()

        emit_slab_load_split(0)
        emit_gather(0)
        emit_xsplit(0)
        for si, (eix, hg) in enumerate(slabs):
            buf = si % 2
            if si + 1 < len(slabs):
                emit_slab_load_split(si + 1)
            my_split = split_tok[buf]
            if hg == 0:
                def _(e, eix=eix):
                    W(e, b2r_free[0])
                    if eix == 0:
                        W(e, idx_toks[EXP - 1])
                    cst_tok[0] = dma(e, ld, s_b2r[0:1, 0:E], b2[eix:eix + 1, :])
                block.sync(_)
                b2r_tok = cst_tok[0]
            hbuf = si % 2
            h_toks = []
            for hm in range(4):
                gb = B[4 + hm % 2]

                def _(e, hm=hm, gb=gb, buf=buf, st_=my_split, eix=eix,
                      hg=hg):
                    W(e, gb.free)
                    if hm == 0:
                        W(e, st_)
                        if hg == 0:
                            W(e, xsp_tok[0])
                    first = True
                    for dc in range(EC):
                        woff = buf * 4096 + dc * 512 + hm * 128
                        xoff = dc * CAP
                        for (w_, x_) in ((s_w1a, s_xfh), (s_w1b, s_xfh),
                                         (s_w1a, s_xfl)):
                            ins = e.matmul(
                                gb.ps[:], w_[:, woff: woff + 128],
                                x_[:, xoff: xoff + CAP],
                                start=first,
                                stop=(dc == EC - 1 and x_ is s_xfl))
                            first = False
                    gb.ready = pe.inc(ins)
                block.tensor(_)

                def _(e, eix=eix, hg=hg, hm=hm, gb=gb):
                    W(e, gb.ready)
                    if hm == 0:
                        W(e, h_free[hbuf])
                        W(e, hf_free[0])
                    col = eix * HC + hg * 4 + hm
                    ins = e.activation(
                        s_hf[:, hm * 512: hm * 512 + 512], gb.ps[:],
                        AF.Silu, bias=s_b1[:, col:col + 1], scale=1.0)
                    gb.free = ex.inc(ins)
                block.scalar(_)
                h_toks.append(ex.tok())

            def _(e, hbuf=hbuf, ht=h_toks):
                W(e, ht[-1])
                for hm in range(4):
                    ins = e.tensor_copy(
                        s_ha[:, hbuf * 2048 + hm * 512: hbuf * 2048 + hm * 512 + 512],
                        s_hf[:, hm * 512: hm * 512 + 512])
                    ev.inc(ins)
                e.wait_ge(ev.h, ev.n)
                for hm in range(4):
                    ins = e.tensor_tensor(
                        s_hb[:, hbuf * 2048 + hm * 512: hbuf * 2048 + hm * 512 + 512],
                        s_hf[:, hm * 512: hm * 512 + 512],
                        s_ha[:, hbuf * 2048 + hm * 512: hbuf * 2048 + hm * 512 + 512],
                        ALU.subtract)
                    ev.inc(ins)
            block.vector(_)
            hsp_tok = ev.tok()
            hf_free[0] = hsp_tok

            for yt in range(8):
                qm, fg = divmod(yt, 2)
                gb = B[6 + yt % 2]

                def _(e, qm=qm, fg=fg, gb=gb, ht=hsp_tok, buf=buf, hbuf=hbuf,
                      hg=hg, b2t=(b2r_tok if hg == 0 else None)):
                    W(e, gb.free)
                    if qm == 0 and fg == 0:
                        W(e, ht)
                        if hg == 0:
                            W(e, b2t)
                    first = True
                    mms = []
                    for hc in range(4):
                        woff = buf * 4096 + hc * 1024 + fg * 512
                        hoff = hbuf * 2048 + hc * 512 + qm * 128
                        for (h_, w_) in ((s_ha, s_w2a), (s_ha, s_w2b),
                                         (s_hb, s_w2a)):
                            mms.append((h_[:, hoff: hoff + 128],
                                        w_[:, woff: woff + 512]))
                    for i, (lh, rh) in enumerate(mms):
                        is_last = (i == len(mms) - 1) and hg != 0
                        ins = e.matmul(gb.ps[:], lh, rh, start=first,
                                       stop=is_last)
                        first = False
                    if hg == 0:
                        ins = e.matmul(gb.ps[:], s_1r[0:1, 0:128],
                                       s_b2r[0:1, fg * 512: fg * 512 + 512],
                                       start=False, stop=True)
                    gb.ready = pe.inc(ins)
                block.tensor(_)

                def _(e, qm=qm, fg=fg, gb=gb, hg=hg):
                    W(e, gb.ready)
                    if qm == 0 and fg == 0 and hg == 0:
                        W(e, yT_free[0])
                    dst = s_yT[:, qm * E + fg * 512: qm * E + fg * 512 + 512]
                    if hg == 0:
                        ins = e.tensor_copy(dst, gb.ps[:])
                    else:
                        ins = e.tensor_tensor(dst, dst, gb.ps[:], ALU.add)
                    gb.free = ev.inc(ins)
                block.vector(_)
            h_free[hbuf] = pe.tok()
            slabbuf_free[buf] = pe.tok()

            if hg == 7:
                yev_tok = ev.tok()

                def _(e, eix=eix, t=yev_tok):
                    W(e, t)
                    for c in range(4):
                        if sw.chans[2].n:
                            e.wait_ge(sw.chans[2].h, sw.chans[2].n)
                        ins = e.indirect_dma_start(
                            Ytok,
                            IOA(s_idxall[:, eix * 16 + c * 4: eix * 16 + c * 4 + 1], 0),
                            s_yT[:, c * E:(c + 1) * E], None)
                        sw.chans[2].n += 16
                        ins.then_inc(sw.chans[2].h, 16)
                    scat_toks.append((sw.chans[2].h, sw.chans[2].n))
                block.gpsimd(_)
                yT_free[0] = scat_toks[-1]
                b2r_free[0] = pe.tok()
                if eix + 1 < EXP:
                    xsp_free[0] = pe.tok()
                    emit_gather(eix + 1)
                    emit_xsplit(eix + 1)

        def _(e):
            W(e, ws_tok)
            for t in scat_toks:
                W(e, t)
            ins = e.tensor_copy(s_cells[0:1, 30:31], s_ws[0:1, 0:1])
            ev.inc(ins)
        block.vector(_)
        asm_dve = ev.tok()

        at_free = [None] * 2
        ot_free = [None] * 2
        for tc in range(16):
            ab = tc % 2

            def _(e, tc=tc, ab=ab):
                if tc == 0:
                    for t in scat_toks:
                        W(e, t)
                W(e, at_free[ab])
                t1 = dma(e, ld, s_xg[:, ab * 2 * E: ab * 2 * E + E],
                         X1tok[tc * 128: tc * 128 + 128, :])
                t2 = dma(e, ld, s_xg[:, ab * 2 * E + E: ab * 2 * E + 2 * E],
                         Ytok[tc * 128: tc * 128 + 128, :])
                cst_tok[0] = (t1, t2)
            block.sync(_)
            a_tok = cst_tok[0]

            c = lambda i, tc=tc: s_cells[:, (tc % 4) * 8 + i: (tc % 4) * 8 + i + 1]

            def _(e, tc=tc, ab=ab, a_tok=a_tok, c=c):
                W(e, a_tok)
                if tc < 2:
                    W(e, asm_dve)
                xa = s_xg[:, ab * 2 * E: ab * 2 * E + E]
                ya = s_xg[:, ab * 2 * E + E: ab * 2 * E + 2 * E]
                ins = e.tensor_scalar(ya, ya, s_ws[:, tc:tc + 1], None, ALU.mult)
                ev.inc(ins)
                e.wait_ge(ev.h, ev.n)
                ins = e.tensor_tensor(xa, xa, ya, ALU.add)
                ev.inc(ins)
            block.vector(_)
            x2_tok = ev.tok()

            def _(e, tc=tc, ab=ab, x2_tok=x2_tok, c=c):
                W(e, x2_tok)
                if ex.n:
                    e.wait_ge(ex.h, ex.n)
                xa = s_xg[:, ab * 2 * E: ab * 2 * E + E]
                tr = s_xf[:, 0:E]
                ins = e.activation(tr, xa, AF.Identity, bias=0.0, scale=1.0,
                                   accum_out=c(0))
                ex.inc(ins)
                e.wait_ge(ex.h, ex.n)
                ins = e.activation(tr, xa, AF.Square, bias=0.0, scale=1.0,
                                   accum_out=c(1))
                ex.inc(ins)
            block.scalar(_)
            st_tok = ex.tok()

            def _(e, tc=tc, st_tok=st_tok, c=c):
                W(e, st_tok)
                ins = e.tensor_scalar(c(2), c(0), 1.0 / E, None, ALU.mult)
                ev.inc(ins)
                ins = e.tensor_scalar(c(3), c(1), 1.0 / E, None, ALU.mult)
                ev.inc(ins)
                e.wait_ge(ev.h, ev.n)
                ins = e.tensor_tensor(c(4), c(2), c(2), ALU.mult)
                ev.inc(ins)
                e.wait_ge(ev.h, ev.n)
                ins = e.tensor_tensor(c(4), c(3), c(4), ALU.subtract)
                ev.inc(ins)
            block.vector(_)
            var_tok = ev.tok()

            def _(e, tc=tc, var_tok=var_tok, c=c):
                W(e, var_tok)
                ins = e.activation(c(5), c(4), AF.Sqrt, bias=EPS, scale=1.0)
                ex.inc(ins)
            block.scalar(_)
            sq_tok = ex.tok()

            def _(e, tc=tc, ab=ab, sq_tok=sq_tok, c=c):
                W(e, sq_tok)
                ins = e.reciprocal(c(6), c(5))
                ev.inc(ins)
                ins = e.tensor_scalar(c(7), c(2), -1.0, None, ALU.mult)
                ev.inc(ins)
                e.wait_ge(ev.h, ev.n)
                W(e, ot_free[ab])
                xa = s_xg[:, ab * 2 * E: ab * 2 * E + E]
                ins = e.tensor_scalar(s_xf[:, (2 + ab) * E:(3 + ab) * E],
                                      xa, c(7), c(6), ALU.add, ALU.mult)
                ev.inc(ins)
            block.vector(_)
            o_tok = ev.tok()
            at_free[ab] = o_tok

            def _(e, tc=tc, ab=ab, o_tok=o_tok):
                W(e, o_tok)
                ot_free[ab] = dma(e, sw, out[tc * 128: tc * 128 + 128, :],
                                  s_xf[:, (2 + ab) * E:(3 + ab) * E])
            block.gpsimd(_)

        def _(e):
            dma_drain(e, sw)
        block.gpsimd(_)

    close_scope(phC)
    for b in reversed(B):
        b.cm.__exit__(None, None, None)
    close_scope(live)
    return nc


def _make_consts():
    identf = np.eye(128, dtype=np.float32)
    identbf = np.eye(128, dtype=np.float32).astype(ml_dtypes.bfloat16)
    onescol = np.ones((128, 1), dtype=np.float32)
    onesrow = np.ones((1, 128), dtype=np.float32)
    cvec = np.zeros((128, 4), dtype=np.float32)
    cvec[:, 0] = EPS
    selm = np.zeros((EXP, EXP * 128), dtype=np.float32)
    for e_ in range(EXP):
        selm[e_, e_ * 128:(e_ + 1) * 128] = 1.0
    return identf, identbf, onescol, onesrow, cvec, selm


def _make_sparse_consts():
    p = np.arange(128)[:, None]
    j = np.arange(128)[None, :]
    u128 = (p <= j).astype(np.float32).astype(ml_dtypes.bfloat16)
    tid4 = np.zeros((128, 64), np.uint32)
    for tc in range(16):
        for x in range(4):
            tid4[:, tc * 4 + x] = tc * 128 + np.arange(128)
    dmpc = np.zeros((128, 16), np.float32)
    for tc in range(16):
        dmpc[:, tc] = 512 + tc * 128 + np.arange(128)
    pflc = np.zeros((128, 16), np.uint32)
    for jj in range(4):
        for x in range(4):
            pflc[:, jj * 4 + x] = 2048 + jj * 128 + np.arange(128)
    pfic = np.zeros((128, 4), np.uint32)
    for jj in range(4):
        pfic[:, jj] = jj * 128 + np.arange(128)
    return (np.ascontiguousarray(u128), tid4, dmpc, pflc, pfic)


def _make_mask(par):
    i = np.arange(QS)[:, None]
    j = np.arange(TAIL)[None, :]
    m = np.where(j > par * 512 + i, NEG, 0.0).astype(np.float32)
    return np.ascontiguousarray(m.astype(ml_dtypes.bfloat16))


def _make_maskT(par):
    j = np.arange(TAIL)[:, None]
    i = np.arange(QS)[None, :]
    m = np.where(j > par * 512 + i, NEG, 0.0).astype(np.float32)
    return np.ascontiguousarray(m.astype(ml_dtypes.bfloat16))


def kernel(L0_attn_x, L0_attn_in_proj_w, L0_attn_in_proj_b, L0_attn_out_proj_w,
           L0_attn_out_proj_b, L0_ffn_Wg, L0_ffn_bg, L0_ffn_W1, L0_ffn_b1,
           L0_ffn_W2, L0_ffn_b2):
    x = np.ascontiguousarray(np.asarray(L0_attn_x, dtype=np.float32))
    Bb = x.shape[0]
    identf, identbf, onescol, onesrow, cvec, selm = _make_consts()
    weights = dict(
        ipw=np.ascontiguousarray(np.asarray(L0_attn_in_proj_w, np.float32)),
        ipb=np.ascontiguousarray(np.asarray(L0_attn_in_proj_b, np.float32)),
        opw=np.ascontiguousarray(np.asarray(L0_attn_out_proj_w, np.float32)),
        opb=np.ascontiguousarray(np.asarray(L0_attn_out_proj_b, np.float32)),
        wgw=np.ascontiguousarray(np.asarray(L0_ffn_Wg, np.float32)),
        bgv=np.ascontiguousarray(np.asarray(L0_ffn_bg, np.float32)),
        w1=np.ascontiguousarray(np.asarray(L0_ffn_W1, np.float32)),
        b1=np.ascontiguousarray(np.asarray(L0_ffn_b1, np.float32)),
        w2=np.ascontiguousarray(np.asarray(L0_ffn_W2, np.float32)),
        b2=np.ascontiguousarray(np.asarray(L0_ffn_b2, np.float32)),
        ident=identf, identb=identbf, onescol=onescol, onesrow=onesrow,
        cvec=cvec, selm=selm,
    )
    u128_, tid4_, dmpc_, pflc_, pfic_ = _make_sparse_consts()
    weights.update(u128=u128_, tid4=tid4_, dmpc=dmpc_, pflc=pflc_, pfic=pfic_)
    masks = [_make_mask(0), _make_mask(1)]
    masksT = [_make_maskT(0), _make_maskT(1)]
    in_maps = []
    for c in range(8):
        b, par = c // 2, c % 2
        blocks = [2 * k + par for k in range(NSLOT)]
        xo = np.ascontiguousarray(
            np.concatenate([x[b, a * 512:(a + 1) * 512] for a in blocks], axis=0))
        m = dict(weights)
        m["x_full"] = np.ascontiguousarray(x[b])
        m["x_own"] = xo
        m["mask_in"] = masks[par]
        m["maskT_in"] = masksT[par]
        in_maps.append(m)

    nc = build_program()
    res = run_bass_kernel_spmd(nc, in_maps, core_ids=list(range(8)))

    y = np.empty((Bb, S, E), dtype=np.float32)
    for c in range(8):
        b, par = c // 2, c % 2
        oc = np.asarray(res.results[c]["out"])
        for k in range(NSLOT):
            a = 2 * k + par
            y[b, a * 512:(a + 1) * 512] = oc[k * 512:(k + 1) * 512]
    return y


# version 18
# speedup vs baseline: 2.9887x; 1.0028x over previous
import numpy as np
import ml_dtypes
import concourse.bass as bass
from concourse import library_config
import concourse.mybir as mybir
from concourse.bass_utils import run_bass_kernel_spmd

F32 = mybir.dt.float32
BF16 = mybir.dt.bfloat16
U32 = mybir.dt.uint32
IOA = bass.IndirectOffsetOnAxis
AF = mybir.ActivationFunctionType
ALU = mybir.AluOpType
AXX = mybir.AxisListType.X

NEG = -1e30
EPS = 1e-5

S, E, H, HID, EXP = 4096, 1024, 16, 4096, 8
D = E // H
OWN = S // 2
QS = S // 8
EC = E // 128
HC = HID // 128
TAIL = S // 4
NSLOT = 4
NQT = QS // 128
SC = S // 128
ISQ = 1.0 / float(D) ** 0.5


def ceil_div(a, b):
    return (a + b - 1) // b


class Sem:
    __slots__ = ("h", "n")

    def __init__(self, nc, name):
        self.h = nc.alloc_semaphore(name=name)
        self.n = 0

    def inc(self, instr, k=1):
        instr.then_inc(self.h, k)
        self.n += k
        return (self, self.n)

    def tok(self):
        return (self, self.n)


def W(e, tk):
    if tk is None:
        return
    if isinstance(tk, (list, tuple)) and tk and isinstance(tk[0], (list, tuple)):
        for t in tk:
            W(e, t)
        return
    if tk[1] > 0:
        h = tk[0].h if hasattr(tk[0], "h") else tk[0]
        e.wait_ge(h, tk[1])


class DmaPool:
    __slots__ = ("chans", "i")

    def __init__(self, nc, name, n=8):
        self.chans = [Sem(nc, "%s%d" % (name, i)) for i in range(n)]
        self.i = 0


class Bank:
    __slots__ = ("cm", "ps", "ready", "free")

    def __init__(self, nc, i):
        self.cm = nc.psum_tensor("pbank%d" % i, [128, 512], F32)
        self.ps = self.cm.__enter__()
        self.ready = None
        self.free = None


def build_program():
    nc = bass.Bass("TRN2", target_bir_lowering=False)

    x_full = nc.dram_tensor("x_full", [S, E], F32, kind="ExternalInput").ap()
    x_own = nc.dram_tensor("x_own", [OWN, E], F32, kind="ExternalInput").ap()
    mask_in = nc.dram_tensor("mask_in", [QS, TAIL], BF16, kind="ExternalInput").ap()
    maskT_in = nc.dram_tensor("maskT_in", [TAIL, QS], BF16, kind="ExternalInput").ap()
    ipw = nc.dram_tensor("ipw", [3 * E, E], F32, kind="ExternalInput").ap()
    ipb = nc.dram_tensor("ipb", [3 * E], F32, kind="ExternalInput").ap()
    opw = nc.dram_tensor("opw", [E, E], F32, kind="ExternalInput").ap()
    opb = nc.dram_tensor("opb", [E], F32, kind="ExternalInput").ap()
    wgw = nc.dram_tensor("wgw", [E, EXP], F32, kind="ExternalInput").ap()
    bgv = nc.dram_tensor("bgv", [EXP], F32, kind="ExternalInput").ap()
    w1 = nc.dram_tensor("w1", [EXP, E, HID], F32, kind="ExternalInput").ap()
    b1 = nc.dram_tensor("b1", [EXP, HID], F32, kind="ExternalInput").ap()
    w2 = nc.dram_tensor("w2", [EXP, HID, E], F32, kind="ExternalInput").ap()
    b2 = nc.dram_tensor("b2", [EXP, E], F32, kind="ExternalInput").ap()
    ident = nc.dram_tensor("ident", [128, 128], F32, kind="ExternalInput").ap()
    identb = nc.dram_tensor("identb", [128, 128], BF16, kind="ExternalInput").ap()
    onescol = nc.dram_tensor("onescol", [128, 1], F32, kind="ExternalInput").ap()
    onesrow = nc.dram_tensor("onesrow", [1, 128], F32, kind="ExternalInput").ap()
    cvec = nc.dram_tensor("cvec", [128, 4], F32, kind="ExternalInput").ap()
    selm = nc.dram_tensor("selm", [EXP, EXP * 128], F32, kind="ExternalInput").ap()
    out = nc.dram_tensor("out", [OWN, E], F32, kind="ExternalOutput").ap()

    Ks = nc.dram_tensor("Ks", [E, S], F32, kind="Internal").ap()
    Vs = nc.dram_tensor("Vs", [H, S, D], F32, kind="Internal").ap()
    Qs = nc.dram_tensor("Qs", [E, OWN], F32, kind="Internal").ap()
    X1s = nc.dram_tensor("X1s", [E, OWN], F32, kind="Internal").ap()
    X1tok = nc.dram_tensor("X1tok", [2560, E], F32, kind="Internal").ap()
    Ytok = nc.dram_tensor("Ytok", [2560, E], F32, kind="Internal").ap()
    IdxH = nc.dram_tensor("IdxH", [EXP * 2560, 4], U32, kind="Internal").ap()
    u128 = nc.dram_tensor("u128", [128, 128], BF16, kind="ExternalInput").ap()
    tid4 = nc.dram_tensor("tid4", [128, 64], U32, kind="ExternalInput").ap()
    dmpc = nc.dram_tensor("dmpc", [128, 16], F32, kind="ExternalInput").ap()
    pflc = nc.dram_tensor("pflc", [128, 16], U32, kind="ExternalInput").ap()
    pfic = nc.dram_tensor("pfic", [128, 4], U32, kind="ExternalInput").ap()

    live = []

    def SBuf(scope, name, shape, dt=F32):
        cm = nc.sbuf_tensor(name, shape, dt)
        t = cm.__enter__()
        scope.append(cm)
        return t

    def close_scope(scope):
        for cm in reversed(scope):
            cm.__exit__(None, None, None)
        scope.clear()

    s_id = SBuf(live, "s_id", [128, 128])
    s_idb = SBuf(live, "s_idb", [128, 128], BF16)
    s_1c = SBuf(live, "s_1c", [128, 1])
    s_1r = SBuf(live, "s_1r", [1, 128])
    s_cv = SBuf(live, "s_cv", [128, 4])
    s_bqkv = SBuf(live, "s_bqkv", [128, 24])
    s_bo = SBuf(live, "s_bo", [128, EC])
    s_bg = SBuf(live, "s_bg", [EXP, 1])
    s_b1 = SBuf(live, "s_b1", [128, EXP * HC])
    s_b2 = SBuf(live, "s_b2", [EXP, E])
    s_wg = SBuf(live, "s_wg", [128, EC * EXP])
    s_gl = SBuf(live, "s_gl", [EXP, OWN])
    s_sel = SBuf(live, "s_sel", [EXP, EXP * 128])
    nc.const_aps.register(F32, EPS, s_cv[:, 0:1])
    nc.const_aps.register(F32, 0.0, s_cv[:, 1:2])

    B = [Bank(nc, i) for i in range(8)]
    pe = Sem(nc, "pe")
    ev = Sem(nc, "ev")
    ex = Sem(nc, "ex")
    gp = Sem(nc, "gp")
    ld = DmaPool(nc, "ld", 8)
    sw = DmaPool(nc, "sw", 8)

    def dma(e, pool, dst, src, slow=False):
        ch = pool.chans[pool.i % len(pool.chans)]
        pool.i += 1
        if ch.n:
            e.wait_ge(ch.h, ch.n)
        if slow:
            ins = e.dma_start(dst, src, allow_slow_non_contiguous=True)
        else:
            ins = e.dma_start(dst, src)
        ins.then_inc(ch.h, 16)
        ch.n += 16
        return (ch, ch.n)

    def dma_drain(e, pool):
        for ch in pool.chans:
            if ch.n:
                e.wait_ge(ch.h, ch.n)

    def emit_transposes(block, jobs, banks, evac="vector", pre_toks=(),
                        evac_pre_toks=()):
        nb = len(jobs)
        nbatch = ceil_div(nb, 4)
        sem = ev if evac == "vector" else ex

        for bi in range(nbatch):
            bank = banks[bi % 2]

            def _(e, bi=bi, bank=bank):
                if bi == 0:
                    for tk in pre_toks:
                        W(e, tk)
                W(e, bank.free)
                n4 = min(4, nb - bi * 4)
                for s in range(n4):
                    src, _d, idap = jobs[bi * 4 + s]
                    op_, ow = src.shape[1], src.shape[0]
                    ins = e.transpose(bank.ps[0:op_, s * 128:s * 128 + ow], src, idap)
                bank.ready = pe.inc(ins)
            block.tensor(_)

            def _(e, bi=bi, bank=bank):
                if bi == 0:
                    for tk in evac_pre_toks:
                        W(e, tk)
                W(e, bank.ready)
                n4 = min(4, nb - bi * 4)
                for s in range(n4):
                    src, dst, _i = jobs[bi * 4 + s]
                    op_, ow = src.shape[1], src.shape[0]
                    if evac == "vector":
                        ins = e.tensor_copy(dst, bank.ps[0:op_, s * 128:s * 128 + ow])
                    else:
                        ins = e.copy(dst, bank.ps[0:op_, s * 128:s * 128 + ow])
                bank.free = sem.inc(ins)
            if evac == "vector":
                block.vector(_)
            else:
                block.scalar(_)
        return sem.tok()

    phA = []
    s_wkvT = SBuf(phA, "s_wkvT", [128, EC * 2 * E])
    s_wqT = SBuf(phA, "s_wqT", [128, EC * E])
    s_wtmp = SBuf(phA, "s_wtmp", [128, 2 * E])
    s_xtm = SBuf(phA, "s_xtm", [128, 2 * 4 * E])
    s_xfm = SBuf(phA, "s_xfm", [128, 2 * EC * 512])
    s_kvq = SBuf(phA, "s_kvq", [128, 4 * 512])
    s_vout = SBuf(phA, "s_vout", [128, 2 * 512])

    with nc.named_scope("phaseA"), nc.Block() as block:
        def _(e):
            dma(e, ld, s_id[:], ident)
            dma(e, ld, s_idb[:], identb)
            dma(e, ld, s_1c[:], onescol)
            dma(e, ld, s_1r[:], onesrow)
            dma(e, ld, s_cv[:], cvec)
            dma(e, ld, s_bqkv[:], ipb.rearrange("(c p) -> p c", p=128), slow=True)
            dma(e, ld, s_bo[:], opb.rearrange("(c p) -> p c", p=128), slow=True)
            dma(e, ld, s_bg[:], bgv.rearrange("(e o) -> e o", o=1), slow=True)
            dma(e, ld, s_b1[:].rearrange("p (ee c) -> p ee c", c=HC),
                b1.rearrange("ee (c p) -> p ee c", p=128), slow=True)
            dma(e, ld, s_b2[:], b2)
            dma(e, ld, s_sel[:], selm)
            dma(e, ld, s_wg[:].rearrange("p (c e) -> p c e", e=EXP),
                wgw.rearrange("(c p) e -> p c e", p=128), slow=True)
        block.sync(_)

        strip_tok = [None] * 24
        strip_free = [None, None]

        def _(e):
            for r in range(2):
                strip_tok[r] = dma(e, ld, s_wtmp[:, r * E:(r + 1) * E],
                                   ipw[r * 128:(r + 1) * 128, :])
        block.sync(_)

        for r in range(24):
            jobs = []
            for ifc in range(EC):
                src = s_wtmp[:, (r % 2) * E + ifc * 128:(r % 2) * E + ifc * 128 + 128]
                if r < 8:
                    dst = s_wqT[:, ifc * E + r * 128: ifc * E + r * 128 + 128]
                else:
                    dst = s_wkvT[:, ifc * 2 * E + (r - 8) * 128:
                                 ifc * 2 * E + (r - 8) * 128 + 128]
                jobs.append((src, dst, s_id[:]))
            emit_transposes(block, jobs, [B[0], B[1]], pre_toks=(strip_tok[r],))
            strip_free[r % 2] = pe.tok()

            if r + 2 < 24:
                def _(e, r=r):
                    W(e, strip_free[r % 2])
                    strip_tok[r + 2] = dma(
                        e, ld, s_wtmp[:, (r % 2) * E:(r % 2) * E + E],
                        ipw[(r + 2) * 128:(r + 3) * 128, :])
                block.sync(_)
        wT_tok = ev.tok()

        kvq_free = [None] * 4
        kvq_i = [0]
        vout_free = [None, None]
        vb_i = [0]
        xtm_free = [None, None]
        xfm_free = [None, None]

        def emit_x_block(src_rows, parity):
            tok_holder = []

            def _(e, parity=parity):
                W(e, xtm_free[parity])
                tok_holder.append(dma(
                    e, ld,
                    s_xtm[:, parity * 4 * E: parity * 4 * E + 4 * E].rearrange(
                        "p (t d) -> p t d", d=E),
                    src_rows.rearrange("(t p) d -> p t d", p=128)))
            block.sync(_)

            jobs = []
            for j in range(4 * EC):
                t, ifc = divmod(j, EC)
                src = s_xtm[:, parity * 4 * E + t * E + ifc * 128:
                            parity * 4 * E + t * E + ifc * 128 + 128]
                dst = s_xfm[:, parity * EC * 512 + ifc * 512 + t * 128:
                            parity * EC * 512 + ifc * 512 + t * 128 + 128]
                jobs.append((src, dst, s_id[:]))
            emit_transposes(block, jobs, [B[2], B[3]],
                            pre_toks=(tok_holder[0], xfm_free[parity]))
            xtm_free[parity] = pe.tok()

        def emit_proj(parity, ofm_list, blk):
            for ofm in ofm_list:
                gb = B[4 + (ofm % 2)]
                is_q = ofm < 8

                def _(e, ofm=ofm, gb=gb, is_q=is_q, parity=parity):
                    W(e, gb.free)
                    W(e, wT_tok)
                    W(e, ev.tok())
                    for ifc in range(EC):
                        if is_q:
                            lhs = s_wqT[:, ifc * E + ofm * 128: ifc * E + ofm * 128 + 128]
                        else:
                            lhs = s_wkvT[:, ifc * 2 * E + (ofm - 8) * 128:
                                         ifc * 2 * E + (ofm - 8) * 128 + 128]
                        rhs = s_xfm[:, parity * EC * 512 + ifc * 512:
                                    parity * EC * 512 + ifc * 512 + 512]
                        ins = e.matmul(gb.ps[:], lhs, rhs,
                                       start=(ifc == 0), stop=(ifc == EC - 1))
                    gb.ready = pe.inc(ins)
                block.tensor(_)

                slot = kvq_i[0] % 4
                kvq_i[0] += 1

                def _(e, ofm=ofm, gb=gb, slot=slot):
                    W(e, gb.ready)
                    W(e, kvq_free[slot])
                    ins = e.activation(s_kvq[:, slot * 512:slot * 512 + 512],
                                       gb.ps[:], AF.Identity,
                                       bias=s_bqkv[:, ofm:ofm + 1], scale=1.0)
                    gb.free = ex.inc(ins)
                block.scalar(_)
                evac_tok = ex.tok()

                if is_q or ofm < 16:
                    def _(e, ofm=ofm, slot=slot, evac_tok=evac_tok, blk=blk,
                          is_q=is_q):
                        W(e, evac_tok)
                        if is_q:
                            dst = Qs[ofm * 128:(ofm + 1) * 128,
                                     blk * 512:(blk + 1) * 512]
                        else:
                            dst = Ks[(ofm - 8) * 128:(ofm - 7) * 128,
                                     blk * 512:(blk + 1) * 512]
                        kvq_free[slot] = dma(e, sw, dst,
                                             s_kvq[:, slot * 512:slot * 512 + 512])
                    block.gpsimd(_)
                else:
                    vb = vb_i[0] % 2
                    vb_i[0] += 1
                    vjobs = []
                    for t in range(4):
                        src = s_kvq[:, slot * 512 + t * 128: slot * 512 + t * 128 + 128]
                        dst = s_vout[:, vb * 512 + t * 128: vb * 512 + t * 128 + 128]
                        vjobs.append((src, dst, s_id[:]))
                    vtok = emit_transposes(block, vjobs, [B[6], B[7]],
                                           pre_toks=(evac_tok,),
                                           evac_pre_toks=vout_free[vb] or ())
                    kvq_free[slot] = pe.tok()

                    def _(e, ofm=ofm, blk=blk, vb=vb, vtok=vtok):
                        W(e, vtok)
                        tks = []
                        for hh in range(2):
                            hidx = (ofm - 16) * 2 + hh
                            src = s_vout[:, vb * 512:(vb + 1) * 512].rearrange(
                                "p (t dd) -> p t dd", dd=128)[:, :, hh * 64:(hh + 1) * 64]
                            dst = Vs[hidx, blk * 512:(blk + 1) * 512, :].rearrange(
                                "(t p) d -> p t d", p=128)
                            tks.append(dma(e, sw, dst, src))
                        vout_free[vb] = tuple(tks)
                    block.gpsimd(_)

        for tb in range(8):
            emit_x_block(x_full[tb * 512:(tb + 1) * 512, :], tb % 2)
            emit_proj(tb % 2, list(range(8, 24)), tb)
            xfm_free[tb % 2] = ex.tok()
        for tq in range(4):
            emit_x_block(x_own[tq * 512:(tq + 1) * 512, :], tq % 2)
            emit_proj(tq % 2, list(range(0, 8)), tq)
            xfm_free[tq % 2] = ex.tok()

        def _(e):
            dma_drain(e, sw)
        block.gpsimd(_)

    close_scope(phA)

    phBB2 = []
    s_ctx = SBuf(phBB2, "s_ctx", [128, EC * OWN])

    phB = []
    s_K = SBuf(phB, "s_K", [128, S])
    s_Kh = SBuf(phB, "s_Kh", [128, S], BF16)
    s_Kl = SBuf(phB, "s_Kl", [128, S], BF16)
    s_V = SBuf(phB, "s_V", [128, 2 * SC * 65])
    s_Vh = SBuf(phB, "s_Vh", [128, 2 * SC * 65], BF16)
    s_Vl = SBuf(phB, "s_Vl", [128, 2 * SC * 65], BF16)
    s_Q = SBuf(phB, "s_Q", [128, OWN])
    s_Qh = SBuf(phB, "s_Qh", [128, OWN], BF16)
    s_Ql = SBuf(phB, "s_Ql", [128, OWN], BF16)
    s_maskT = SBuf(phB, "s_maskT", [128, (TAIL // 128) * QS], BF16)
    s_ptf = SBuf(phB, "s_ptf", [128, 8 * 512])
    s_pth = SBuf(phB, "s_pth", [128, 8 * 512], BF16)
    s_ptl = SBuf(phB, "s_ptl", [128, 8 * 512], BF16)
    s_rt = SBuf(phB, "s_rt", [1, 512])
    s_rb = SBuf(phB, "s_rb", [64, 512])

    with nc.named_scope("phaseB"), nc.Block() as block:
        fenceA = (pe.tok(), ev.tok(), ex.tok())
        mask_tok = [None]

        def _(e):
            t2 = dma(e, ld, s_maskT[:].rearrange("p (c q) -> p c q", q=QS),
                     maskT_in.rearrange("(c p) q -> p c q", p=128))
            mask_tok[0] = t2
        block.sync(_)

        split_toks = [None, None]
        head_end = [None, None]
        kvq_toks = [None, None]
        pt_free = [None] * 8
        ptf_free = [None] * 8
        rb_free = [None]
        ctx_banks = [B[4], B[5]]
        bc_bank = B[6]

        def emit_kvq_load(h):
            hb = h % 2

            def _(e, h=h, hb=hb):
                W(e, head_end[hb])
                if h < 2:
                    W(e, fenceA)
                t1 = dma(e, ld, s_K[hb * 64:(hb + 1) * 64, :], Ks[h * D:(h + 1) * D, :])
                vdst = s_V[:].rearrange("p (b c d) -> p b c d", b=2, d=65)[:, hb, :, 0:64]
                t2 = dma(e, ld, vdst, Vs[h].rearrange("(c p) d -> p c d", p=128))
                t3 = dma(e, ld, s_Q[hb * 64:(hb + 1) * 64, :],
                         Qs[h * D:(h + 1) * D, :])
                kvq_toks[hb] = (t1, t2, t3)
            block.sync(_)

            def _(e, hb=hb):
                W(e, kvq_toks[hb])
                vv = s_V[:].rearrange("p (b c d) -> p b c d", b=2, d=65)
                ins = e.tensor_scalar(vv[:, hb, :, 64:65], vv[:, hb, :, 63:64],
                                      0.0, 1.0, ALU.mult, ALU.add)
                ev.inc(ins)
                e.wait_ge(ev.h, ev.n)
                ks = s_K[hb * 64:(hb + 1) * 64, :]
                qs_ = s_Q[hb * 64:(hb + 1) * 64, :]
                vs_ = s_V[:, hb * SC * 65:(hb + 1) * SC * 65]
                ins = e.tensor_copy(s_Kh[hb * 64:(hb + 1) * 64, :], ks)
                ev.inc(ins)
                ins = e.tensor_copy(s_Qh[hb * 64:(hb + 1) * 64, :], qs_)
                ev.inc(ins)
                ins = e.tensor_copy(s_Vh[:, hb * SC * 65:(hb + 1) * SC * 65], vs_)
                ev.inc(ins)
                e.wait_ge(ev.h, ev.n)
                ins = e.tensor_tensor(s_Kl[hb * 64:(hb + 1) * 64, :], ks,
                                      s_Kh[hb * 64:(hb + 1) * 64, :], ALU.subtract)
                ev.inc(ins)
                ins = e.tensor_tensor(s_Ql[hb * 64:(hb + 1) * 64, :], qs_,
                                      s_Qh[hb * 64:(hb + 1) * 64, :], ALU.subtract)
                ev.inc(ins)
                ins = e.tensor_tensor(s_Vl[:, hb * SC * 65:(hb + 1) * SC * 65], vs_,
                                      s_Vh[:, hb * SC * 65:(hb + 1) * SC * 65],
                                      ALU.subtract)
                split_toks[hb] = ev.inc(ins)
            block.vector(_)

        def emit_slot(h, k):
            ek = (k + 1) * TAIL
            nkc128 = ek // 128
            hb = h % 2
            cbank = ctx_banks[k % 2]
            npairs = nkc128 // 2
            pv_jobs = []
            for p in range(npairs):
                kcs = [2 * p, 2 * p + 1]
                banks = [B[kc % 4] for kc in kcs]

                def _(e, kcs=kcs, banks=banks, hb=hb, k=k, h=h, nkc128=nkc128):
                    for b_ in banks:
                        W(e, b_.free)
                    if kcs[0] == 0 and k == 0:
                        W(e, split_toks[hb])
                        if h == 0:
                            W(e, mask_tok[0])
                    for kc, b_ in zip(kcs, banks):
                        qh = s_Qh[hb * 64:(hb + 1) * 64, k * QS: k * QS + 512]
                        ql = s_Ql[hb * 64:(hb + 1) * 64, k * QS: k * QS + 512]
                        kh = s_Kh[hb * 64:(hb + 1) * 64, kc * 128: kc * 128 + 128]
                        kl = s_Kl[hb * 64:(hb + 1) * 64, kc * 128: kc * 128 + 128]
                        tailc = kc >= nkc128 - 8
                        e.matmul(b_.ps[:], kh, qh, start=True, stop=False)
                        e.matmul(b_.ps[:], kh, ql, start=False, stop=False)
                        ins = e.matmul(b_.ps[:], kl, qh, start=False,
                                       stop=not tailc)
                        if tailc:
                            tc = kc - (nkc128 - 8)
                            ins = e.matmul(b_.ps[:], s_idb[:],
                                           s_maskT[:, tc * QS: tc * QS + 512],
                                           start=False, stop=True)
                        b_.ready = pe.inc(ins)
                block.tensor(_)

                for kc, b_ in zip(kcs, banks):
                    ps_ = kc % 8

                    def _(e, kc=kc, b_=b_, ps_=ps_):
                        W(e, b_.ready)
                        W(e, ptf_free[ps_])
                        ins = e.activation(s_ptf[:, ps_ * 512: ps_ * 512 + 512],
                                           b_.ps[:], AF.Exp, bias=0.0, scale=ISQ)
                        b_.free = ex.inc(ins)
                    block.scalar(_)
                    exp_tok = ex.tok()

                    def _(e, ps_=ps_, exp_tok=exp_tok):
                        W(e, exp_tok)
                        W(e, pt_free[ps_])
                        pf = s_ptf[:, ps_ * 512: ps_ * 512 + 512]
                        ph = s_pth[:, ps_ * 512: ps_ * 512 + 512]
                        ins = e.tensor_copy(ph, pf)
                        ev.inc(ins)
                        e.wait_ge(ev.h, ev.n)
                        ins = e.tensor_tensor(s_ptl[:, ps_ * 512: ps_ * 512 + 512],
                                              pf, ph, ALU.subtract)
                        ev.inc(ins)
                    block.vector(_)
                    ptf_free[ps_] = ev.tok()

                split_pt = ev.tok()

                def pv_(e, kcs=kcs, hb=hb, split_pt=split_pt, cbank=cbank,
                        nkc128=nkc128, h=h, k=k):
                    W(e, split_pt)
                    if kcs[0] == 0:
                        W(e, cbank.free)
                    for kc in kcs:
                        ps_ = kc % 8
                        vh = s_Vh[:, hb * SC * 65 + kc * 65: hb * SC * 65 + kc * 65 + 65]
                        vl = s_Vl[:, hb * SC * 65 + kc * 65: hb * SC * 65 + kc * 65 + 65]
                        ph = s_pth[:, ps_ * 512: ps_ * 512 + 512]
                        pl = s_ptl[:, ps_ * 512: ps_ * 512 + 512]
                        last = kc == nkc128 - 1
                        e.matmul(cbank.ps[0:65, 0:512], vh, ph,
                                 start=(kc == 0), stop=False)
                        e.matmul(cbank.ps[0:65, 0:512], vh, pl,
                                 start=False, stop=False)
                        ins = e.matmul(cbank.ps[0:65, 0:512], vl, ph,
                                       start=False, stop=last)
                        t = pe.inc(ins)
                        pt_free[ps_] = pe.tok()
                        if last:
                            cbank.ready = t
                pv_jobs.append(pv_)
                if p >= 2:
                    block.tensor(pv_jobs[p - 2])
            block.tensor(pv_jobs[npairs - 2])
            block.tensor(pv_jobs[npairs - 1])

            def _(e, cbank=cbank):
                W(e, cbank.ready)
                ins = e.tensor_copy(s_rt[0:1, :], cbank.ps[64:65, 0:512])
                ev.inc(ins)
                e.wait_ge(ev.h, ev.n)
                ins = e.reciprocal(s_rt[0:1, :], s_rt[0:1, :])
                ev.inc(ins)
            block.vector(_)
            rt_tok = ev.tok()

            def _(e, rt_tok=rt_tok):
                W(e, rt_tok)
                W(e, bc_bank.free)
                ins = e.matmul(bc_bank.ps[0:64, 0:512], s_1r[0:1, 0:64],
                               s_rt[0:1, :], start=True, stop=True)
                bc_bank.ready = pe.inc(ins)
            block.tensor(_)

            def _(e):
                W(e, bc_bank.ready)
                W(e, rb_free[0])
                ins = e.tensor_copy(s_rb[:], bc_bank.ps[0:64, 0:512])
                bc_bank.free = ev.inc(ins)
            block.vector(_)
            rb_tok = ev.tok()

            def _(e, h=h, k=k, cbank=cbank, rb_tok=rb_tok):
                W(e, cbank.ready)
                W(e, rb_tok)
                ins = e.tensor_tensor(
                    s_ctx[(h % 2) * 64:(h % 2) * 64 + 64,
                          (h // 2) * OWN + k * QS:(h // 2) * OWN + k * QS + QS],
                    cbank.ps[0:64, 0:QS], s_rb[:], ALU.mult)
                cbank.free = ev.inc(ins)
            block.vector(_)
            rb_free[0] = ev.tok()

        for h in range(H):
            emit_kvq_load(h)
            for k in range(NSLOT):
                emit_slot(h, k)
            head_end[h % 2] = pe.tok()

    close_scope(phB)

    def emit_ln(block, src, tmp, dst, s_sq, s_row, pre_tok):
        bsum, bsq, bbc1, bbc2 = B[0], B[1], B[2], B[3]
        sq_toks = []
        sqmm_toks = []

        for ofc in range(EC):
            def _(e, ofc=ofc):
                if ofc == 0:
                    W(e, pre_tok)
                if ofc >= 2:
                    W(e, sqmm_toks[ofc - 2])
                ins = e.activation(s_sq[:, (ofc % 2) * 512:(ofc % 2) * 512 + 512],
                                   src(ofc), AF.Square, bias=0.0, scale=1.0)
                sq_toks.append(ex.inc(ins))
            block.scalar(_)

            def _(e, ofc=ofc):
                if ofc == 0:
                    W(e, pre_tok)
                    W(e, bsum.free)
                    W(e, bsq.free)
                ins = e.matmul(bsum.ps[0:1, 0:512], s_1c[:], src(ofc),
                               start=(ofc == 0), stop=(ofc == EC - 1))
                if ofc == EC - 1:
                    bsum.ready = pe.inc(ins)
                W(e, sq_toks[ofc])
                ins = e.matmul(bsq.ps[0:1, 0:512], s_1c[:],
                               s_sq[:, (ofc % 2) * 512:(ofc % 2) * 512 + 512],
                               start=(ofc == 0), stop=(ofc == EC - 1))
                sqmm_toks.append(pe.inc(ins))
            block.tensor(_)
        bsq.ready = sqmm_toks[-1]

        r = lambda i: s_row[0:1, i * 512:(i + 1) * 512]

        def _(e):
            W(e, bsum.ready)
            ins = e.tensor_copy(r(0), bsum.ps[0:1, 0:512])
            bsum.free = ev.inc(ins)
            W(e, bsq.ready)
            ins = e.tensor_copy(r(1), bsq.ps[0:1, 0:512])
            bsq.free = ev.inc(ins)
            e.wait_ge(ev.h, ev.n)
            ins = e.tensor_scalar(r(2), r(0), 1.0 / E, None, ALU.mult)
            ev.inc(ins)
            ins = e.tensor_scalar(r(3), r(1), 1.0 / E, None, ALU.mult)
            ev.inc(ins)
            e.wait_ge(ev.h, ev.n)
            ins = e.tensor_tensor(r(4), r(2), r(2), ALU.mult)
            ev.inc(ins)
            e.wait_ge(ev.h, ev.n)
            ins = e.tensor_tensor(r(4), r(3), r(4), ALU.subtract)
            ev.inc(ins)
        block.vector(_)
        var_tok = ev.tok()

        def _(e, var_tok=var_tok):
            W(e, var_tok)
            ins = e.activation(r(3), r(4), AF.Sqrt, bias=EPS, scale=1.0)
            ex.inc(ins)
        block.scalar(_)
        sd_tok = ex.tok()

        def _(e, sd_tok=sd_tok):
            W(e, sd_tok)
            ins = e.reciprocal(r(5), r(3))
            ev.inc(ins)
            e.wait_ge(ev.h, ev.n)
            ins = e.tensor_tensor(r(4), r(2), r(5), ALU.mult)
            ev.inc(ins)
        block.vector(_)
        nmr_tok = ev.tok()

        def _(e, nmr_tok=nmr_tok):
            W(e, nmr_tok)
            W(e, bbc1.free)
            ins = e.matmul(bbc1.ps[:, 0:512], s_1r[:], r(5), start=True, stop=True)
            bbc1.ready = pe.inc(ins)
            W(e, bbc2.free)
            ins = e.matmul(bbc2.ps[:, 0:512], s_1r[:], r(4), start=True, stop=True)
            bbc2.ready = pe.inc(ins)
        block.tensor(_)

        def _(e):
            W(e, bbc1.ready)
            W(e, bbc2.ready)
            for ofc in range(EC):
                ins = e.tensor_tensor(tmp(ofc), src(ofc), bbc1.ps[:, 0:512], ALU.mult)
                if ofc == EC - 1:
                    bbc1.free = ev.inc(ins)
            e.wait_ge(ev.h, ev.n)
            for ofc in range(EC):
                ins = e.tensor_tensor(dst(ofc), tmp(ofc), bbc2.ps[:, 0:512], ALU.subtract)
                if ofc == EC - 1:
                    bbc2.free = ev.inc(ins)
        block.vector(_)
        return ev.tok()

    phB2 = []
    s_woT = SBuf(phB2, "s_woT", [128, EC * E])
    s_xo = SBuf(phB2, "s_xo", [128, EC * 512])
    s_xtm2 = SBuf(phB2, "s_xtm2", [128, 4 * E])
    s_t1 = SBuf(phB2, "s_t1", [128, EC * 512])
    s_sqA = SBuf(phB2, "s_sqA", [128, 2 * 512])
    s_rowA = SBuf(phB2, "s_rowA", [1, 6 * 512])
    s_xtk = SBuf(phB2, "s_xtk", [128, 2 * E])

    with nc.named_scope("phaseB2"), nc.Block() as block:
        xtm2_free = [None]
        for rr in range(8):
            tok_holder = []

            def _(e, rr=rr):
                W(e, xtm2_free[0])
                tok_holder.append(dma(e, ld, s_xtm2[:, 0:E],
                                      opw[rr * 128:(rr + 1) * 128, :]))
            block.sync(_)
            jobs = []
            for cfc in range(EC):
                src = s_xtm2[:, cfc * 128:cfc * 128 + 128]
                dst = s_woT[:, cfc * E + rr * 128: cfc * E + rr * 128 + 128]
                jobs.append((src, dst, s_id[:]))
            emit_transposes(block, jobs, [B[0], B[1]], pre_toks=(tok_holder[0],))
            xtm2_free[0] = pe.tok()
        woT_tok = ev.tok()

        xtk_free = [None, None]
        x1dump_tok = [None]
        for nt in range(4):
            tok_holder = []
            xo_prev = ev.tok()

            def _(e, nt=nt):
                W(e, xtm2_free[0])
                tok_holder.append(dma(
                    e, ld,
                    s_xtm2[:].rearrange("p (t d) -> p t d", d=E),
                    x_own[nt * 512:(nt + 1) * 512, :].rearrange("(t p) d -> p t d", p=128)))
            block.sync(_)
            jobs = []
            for j in range(4 * EC):
                t, ifc = divmod(j, EC)
                srcj = s_xtm2[:, t * E + ifc * 128: t * E + ifc * 128 + 128]
                dstj = s_xo[:, ifc * 512 + t * 128: ifc * 512 + t * 128 + 128]
                jobs.append((srcj, dstj, s_id[:]))
            emit_transposes(block, jobs, [B[2], B[3]],
                            pre_toks=(tok_holder[0],), evac_pre_toks=(xo_prev,))
            xtm2_free[0] = pe.tok()
            xo_tok = ev.tok()

            evac_toks = []
            for ofm in range(EC):
                gb = B[4 + ofm % 2]

                def _(e, ofm=ofm, gb=gb, nt=nt):
                    W(e, gb.free)
                    if nt == 0 and ofm == 0:
                        W(e, woT_tok)
                    for cfc in range(EC):
                        lhs = s_woT[:, cfc * E + ofm * 128: cfc * E + ofm * 128 + 128]
                        rhs = s_ctx[:, cfc * OWN + nt * 512: cfc * OWN + nt * 512 + 512]
                        ins = e.matmul(gb.ps[:], lhs, rhs,
                                       start=(cfc == 0), stop=(cfc == EC - 1))
                    gb.ready = pe.inc(ins)
                block.tensor(_)

                def _(e, ofm=ofm, gb=gb):
                    W(e, gb.ready)
                    ins = e.activation(s_t1[:, ofm * 512: ofm * 512 + 512],
                                       gb.ps[:], AF.Identity,
                                       bias=s_bo[:, ofm:ofm + 1], scale=1.0)
                    gb.free = ex.inc(ins)
                block.scalar(_)
                evac_toks.append(ex.tok())

            def _(e, nt=nt, et=evac_toks[-1]):
                W(e, et)
                W(e, xo_tok)
                for ofc in range(EC):
                    ins = e.tensor_tensor(
                        s_ctx[:, ofc * OWN + nt * 512: ofc * OWN + nt * 512 + 512],
                        s_xo[:, ofc * 512: ofc * 512 + 512],
                        s_t1[:, ofc * 512: ofc * 512 + 512], ALU.add)
                ev.inc(ins)
            block.vector(_)
            add_tok = ev.tok()

            src = lambda ofc, nt=nt: s_ctx[:, ofc * OWN + nt * 512: ofc * OWN + nt * 512 + 512]
            tmp = lambda ofc, nt=nt: s_xo[:, ofc * 512: ofc * 512 + 512]
            ln_tok = emit_ln(block, src, tmp, src, s_sqA, s_rowA, add_tok)

            def _(e, nt=nt, ln_tok=ln_tok):
                W(e, B[6].free)
                W(e, ln_tok)
                for cfc in range(EC):
                    ins = e.matmul(
                        B[6].ps[0:EXP, 0:512],
                        s_wg[:, cfc * EXP:(cfc + 1) * EXP],
                        s_ctx[:, cfc * OWN + nt * 512: cfc * OWN + nt * 512 + 512],
                        start=(cfc == 0), stop=(cfc == EC - 1))
                B[6].ready = pe.inc(ins)
            block.tensor(_)

            def _(e, nt=nt):
                W(e, B[6].ready)
                ins = e.activation(s_gl[0:EXP, nt * 512: nt * 512 + 512],
                                   B[6].ps[0:EXP, 0:512], AF.Identity,
                                   bias=s_bg[:], scale=1.0)
                B[6].free = ex.inc(ins)
            block.scalar(_)

            for tq in range(4):
                jobs = []
                for fc in range(EC):
                    srcj = s_ctx[:, fc * OWN + nt * 512 + tq * 128:
                                 fc * OWN + nt * 512 + tq * 128 + 128]
                    dstj = s_xtk[:, (tq % 2) * E + fc * 128:
                                 (tq % 2) * E + fc * 128 + 128]
                    jobs.append((srcj, dstj, s_id[:]))
                pre = (ln_tok,) if tq == 0 else ()
                xt_tok = emit_transposes(block, jobs, [B[0], B[1]],
                                         pre_toks=pre,
                                         evac_pre_toks=(xtk_free[tq % 2],))

                def _(e, nt=nt, tq=tq, xt_tok=xt_tok):
                    W(e, xt_tok)
                    row = nt * 512 + tq * 128
                    xtk_free[tq % 2] = dma(
                        e, sw, X1tok[row:row + 128, :],
                        s_xtk[:, (tq % 2) * E:(tq % 2) * E + E])
                    if nt == 3 and tq == 3:
                        toks = [xtk_free[tq % 2]]
                        for k in range(4):
                            toks.append(dma(
                                e, sw, X1tok[2048 + k * 128: 2176 + k * 128, :],
                                s_xtk[:, (tq % 2) * E:(tq % 2) * E + E]))
                        x1dump_tok[0] = tuple(toks)
                block.gpsimd(_)

        def _(e):
            dma_drain(e, sw)
        block.gpsimd(_)

    close_scope(phB2)
    close_scope(phBB2)

    phC = []
    CAP = 512
    s_gt = SBuf(phC, "s_gt", [128, 16 * EXP])
    s_gst = SBuf(phC, "s_gst", [128, 64])
    s_mf = SBuf(phC, "s_mf", [EXP, OWN])
    s_row1 = SBuf(phC, "s_row1", [1, OWN])
    s_u = SBuf(phC, "s_u", [128, 128], BF16)
    s_tid = SBuf(phC, "s_tid", [128, 64], U32)
    s_dmp = SBuf(phC, "s_dmp", [128, 16])
    s_pfl = SBuf(phC, "s_pfl", [128, 16], U32)
    s_pfi = SBuf(phC, "s_pfi", [128, 4], U32)
    s_i16 = SBuf(phC, "s_i16", [128, 4 * 16])
    s_slot = SBuf(phC, "s_slot", [128, 16], U32)
    s_indc = SBuf(phC, "s_indc", [128, 128])
    s_indcb = SBuf(phC, "s_indcb", [128, 128], BF16)
    s_idxall = SBuf(phC, "s_idxall", [128, 16 * EXP], U32)
    s_ws = SBuf(phC, "s_ws", [128, 16])
    s_xg = SBuf(phC, "s_xg", [128, 4 * E])
    s_xf = SBuf(phC, "s_xf", [128, EC * CAP])
    s_xfh = SBuf(phC, "s_xfh", [128, EC * CAP], BF16)
    s_xfl = SBuf(phC, "s_xfl", [128, EC * CAP], BF16)
    s_hf = SBuf(phC, "s_hf", [128, 4 * 512])
    s_ha = SBuf(phC, "s_ha", [128, 2 * 4 * 512], BF16)
    s_hb = SBuf(phC, "s_hb", [128, 2 * 4 * 512], BF16)
    s_wst = SBuf(phC, "s_wst", [128, 4096])
    s_w1a = SBuf(phC, "s_w1a", [128, 2 * EC * 512], BF16)
    s_w1b = SBuf(phC, "s_w1b", [128, 2 * EC * 512], BF16)
    s_w2a = SBuf(phC, "s_w2a", [128, 2 * 4 * 1024], BF16)
    s_w2b = SBuf(phC, "s_w2b", [128, 2 * 4 * 1024], BF16)
    s_yT = SBuf(phC, "s_yT", [128, 4 * E])
    s_cells = SBuf(phC, "s_cells", [128, 32])

    s_b2r = s_row1

    with nc.named_scope("phaseC"), nc.Block() as block:
        fB2 = (pe.tok(), ev.tok(), ex.tok())
        cst_tok = [None]

        def _(e):
            W(e, x1dump_tok[0])
            t1 = dma(e, ld, s_u[:], u128)
            t2 = dma(e, ld, s_tid[:], tid4)
            t3 = dma(e, ld, s_dmp[:], dmpc)
            t4 = dma(e, ld, s_pfl[:], pflc)
            t5 = dma(e, ld, s_pfi[:], pfic)
            cst_tok[0] = (t1, t2, t3, t4, t5)
        block.sync(_)
        consts_tok = cst_tok[0]

        slabs = [(eix, hg) for eix in range(EXP) for hg in range(8)]
        _cfence = tuple([fB2] + [(ch.h, ch.n) for ch in sw.chans]
                        + [(ch.h, ch.n) for ch in ld.chans])
        stage_tok = [None, None]
        stage_free = [_cfence, _cfence]
        split_tok = [None, None]
        slabbuf_free = [None, None]
        h_free = [None, None]
        hf_free = [None]
        xg_free = [None]
        xg_tok = [None]
        xf_free = [None]
        xsp_free = [None]
        xsp_tok = [None]
        yT_free = [None]
        b2r_free = [None]
        scat_toks = []

        def emit_slab_load_split(si):
            buf = si % 2
            eix, hg = slabs[si]
            for ci in range(4):
                part, sub = ci // 2, ci % 2
                rs = ci % 2
                stg = s_wst[:, rs * 2048:(rs + 1) * 2048]

                def _(e, part=part, sub=sub, rs=rs, eix=eix, hg=hg, stg=stg):
                    if stage_free[rs] is not None:
                        W(e, stage_free[rs])
                    if part == 0:
                        sl = w1[eix][sub * 512:(sub + 1) * 512,
                                     hg * 512:(hg + 1) * 512].rearrange(
                            "(c p) h -> p c h", p=128)
                        stage_tok[rs] = dma(
                            e, ld, stg.rearrange("p (c h) -> p c h", h=512), sl)
                    else:
                        sl = w2[eix][hg * 512 + sub * 256:
                                     hg * 512 + (sub + 1) * 256, :].rearrange(
                            "(c p) o -> p c o", p=128)
                        stage_tok[rs] = dma(
                            e, ld, stg.rearrange("p (c o) -> p c o", o=E), sl)
                block.sync(_)

                def _(e, buf=buf, part=part, sub=sub, rs=rs, stg=stg):
                    W(e, stage_tok[rs])
                    if part == 0 and sub == 0 and slabbuf_free[buf] is not None:
                        for tk in (slabbuf_free[buf]
                                   if isinstance(slabbuf_free[buf], list)
                                   else [slabbuf_free[buf]]):
                            W(e, tk)
                    wa = s_w1a if part == 0 else s_w2a
                    wb = s_w1b if part == 0 else s_w2b
                    off = buf * 4096 + sub * 2048
                    ins = e.tensor_copy(wa[:, off: off + 2048], stg)
                    ev.inc(ins)
                    e.wait_ge(ev.h, ev.n)
                    ins = e.tensor_tensor(wb[:, off: off + 2048], stg,
                                          wa[:, off: off + 2048],
                                          ALU.subtract)
                    t = ev.inc(ins)
                    if part == 1 and sub == 1:
                        split_tok[buf] = t
                    stage_free[rs] = ev.tok()
                block.vector(_)


        emit_slab_load_split(0)
        emit_slab_load_split(1)
        preloaded = [0, 1]

        jobs = []
        for ti in range(16):
            jobs.append((s_gl[0:EXP, ti * 128:(ti + 1) * 128],
                         s_gt[:, ti * EXP:(ti + 1) * EXP], s_id[0:EXP, 0:EXP]))
        gt_tok = emit_transposes(block, jobs, [B[2], B[3]], pre_toks=fB2)

        def _(e, gt_tok=gt_tok):
            W(e, gt_tok)
            for ti in range(16):
                ins = e.reduce_max(s_gst[:, ti * 4: ti * 4 + 1],
                                   s_gt[:, ti * EXP:(ti + 1) * EXP], axis=AXX)
                ev.inc(ins)
            e.wait_ge(ev.h, ev.n)
            for ti in range(16):
                ins = e.tensor_scalar(s_gst[:, ti * 4 + 1: ti * 4 + 2],
                                      s_gst[:, ti * 4: ti * 4 + 1],
                                      -1.0, None, ALU.mult)
                ev.inc(ins)
        block.vector(_)
        gnm_tok = ev.tok()

        def _(e, gnm_tok=gnm_tok):
            W(e, gnm_tok)
            for ti in range(16):
                ins = e.activation(s_gt[:, ti * EXP: ti * EXP + EXP],
                                   s_gt[:, ti * EXP:(ti + 1) * EXP],
                                   AF.Exp, bias=s_gst[:, ti * 4 + 1: ti * 4 + 2],
                                   scale=1.0,
                                   accum_out=s_gst[:, ti * 4 + 2: ti * 4 + 3])
                ex.inc(ins)
        block.scalar(_)
        gs_tok = ex.tok()

        def _(e, gs_tok=gs_tok):
            W(e, gs_tok)
            for ti in range(16):
                ins = e.reciprocal(s_gst[:, ti * 4 + 3: ti * 4 + 4],
                                   s_gst[:, ti * 4 + 2: ti * 4 + 3])
                ev.inc(ins)
            e.wait_ge(ev.h, ev.n)
            for ti in range(16):
                ins = e.tensor_scalar(s_gt[:, ti * EXP:(ti + 1) * EXP],
                                      s_gt[:, ti * EXP:(ti + 1) * EXP],
                                      1.0, s_gst[:, ti * 4 + 3: ti * 4 + 4],
                                      ALU.is_equal, ALU.mult)
                ev.inc(ins)
        block.vector(_)
        mt_tok = ev.tok()

        jobs = []
        for ti in range(16):
            jobs.append((s_gt[:, ti * EXP:(ti + 1) * EXP],
                         s_mf[0:EXP, ti * 128:(ti + 1) * 128], s_id[:]))
        mf_tok = emit_transposes(block, jobs, [B[2], B[3]], pre_toks=(mt_tok,))

        for c in range(4):
            wb = B[c % 2]

            def _(e, c=c, wb=wb, mf_tok=mf_tok):
                W(e, wb.free)
                if c == 0:
                    W(e, mf_tok)
                ins = e.matmul(wb.ps[0:1, 0:512], s_1c[0:EXP, 0:1],
                               s_mf[0:EXP, c * 512:(c + 1) * 512],
                               start=True, stop=True)
                wb.ready = pe.inc(ins)
            block.tensor(_)

            def _(e, c=c, wb=wb):
                W(e, wb.ready)
                ins = e.tensor_copy(s_row1[0:1, c * 512:(c + 1) * 512],
                                    wb.ps[0:1, 0:512])
                wb.free = ev.inc(ins)
            block.vector(_)

        def _(e):
            ins = e.tensor_scalar(s_mf[0:EXP, :], s_mf[0:EXP, :], 0.0, None,
                                  ALU.is_gt)
            ev.inc(ins)
        block.vector(_)
        ws_ind_tok = ev.tok()

        jobs = []
        for c in range(16):
            jobs.append((s_mf[0:EXP, c * 128:(c + 1) * 128],
                         s_indc[:, c * EXP:(c + 1) * EXP], s_id[0:EXP, 0:EXP]))
        for c in range(16):
            jobs.append((s_row1[0:1, c * 128:(c + 1) * 128],
                         s_ws[:, c:c + 1], s_id[0:1, 0:1]))
        red_tok = emit_transposes(block, jobs, [B[2], B[3]],
                                  pre_toks=(ws_ind_tok,))

        def _(e, t=red_tok):
            W(e, t)
            ins = e.tensor_copy(s_indcb[:], s_indc[:])
            ev.inc(ins)
        block.vector(_)
        indb_tok = ev.tok()
        ws_tok = indb_tok

        idx_toks = []
        slot_tok = [None]
        slot_free = [None]
        row1_free = [None]
        for eix in range(EXP):
            for tc0 in range(0, 16, 2):
                def _(e, eix=eix, tc0=tc0, t=indb_tok):
                    for tc in (tc0, tc0 + 1):
                        b_ = B[tc % 2]
                        W(e, b_.free)
                        if tc == 0:
                            W(e, t)
                            W(e, row1_free[0])
                            if eix == 0:
                                W(e, consts_tok)
                        ins = e.matmul(b_.ps[0:1, 0:128],
                                       s_indcb[:, tc * EXP + eix: tc * EXP + eix + 1],
                                       s_u[:], start=True, stop=True)
                        b_.ready = pe.inc(ins)
                block.tensor(_)

                def _(e, tc0=tc0):
                    for tc in (tc0, tc0 + 1):
                        b_ = B[tc % 2]
                        W(e, b_.ready)
                        ins = e.tensor_copy(s_row1[0:1, tc * 128:(tc + 1) * 128],
                                            b_.ps[0:1, 0:128])
                        b_.free = ev.inc(ins)
                block.vector(_)
            pfx_tok = ev.tok()

            def _(e, t=pfx_tok):
                W(e, t)
                c = lambda i: s_cells[0:1, i:i + 1]
                ins = e.tensor_scalar(c(0), s_row1[0:1, 0:1], 0.0, None,
                                      ALU.mult)
                ev.inc(ins)
                for tc in range(1, 16):
                    e.wait_ge(ev.h, ev.n)
                    ins = e.tensor_tensor(c(tc), c(tc - 1),
                                          s_row1[0:1, tc * 128 - 1: tc * 128],
                                          ALU.add)
                    ev.inc(ins)
                e.wait_ge(ev.h, ev.n)
                for tc in range(16):
                    ins = e.tensor_scalar(s_row1[0:1, tc * 128:(tc + 1) * 128],
                                          s_row1[0:1, tc * 128:(tc + 1) * 128],
                                          c(tc), -1.0, ALU.add, ALU.add)
                    ev.inc(ins)
            block.vector(_)
            posg_tok = ev.tok()

            jobs = []
            for tc in range(16):
                jobs.append((s_row1[0:1, tc * 128:(tc + 1) * 128],
                             s_i16[:, 16 + tc: 17 + tc], s_id[0:1, 0:1]))
            posc_tok = emit_transposes(block, jobs, [B[2], B[3]],
                                       pre_toks=(posg_tok,),
                                       evac_pre_toks=(slot_tok[0],))
            row1_free[0] = pe.tok()

            def _(e, eix=eix, t=posc_tok):
                W(e, t)
                W(e, slot_free[0])
                indv = s_indc[:].rearrange("p (c e) -> p c e", e=EXP)[:, :, eix]
                ins = e.tensor_tensor(s_i16[:, 48:64], s_i16[:, 16:32],
                                      indv, ALU.mult)
                ev.inc(ins)
                ins = e.tensor_tensor(s_i16[:, 32:48], s_dmp[:],
                                      indv, ALU.mult)
                ev.inc(ins)
                e.wait_ge(ev.h, ev.n)
                ins = e.tensor_tensor(s_i16[:, 32:48], s_dmp[:],
                                      s_i16[:, 32:48], ALU.subtract)
                ev.inc(ins)
                e.wait_ge(ev.h, ev.n)
                ins = e.tensor_tensor(s_i16[:, 48:64], s_i16[:, 48:64],
                                      s_i16[:, 32:48], ALU.add)
                ev.inc(ins)
                e.wait_ge(ev.h, ev.n)
                ins = e.tensor_copy(s_slot[:], s_i16[:, 48:64])
                ev.inc(ins)
            block.vector(_)
            sl_tok = ev.tok()
            slot_tok[0] = sl_tok

            def _(e, eix=eix, t=sl_tok):
                if eix == 0:
                    W(e, cst_tok[0])
                for j in range(4):
                    if sw.chans[0].n:
                        e.wait_ge(sw.chans[0].h, sw.chans[0].n)
                    ins = e.indirect_dma_start(
                        IdxH, IOA(s_pfi[:, j:j + 1], 0),
                        s_pfl[:, j * 4:(j + 1) * 4],
                        None, element_offset=eix * 2560 * 4)
                    sw.chans[0].n += 16
                    ins.then_inc(sw.chans[0].h, 16)
                W(e, t)
                for tc in range(16):
                    e.wait_ge(sw.chans[0].h, sw.chans[0].n)
                    ins = e.indirect_dma_start(
                        IdxH, IOA(s_slot[:, tc:tc + 1], 0),
                        s_tid[:, tc * 4:(tc + 1) * 4],
                        None, element_offset=eix * 2560 * 4)
                    sw.chans[0].n += 16
                    ins.then_inc(sw.chans[0].h, 16)
                e.wait_ge(sw.chans[0].h, sw.chans[0].n)
                t1 = dma(e, sw, s_idxall[:, eix * 16:(eix + 1) * 16],
                         IdxH[eix * 2560: eix * 2560 + CAP, :].rearrange(
                             "(p c) x -> p (c x)", p=128))
                idx_toks.append(t1)
                slot_free[0] = (sw.chans[0].h, sw.chans[0].n)
            block.gpsimd(_)

        def emit_gather(eix):
            def _(e, eix=eix):
                W(e, idx_toks[eix])
                W(e, xg_free[0])
                for c in range(4):
                    if sw.chans[1].n:
                        e.wait_ge(sw.chans[1].h, sw.chans[1].n)
                    ins = e.indirect_dma_start(
                        s_xg[:, c * E:(c + 1) * E],
                        None, X1tok,
                        IOA(s_idxall[:, eix * 16 + c * 4: eix * 16 + c * 4 + 1], 0))
                    sw.chans[1].n += 16
                    ins.then_inc(sw.chans[1].h, 16)
                xg_tok[0] = (sw.chans[1].h, sw.chans[1].n)
            block.gpsimd(_)

        def emit_xsplit(eix):
            jobs = []
            for c in range(4):
                for fj in range(EC):
                    srcj = s_xg[:, c * E + fj * 128: c * E + fj * 128 + 128]
                    dstj = s_xf[:, fj * CAP + c * 128: fj * CAP + c * 128 + 128]
                    jobs.append((srcj, dstj, s_id[:]))
            xf_tok = emit_transposes(block, jobs, [B[2], B[3]],
                                     pre_toks=(xg_tok[0],),
                                     evac_pre_toks=(xf_free[0],))
            xg_free[0] = pe.tok()

            def _(e, eix=eix, t=xf_tok):
                W(e, t)
                W(e, xsp_free[0])
                ins = e.tensor_copy(s_xfh[:], s_xf[:])
                ev.inc(ins)
                e.wait_ge(ev.h, ev.n)
                ins = e.tensor_tensor(s_xfl[:], s_xf[:], s_xfh[:], ALU.subtract)
                ev.inc(ins)
            block.vector(_)
            xf_free[0] = ev.tok()
            xsp_tok[0] = ev.tok()

        emit_gather(0)
        emit_xsplit(0)
        for si, (eix, hg) in enumerate(slabs):
            buf = si % 2
            if si + 1 < len(slabs) and si + 1 not in preloaded:
                emit_slab_load_split(si + 1)
            my_split = split_tok[buf]
            if hg == 0:
                def _(e, eix=eix):
                    W(e, b2r_free[0])
                    if eix == 0:
                        W(e, idx_toks[EXP - 1])
                    cst_tok[0] = dma(e, ld, s_b2r[0:1, 0:E], b2[eix:eix + 1, :])
                block.sync(_)
                b2r_tok = cst_tok[0]
            hbuf = si % 2
            h_toks = []
            for hm in range(4):
                gb = B[4 + hm % 2]

                def _(e, hm=hm, gb=gb, buf=buf, st_=my_split, eix=eix,
                      hg=hg):
                    W(e, gb.free)
                    if hm == 0:
                        W(e, st_)
                        if hg == 0:
                            W(e, xsp_tok[0])
                    first = True
                    for dc in range(EC):
                        woff = buf * 4096 + dc * 512 + hm * 128
                        xoff = dc * CAP
                        for (w_, x_) in ((s_w1a, s_xfh), (s_w1b, s_xfh),
                                         (s_w1a, s_xfl)):
                            ins = e.matmul(
                                gb.ps[:], w_[:, woff: woff + 128],
                                x_[:, xoff: xoff + CAP],
                                start=first,
                                stop=(dc == EC - 1 and x_ is s_xfl))
                            first = False
                    gb.ready = pe.inc(ins)
                block.tensor(_)

                def _(e, eix=eix, hg=hg, hm=hm, gb=gb):
                    W(e, gb.ready)
                    if hm == 0:
                        W(e, h_free[hbuf])
                        W(e, hf_free[0])
                    col = eix * HC + hg * 4 + hm
                    ins = e.activation(
                        s_hf[:, hm * 512: hm * 512 + 512], gb.ps[:],
                        AF.Silu, bias=s_b1[:, col:col + 1], scale=1.0)
                    gb.free = ex.inc(ins)
                block.scalar(_)
                h_toks.append(ex.tok())

            def _(e, hbuf=hbuf, ht=h_toks):
                W(e, ht[-1])
                for hm in range(4):
                    ins = e.tensor_copy(
                        s_ha[:, hbuf * 2048 + hm * 512: hbuf * 2048 + hm * 512 + 512],
                        s_hf[:, hm * 512: hm * 512 + 512])
                    ev.inc(ins)
                e.wait_ge(ev.h, ev.n)
                for hm in range(4):
                    ins = e.tensor_tensor(
                        s_hb[:, hbuf * 2048 + hm * 512: hbuf * 2048 + hm * 512 + 512],
                        s_hf[:, hm * 512: hm * 512 + 512],
                        s_ha[:, hbuf * 2048 + hm * 512: hbuf * 2048 + hm * 512 + 512],
                        ALU.subtract)
                    ev.inc(ins)
            block.vector(_)
            hsp_tok = ev.tok()
            hf_free[0] = hsp_tok

            for yt in range(8):
                qm, fg = divmod(yt, 2)
                gb = B[6 + yt % 2]

                def _(e, qm=qm, fg=fg, gb=gb, ht=hsp_tok, buf=buf, hbuf=hbuf,
                      hg=hg, b2t=(b2r_tok if hg == 0 else None)):
                    W(e, gb.free)
                    if qm == 0 and fg == 0:
                        W(e, ht)
                        if hg == 0:
                            W(e, b2t)
                    first = True
                    mms = []
                    for hc in range(4):
                        woff = buf * 4096 + hc * 1024 + fg * 512
                        hoff = hbuf * 2048 + hc * 512 + qm * 128
                        for (h_, w_) in ((s_ha, s_w2a), (s_ha, s_w2b),
                                         (s_hb, s_w2a)):
                            mms.append((h_[:, hoff: hoff + 128],
                                        w_[:, woff: woff + 512]))
                    for i, (lh, rh) in enumerate(mms):
                        is_last = (i == len(mms) - 1) and hg != 0
                        ins = e.matmul(gb.ps[:], lh, rh, start=first,
                                       stop=is_last)
                        first = False
                    if hg == 0:
                        ins = e.matmul(gb.ps[:], s_1r[0:1, 0:128],
                                       s_b2r[0:1, fg * 512: fg * 512 + 512],
                                       start=False, stop=True)
                    gb.ready = pe.inc(ins)
                block.tensor(_)

                def _(e, qm=qm, fg=fg, gb=gb, hg=hg):
                    W(e, gb.ready)
                    if qm == 0 and fg == 0 and hg == 0:
                        W(e, yT_free[0])
                    dst = s_yT[:, qm * E + fg * 512: qm * E + fg * 512 + 512]
                    if hg == 0:
                        ins = e.tensor_copy(dst, gb.ps[:])
                    else:
                        ins = e.tensor_tensor(dst, dst, gb.ps[:], ALU.add)
                    gb.free = ev.inc(ins)
                block.vector(_)
            h_free[hbuf] = pe.tok()
            slabbuf_free[buf] = pe.tok()

            if hg == 7:
                yev_tok = ev.tok()

                def _(e, eix=eix, t=yev_tok):
                    W(e, t)
                    for c in range(4):
                        if sw.chans[2].n:
                            e.wait_ge(sw.chans[2].h, sw.chans[2].n)
                        ins = e.indirect_dma_start(
                            Ytok,
                            IOA(s_idxall[:, eix * 16 + c * 4: eix * 16 + c * 4 + 1], 0),
                            s_yT[:, c * E:(c + 1) * E], None)
                        sw.chans[2].n += 16
                        ins.then_inc(sw.chans[2].h, 16)
                    scat_toks.append((sw.chans[2].h, sw.chans[2].n))
                block.gpsimd(_)
                yT_free[0] = scat_toks[-1]
                b2r_free[0] = pe.tok()
                if eix + 1 < EXP:
                    xsp_free[0] = pe.tok()
                    emit_gather(eix + 1)
                    emit_xsplit(eix + 1)

        def _(e):
            W(e, ws_tok)
            for t in scat_toks:
                W(e, t)
            ins = e.tensor_copy(s_cells[0:1, 30:31], s_ws[0:1, 0:1])
            ev.inc(ins)
        block.vector(_)
        asm_dve = ev.tok()

        at_free = [None] * 2
        ot_free = [None] * 2
        for tc in range(16):
            ab = tc % 2

            def _(e, tc=tc, ab=ab):
                if tc == 0:
                    for t in scat_toks:
                        W(e, t)
                W(e, at_free[ab])
                t1 = dma(e, ld, s_xg[:, ab * 2 * E: ab * 2 * E + E],
                         X1tok[tc * 128: tc * 128 + 128, :])
                t2 = dma(e, ld, s_xg[:, ab * 2 * E + E: ab * 2 * E + 2 * E],
                         Ytok[tc * 128: tc * 128 + 128, :])
                cst_tok[0] = (t1, t2)
            block.sync(_)
            a_tok = cst_tok[0]

            c = lambda i, tc=tc: s_cells[:, (tc % 4) * 8 + i: (tc % 4) * 8 + i + 1]

            def _(e, tc=tc, ab=ab, a_tok=a_tok, c=c):
                W(e, a_tok)
                if tc < 2:
                    W(e, asm_dve)
                xa = s_xg[:, ab * 2 * E: ab * 2 * E + E]
                ya = s_xg[:, ab * 2 * E + E: ab * 2 * E + 2 * E]
                ins = e.tensor_scalar(ya, ya, s_ws[:, tc:tc + 1], None, ALU.mult)
                ev.inc(ins)
                e.wait_ge(ev.h, ev.n)
                ins = e.tensor_tensor(xa, xa, ya, ALU.add)
                ev.inc(ins)
            block.vector(_)
            x2_tok = ev.tok()

            def _(e, tc=tc, ab=ab, x2_tok=x2_tok, c=c):
                W(e, x2_tok)
                if ex.n:
                    e.wait_ge(ex.h, ex.n)
                xa = s_xg[:, ab * 2 * E: ab * 2 * E + E]
                tr = s_xf[:, 0:E]
                ins = e.activation(tr, xa, AF.Identity, bias=0.0, scale=1.0,
                                   accum_out=c(0))
                ex.inc(ins)
                e.wait_ge(ex.h, ex.n)
                ins = e.activation(tr, xa, AF.Square, bias=0.0, scale=1.0,
                                   accum_out=c(1))
                ex.inc(ins)
            block.scalar(_)
            st_tok = ex.tok()

            def _(e, tc=tc, st_tok=st_tok, c=c):
                W(e, st_tok)
                ins = e.tensor_scalar(c(2), c(0), 1.0 / E, None, ALU.mult)
                ev.inc(ins)
                ins = e.tensor_scalar(c(3), c(1), 1.0 / E, None, ALU.mult)
                ev.inc(ins)
                e.wait_ge(ev.h, ev.n)
                ins = e.tensor_tensor(c(4), c(2), c(2), ALU.mult)
                ev.inc(ins)
                e.wait_ge(ev.h, ev.n)
                ins = e.tensor_tensor(c(4), c(3), c(4), ALU.subtract)
                ev.inc(ins)
            block.vector(_)
            var_tok = ev.tok()

            def _(e, tc=tc, var_tok=var_tok, c=c):
                W(e, var_tok)
                ins = e.activation(c(5), c(4), AF.Sqrt, bias=EPS, scale=1.0)
                ex.inc(ins)
            block.scalar(_)
            sq_tok = ex.tok()

            def _(e, tc=tc, ab=ab, sq_tok=sq_tok, c=c):
                W(e, sq_tok)
                ins = e.reciprocal(c(6), c(5))
                ev.inc(ins)
                ins = e.tensor_scalar(c(7), c(2), -1.0, None, ALU.mult)
                ev.inc(ins)
                e.wait_ge(ev.h, ev.n)
                W(e, ot_free[ab])
                xa = s_xg[:, ab * 2 * E: ab * 2 * E + E]
                ins = e.tensor_scalar(s_xf[:, (2 + ab) * E:(3 + ab) * E],
                                      xa, c(7), c(6), ALU.add, ALU.mult)
                ev.inc(ins)
            block.vector(_)
            o_tok = ev.tok()
            at_free[ab] = o_tok

            def _(e, tc=tc, ab=ab, o_tok=o_tok):
                W(e, o_tok)
                ot_free[ab] = dma(e, sw, out[tc * 128: tc * 128 + 128, :],
                                  s_xf[:, (2 + ab) * E:(3 + ab) * E])
            block.gpsimd(_)

        def _(e):
            dma_drain(e, sw)
        block.gpsimd(_)

    close_scope(phC)
    for b in reversed(B):
        b.cm.__exit__(None, None, None)
    close_scope(live)
    return nc


def _make_consts():
    identf = np.eye(128, dtype=np.float32)
    identbf = np.eye(128, dtype=np.float32).astype(ml_dtypes.bfloat16)
    onescol = np.ones((128, 1), dtype=np.float32)
    onesrow = np.ones((1, 128), dtype=np.float32)
    cvec = np.zeros((128, 4), dtype=np.float32)
    cvec[:, 0] = EPS
    selm = np.zeros((EXP, EXP * 128), dtype=np.float32)
    for e_ in range(EXP):
        selm[e_, e_ * 128:(e_ + 1) * 128] = 1.0
    return identf, identbf, onescol, onesrow, cvec, selm


def _make_sparse_consts():
    p = np.arange(128)[:, None]
    j = np.arange(128)[None, :]
    u128 = (p <= j).astype(np.float32).astype(ml_dtypes.bfloat16)
    tid4 = np.zeros((128, 64), np.uint32)
    for tc in range(16):
        for x in range(4):
            tid4[:, tc * 4 + x] = tc * 128 + np.arange(128)
    dmpc = np.zeros((128, 16), np.float32)
    for tc in range(16):
        dmpc[:, tc] = 512 + tc * 128 + np.arange(128)
    pflc = np.zeros((128, 16), np.uint32)
    for jj in range(4):
        for x in range(4):
            pflc[:, jj * 4 + x] = 2048 + jj * 128 + np.arange(128)
    pfic = np.zeros((128, 4), np.uint32)
    for jj in range(4):
        pfic[:, jj] = jj * 128 + np.arange(128)
    return (np.ascontiguousarray(u128), tid4, dmpc, pflc, pfic)


def _make_mask(par):
    i = np.arange(QS)[:, None]
    j = np.arange(TAIL)[None, :]
    m = np.where(j > par * 512 + i, NEG, 0.0).astype(np.float32)
    return np.ascontiguousarray(m.astype(ml_dtypes.bfloat16))


def _make_maskT(par):
    j = np.arange(TAIL)[:, None]
    i = np.arange(QS)[None, :]
    m = np.where(j > par * 512 + i, NEG, 0.0).astype(np.float32)
    return np.ascontiguousarray(m.astype(ml_dtypes.bfloat16))


def kernel(L0_attn_x, L0_attn_in_proj_w, L0_attn_in_proj_b, L0_attn_out_proj_w,
           L0_attn_out_proj_b, L0_ffn_Wg, L0_ffn_bg, L0_ffn_W1, L0_ffn_b1,
           L0_ffn_W2, L0_ffn_b2):
    x = np.ascontiguousarray(np.asarray(L0_attn_x, dtype=np.float32))
    Bb = x.shape[0]
    identf, identbf, onescol, onesrow, cvec, selm = _make_consts()
    weights = dict(
        ipw=np.ascontiguousarray(np.asarray(L0_attn_in_proj_w, np.float32)),
        ipb=np.ascontiguousarray(np.asarray(L0_attn_in_proj_b, np.float32)),
        opw=np.ascontiguousarray(np.asarray(L0_attn_out_proj_w, np.float32)),
        opb=np.ascontiguousarray(np.asarray(L0_attn_out_proj_b, np.float32)),
        wgw=np.ascontiguousarray(np.asarray(L0_ffn_Wg, np.float32)),
        bgv=np.ascontiguousarray(np.asarray(L0_ffn_bg, np.float32)),
        w1=np.ascontiguousarray(np.asarray(L0_ffn_W1, np.float32)),
        b1=np.ascontiguousarray(np.asarray(L0_ffn_b1, np.float32)),
        w2=np.ascontiguousarray(np.asarray(L0_ffn_W2, np.float32)),
        b2=np.ascontiguousarray(np.asarray(L0_ffn_b2, np.float32)),
        ident=identf, identb=identbf, onescol=onescol, onesrow=onesrow,
        cvec=cvec, selm=selm,
    )
    u128_, tid4_, dmpc_, pflc_, pfic_ = _make_sparse_consts()
    weights.update(u128=u128_, tid4=tid4_, dmpc=dmpc_, pflc=pflc_, pfic=pfic_)
    masks = [_make_mask(0), _make_mask(1)]
    masksT = [_make_maskT(0), _make_maskT(1)]
    in_maps = []
    for c in range(8):
        b, par = c // 2, c % 2
        blocks = [2 * k + par for k in range(NSLOT)]
        xo = np.ascontiguousarray(
            np.concatenate([x[b, a * 512:(a + 1) * 512] for a in blocks], axis=0))
        m = dict(weights)
        m["x_full"] = np.ascontiguousarray(x[b])
        m["x_own"] = xo
        m["mask_in"] = masks[par]
        m["maskT_in"] = masksT[par]
        in_maps.append(m)

    nc = build_program()
    res = run_bass_kernel_spmd(nc, in_maps, core_ids=list(range(8)))

    y = np.empty((Bb, S, E), dtype=np.float32)
    for c in range(8):
        b, par = c // 2, c % 2
        oc = np.asarray(res.results[c]["out"])
        for k in range(NSLOT):
            a = 2 * k + par
            y[b, a * 512:(a + 1) * 512] = oc[k * 512:(k + 1) * 512]
    return y
